# version 06
# baseline (speedup 1.0000x reference)
import numpy as np
import ml_dtypes
import concourse.bass as bass
import concourse.mybir as mybir
from concourse.bass_utils import run_bass_kernel_spmd

F32 = mybir.dt.float32
BF16 = mybir.dt.bfloat16
I32 = mybir.dt.int32
AF = mybir.ActivationFunctionType
ALU = mybir.AluOpType
AX = mybir.AxisListType.X
f32 = np.float32

D = 1024
N = 4096
DK = 128
NT = 16
TOK = 2048
NCORE = 8
RSQRT_ITERS = 9
RC = 12582912.0


class KB:
    def __init__(self, debug=False):
        self.debug = debug
        nc = bass.Bass("TRN2", target_bir_lowering=False,
                       detect_race_conditions=False)
        self.nc = nc
        self.S = {}
        self._sem_h = {}
        self._waits = []

        dt = nc.dram_tensor
        self.x_in = dt("L0_attn_inputs", [TOK, D], F32, kind="ExternalInput").ap()
        self.key_w = dt("L0_attn_key_param_tokens", [N, D], F32, kind="ExternalInput").ap()
        self.val_w = dt("L0_attn_value_param_tokens", [N, D], F32, kind="ExternalInput").ap()
        self.gate_w = dt("L0_ffn_w_gate", [N, D], F32, kind="ExternalInput").ap()
        self.up_w = dt("L0_ffn_w_up", [N, D], F32, kind="ExternalInput").ap()
        self.down_w = dt("L0_ffn_w_down", [D, N], F32, kind="ExternalInput").ap()
        self.wq = dt("L1_attn_Wq", [D, DK], F32, kind="ExternalInput").ap()
        self.wk = dt("L1_attn_Wk", [D, DK], F32, kind="ExternalInput").ap()
        self.wv = dt("L1_attn_Wv", [D, DK], F32, kind="ExternalInput").ap()
        self.wo = dt("L1_attn_Wo", [DK, D], F32, kind="ExternalInput").ap()
        self.w1_w = dt("L1_ffn_w1_w", [N, D], F32, kind="ExternalInput").ap()
        self.w2_w = dt("L1_ffn_w2_w", [N, D], F32, kind="ExternalInput").ap()
        self.h1_w = dt("L1_ffn_h1_w", [N, N], F32, kind="ExternalInput").ap()
        self.h2_w = dt("L1_ffn_h2_w", [N, N], F32, kind="ExternalInput").ap()
        self.w3_w = dt("L1_ffn_w3_w", [D, N], F32, kind="ExternalInput").ap()
        self.eye_in = dt("cst_eye", [128, 128], F32, kind="ExternalInput").ap()
        self.ones_in = dt("cst_ones", [128, 128], F32, kind="ExternalInput").ap()
        self.svec_in = dt("cst_svec", [128, 144], F32, kind="ExternalInput").ap()
        self.cmask_in = dt("cst_cmask", [128, 4096], BF16, kind="ExternalInput").ap()
        self.out = dt("out", [TOK, D], F32, kind="ExternalOutput").ap()
        if debug:
            self.dbg_x1 = dt("dbg_x1", [TOK, D], F32, kind="ExternalOutput").ap()
            self.dbg_x2 = dt("dbg_x2", [TOK, D], F32, kind="ExternalOutput").ap()
            self.dbg_x3 = dt("dbg_x3", [TOK, D], F32, kind="ExternalOutput").ap()

        dkind = dict(kind="ExternalOutput") if debug else {}
        self.key_bf = dt("key_bf", [N, D], BF16).ap()
        self.w1_bf = dt("w1_bf", [N, D], BF16).ap()
        self.w2_bf = dt("w2_bf", [N, D], BF16).ap()
        self.h1_bf = dt("h1_bf", [N, N], BF16).ap()
        self.h2_bf = dt("h2_bf", [N, N], BF16).ap()
        self.gq_bf = dt("gq_bf", [N, D], BF16).ap()
        self.uq_bf = dt("uq_bf", [N, D], BF16).ap()
        self.dq_bf = dt("dq_bf", [D, N], BF16).ap()
        self.w3s_bf = dt("w3s_bf", [D, N], BF16, **dkind).ap()
        self.h1t = dt("h1t", [8, N, 512], BF16, **dkind).ap()
        self.h2t = dt("h2t", [8, N, 512], BF16, **dkind).ap()
        self.act0 = dt("act0", [TOK, D], BF16, **dkind).ap()
        self.act1 = dt("act1", [TOK, N], BF16, **dkind).ap()
        self.gpark = dt("gpark", [N, TOK], BF16, **dkind).ap()
        self.kpark = dt("kpark", [N, TOK], BF16, **dkind).ap()
        self.xa = dt("xa", [TOK, D], F32).ap()
        self.xb = dt("xb", [TOK, D], F32).ap()
        self.kv_in = dt("kv_in", [2, 128 * TOK], BF16).ap()
        self.kv_out = dt("kv_out", [16, 128 * TOK], BF16).ap()

    def sem(self, name):
        if name not in self._sem_h:
            self._sem_h[name] = self.ctx.enter_context(self.nc.semaphore(name))
            self.S[name] = 0
        return self._sem_h[name]

    def inc(self, instr, name, n=1):
        instr.then_inc(self.sem(name), n)
        self.S[name] += n
        return self.S[name]

    def wait(self, eng, name, val=None):
        v = self.S[name] if val is None else val
        self._waits.append((name, v))
        eng.wait_ge(self.sem(name), v)

    def snap(self, name):
        self.sem(name)
        return self.S[name]

    def _vfence(self, vec, instr=None):
        if instr is None:
            instr = vec.engine_nop()
        self.inc(instr, 'v', 1)
        self.wait(vec, 'v')

    def _newton_rsqrt(self, vec, vin_off, y_off, t_off, seed):
        ST = self.STATS
        V = ST[:, vin_off:vin_off + 64]
        Y = ST[:, y_off:y_off + 64]
        T = ST[:, t_off:t_off + 64]
        vec.tensor_scalar(Y, V, 0.0, seed, ALU.mult, ALU.add)
        for _ in range(RSQRT_ITERS):
            vec.tensor_tensor(T, Y, Y, ALU.mult)
            vec.tensor_tensor(T, T, V, ALU.mult)
            vec.tensor_scalar(T, T, -0.5, 1.5, ALU.mult, ALU.add)
            vec.tensor_tensor(Y, Y, T, ALU.mult)

    def build(self):
        nc = self.nc
        from contextlib import ExitStack
        self.ctx = ExitStack()
        e = self.ctx.enter_context
        self.BIGA = e(nc.sbuf_tensor("BIGA", [128, 32768], BF16))
        self.BIGB = e(nc.sbuf_tensor("BIGB", [128, 32768], BF16))
        self.MED0 = e(nc.sbuf_tensor("MED0", [128, 8192], BF16))
        self.MED1 = e(nc.sbuf_tensor("MED1", [128, 8192], BF16))
        self.XF32 = e(nc.sbuf_tensor("XF32", [128, 4096], F32))
        self.STATS = e(nc.sbuf_tensor("STATS", [128, 1024], F32))
        self.DMY = e(nc.sbuf_tensor("DMY", [128, 1024], F32))
        self.I32S = e(nc.sbuf_tensor("I32S", [128, 1024], I32))
        self.QF32 = e(nc.sbuf_tensor("QF32", [128, 4096], F32))
        self.EYE = e(nc.sbuf_tensor("EYEc", [128, 128], F32))
        self.ONES = e(nc.sbuf_tensor("ONESc", [128, 128], F32))
        self.SVEC = e(nc.sbuf_tensor("SVECc", [128, 144], F32))
        self.pb = [e(nc.psum_tensor(f"pb{i}", [128, 512], F32)) for i in range(5)]
        self.pP = e(nc.psum_tensor("pP", [128, 256], F32))
        self.pT = e(nc.psum_tensor("pT", [128, 512], F32))
        self.pS = e(nc.psum_tensor("pS", [128, 256], F32))
        for s in ('w', 'g', 't', 's', 'v', 'e', 'c'):
            pass
        self.block = e(nc.Block())
        for s in ('w', 'g', 't', 's', 'v', 'e', 'c'):
            self.sem(s)

        import os
        kmax = int(os.environ.get("KPHASES", "99"))
        phases = [
            self.phase_consts,
            self.phase_casts,
            lambda: self.phase_ln(src=self.x_in, dst=self.act0, kind='bf16',
                                  tag='ln0', src_ready=None),
            self.phase_l0attn_A,
            self.phase_l0attn_O,
            self.phase_qprep,
            lambda: self.dbg_copy('dbg_x1', self.xa),
            lambda: self.phase_ln(src=self.xa, dst=self.act0, kind='quant',
                                  tag='ln1', src_ready=('g', self.snap('g'))),
            self.phase_l0ffn,
            lambda: self.dbg_copy('dbg_x2', self.xb),
            lambda: self.phase_ln(src=self.xb, dst=self.act0, kind='bf16',
                                  tag='ln2', src_ready=('g', self.snap('g'))),
            self.phase_l1attn_qkv,
            self.phase_l1attn,
            lambda: self.dbg_copy('dbg_x3', self.xa),
            lambda: self.phase_ln(src=self.xa, dst=self.act0, kind='bf16',
                                  tag='ln3', src_ready=('g', self.snap('g'))),
            self.phase_l1ffn,
        ]
        for pi, ph in enumerate(phases):
            if pi < kmax:
                ph()
        self.phase_finish()
        bad = [(n, v) for (n, v) in self._waits if v > self.S[n]]
        assert not bad, f"waits beyond final counts: {bad[:10]} finals={self.S}"
        return nc

    def dbg_copy(self, name, src):
        if not self.debug:
            return
        k = self
        dst = getattr(self, name)
        gsnap = self.snap('g')

        @self.block.gpsimd
        def _(gp):
            k.wait(gp, 'g', gsnap)
            k.inc(gp.dma_start(out=dst[:, :], in_=src[:, :]), 'g', 16)

    def phase_consts(self):
        k = self

        @self.block.sync
        def _(sync):
            k.inc(sync.dma_start(out=k.EYE[:], in_=k.eye_in[:]), 'wc', 16)
            k.inc(sync.dma_start(out=k.ONES[:], in_=k.ones_in[:]), 'wc', 16)
            k.inc(sync.dma_start(out=k.SVEC[:], in_=k.svec_in[:]), 'wc', 16)
        self.const_w = self.S['wc']

        @self.block.vector
        def _(vec):
            k.inc(vec.memset(k.STATS[:, :], 1.0), 'v', 1)
        self.memset_v = self.S['v']

    def phase_casts(self):
        k = self

        @self.block.gpsimd
        def _(gp):
            k.inc(gp.dma_start(out=k.key_bf[:, :], in_=k.key_w[:, :]), 'gk', 16)
            k.cast_key = k.S['gk']
            for nch in range(8, 32):
                k.inc(gp.dma_start(out=k.BIGB[:, nch * D:(nch + 1) * D],
                                   in_=k.val_w[nch * 128:(nch + 1) * 128, :]), 'gv', 16)
            k.cast_val_hi = k.S['gv']
            k.inc(gp.dma_start(out=k.w1_bf[:, :], in_=k.w1_w[:, :]), 'gw', 16)
            k.inc(gp.dma_start(out=k.w2_bf[:, :], in_=k.w2_w[:, :]), 'gw', 16)
            k.cast_w12 = k.S['gw']
            k.inc(gp.dma_start(out=k.h1_bf[:, :], in_=k.h1_w[:, :]), 'gh1', 16)
            k.cast_h1 = k.S['gh1']
            k.inc(gp.dma_start(out=k.h2_bf[:, :], in_=k.h2_w[:, :]), 'gh2', 16)
            k.cast_h2 = k.S['gh2']

    def phase_qprep(self):
        k = self
        ST = self.STATS
        def chunk_ap(t, cc):
            R, C = t.shape
            if C == 1024:
                return t[cc * 256:(cc + 1) * 256, :].rearrange(
                    "(a p) c -> p a c", p=128)
            else:
                r, j = cc // 2, cc % 2
                return t[r * 128:(r + 1) * 128, j * 2048:(j + 1) * 2048]

        def slot_ap(buf, cc, C):
            s = buf[:, (cc % 2) * 2048:(cc % 2) * 2048 + 2048]
            if C == 1024:
                return s.rearrange("p (a c) -> p a c", c=1024)
            return s

        specs = [(k.gate_w, N * D, k.gq_bf, 592, 'gq0'),
                 (k.up_w, N * D, k.uq_bf, 593, 'gq1'),
                 (k.down_w, D * N, k.dq_bf, 594, 'gq2')]
        for wi, (src, numel, dst, mcol, gq) in enumerate(specs):
            nch = numel // (128 * 2048)
            rpc = (128 * 2048) // src.shape[1]

            v1_0 = self.snap('v')

            m1 = []

            @self.block.sync
            def _(sync, src=src, nch=nch, rpc=rpc, v1_0=v1_0):
                for cc in range(nch):
                    if cc >= 2:
                        k.wait(sync, 'v', v1_0 + (cc - 1))
                    elif cc == 0:
                        k.wait(sync, 'v', v1_0)
                    m1.append(k.inc(sync.dma_start(
                        out=slot_ap(k.QF32, cc, src.shape[1]),
                        in_=chunk_ap(src, cc)), 'wl%d' % (cc % 2), 16))

            @self.block.vector
            def _(vec, nch=nch):
                for cc in range(nch):
                    k.wait(vec, 'wl%d' % (cc % 2), m1[cc])
                    i = vec.reduce_sum(ST[:, 576 + cc:577 + cc],
                                       k.QF32[:, (cc % 2) * 2048:(cc % 2) * 2048 + 2048],
                                       AX, apply_absolute_value=True)
                    k.inc(i, 'v', 1)
                self._vfence(vec)
                i = vec.reduce_sum(ST[:, 560:561], ST[:, 576:576 + nch], AX)
                k.inc(i, 'v', 1)
            vpart = self.S['v']

            eprev = self.snap('e')

            @self.block.tensor
            def _(ten, vpart=vpart, eprev=eprev):
                k.wait(ten, 'wc', k.const_w)
                k.wait(ten, 'v', vpart)
                k.wait(ten, 'e', eprev)
                k.inc(ten.matmul(k.pT[0:1, 0:1], ST[:, 560:561], k.ONES[:, 0:1],
                                 start=True, stop=True), 't', 1)
            tsum = self.S['t']

            @self.block.vector
            def _(vec, numel=numel, tsum=tsum):
                k.wait(vec, 't', tsum)
                i = vec.tensor_scalar(ST[0:1, 561:562], k.pT[0:1, 0:1],
                                      1.0 / numel, 1e-5, ALU.mult, ALU.max)
                self._vfence(vec, i)
                i = vec.reciprocal(ST[0:1, 562:563], ST[0:1, 561:562])
                self._vfence(vec, i)
            vrecip = self.S['v']

            @self.block.tensor
            def _(ten, vrecip=vrecip):
                k.wait(ten, 'v', vrecip)
                k.inc(ten.matmul(k.pT[:, 256:257], k.ONES[0:1, :], ST[0:1, 561:562],
                                 start=True, stop=True), 't', 1)
                k.inc(ten.matmul(k.pT[:, 257:258], k.ONES[0:1, :], ST[0:1, 562:563],
                                 start=True, stop=True), 't', 1)
            tbc = self.S['t']

            @self.block.vector
            def _(vec, mcol=mcol, tbc=tbc):
                k.wait(vec, 't', tbc)
                vec.tensor_copy(ST[:, mcol:mcol + 1], k.pT[:, 256:257])
                i = vec.tensor_copy(ST[:, 595:596], k.pT[:, 257:258])
                self._vfence(vec, i)
                k.inc(vec.engine_nop(), 'e', 1)
            vsb = self.S['v']

            v2_0 = self.snap('v')

            m2 = []

            @self.block.sync
            def _(sync, src=src, nch=nch, rpc=rpc, v2_0=v2_0, vpart=vpart):
                for cc in range(nch):
                    if cc >= 2:
                        k.wait(sync, 'v', v2_0 + (cc - 1) * 2)
                    elif cc == 0:
                        k.wait(sync, 'v', vpart)
                    m2.append(k.inc(sync.dma_start(
                        out=slot_ap(k.QF32, cc, src.shape[1]),
                        in_=chunk_ap(src, cc)), 'wl%d' % (cc % 2), 16))

            @self.block.vector
            def _(vec, nch=nch, vsb=vsb, gq=gq):
                k.wait(vec, 'v', vsb)
                for cc in range(nch):
                    k.wait(vec, 'wl%d' % (cc % 2), m2[cc])
                    if cc >= 2:
                        k.wait(vec, gq, cc * 16)
                    for sub in range(2):
                        qsl = slice((cc % 2) * 2048 + sub * 1024,
                                    (cc % 2) * 2048 + sub * 1024 + 1024)
                        msl = slice((cc % 2) * 2048 + sub * 1024,
                                    (cc % 2) * 2048 + sub * 1024 + 1024)
                        vec.tensor_scalar(k.QF32[:, qsl], k.QF32[:, qsl],
                                          ST[:, 595:596], RC, ALU.mult, ALU.add)
                        vec.tensor_scalar(k.QF32[:, qsl], k.QF32[:, qsl],
                                          -RC, None, ALU.add)
                        vec.tensor_copy(k.I32S[:, :], k.QF32[:, qsl])
                        i = vec.tensor_scalar(k.MED1[:, msl], k.I32S[:, :],
                                              -1.0, 1.0, ALU.max, ALU.min)
                        k.inc(i, 'v', 1)
            vB = self.S['v'] - nch * 2

            @self.block.gpsimd
            def _(gp, dst=dst, nch=nch, rpc=rpc, vB=vB, gq=gq):
                for cc in range(nch):
                    k.wait(gp, 'v', vB + (cc + 1) * 2)
                    k.inc(gp.dma_start(
                        out=chunk_ap(dst, cc),
                        in_=slot_ap(k.MED1, cc, dst.shape[1])),
                        gq, 16)
            if wi == 0:
                k.stage_gate = self.S[gq]
            elif wi == 1:
                k.stage_up = self.S[gq]
            else:
                k.stage_down = self.S[gq]

        v3_0 = self.snap('v')

        m3 = []

        @self.block.sync
        def _(sync, v3_0=v3_0):
            for cc in range(8):
                k.wait(sync, 'v', v3_0 + cc)
                m3.append(k.inc(sync.dma_start(
                    out=k.QF32[:, :],
                    in_=k.w3_w[cc * 128:(cc + 1) * 128, :]), 'wl%d' % (cc % 2), 16))

        @self.block.vector
        def _(vec):
            k.wait(vec, 'wc', k.const_w)
            for cc in range(8):
                k.wait(vec, 'wl%d' % (cc % 2), m3[cc])
                if cc >= 1:
                    k.wait(vec, 'gq3', cc * 16)
                for sub in range(4):
                    i = vec.tensor_scalar(
                        k.MED1[:, sub * 1024:sub * 1024 + 1024],
                        k.QF32[:, sub * 1024:sub * 1024 + 1024],
                        k.SVEC[:, 128 + cc:129 + cc], None, ALU.mult)
                k.inc(i, 'v', 1)
        v3B = self.S['v'] - 8

        @self.block.gpsimd
        def _(gp, v3B=v3B):
            for cc in range(8):
                k.wait(gp, 'v', v3B + cc + 1)
                k.inc(gp.dma_start(out=k.w3s_bf[cc * 128:(cc + 1) * 128, :],
                                   in_=k.MED1[:, 0:4096]), 'gq3', 16)
        k.stage_w3 = self.S['gq3']

    def phase_ln(self, src, dst, kind, tag, src_ready):
        import os
        k = self
        ST = self.STATS
        NOSQ = os.environ.get("KLN_NOSQ") == "1"
        NONEWT = os.environ.get("KLN_NONEWT") == "1"
        PLAIN = os.environ.get("KLN_PLAIN") == "1"

        sA0 = self.snap('s')
        vA0 = self.snap('v')
        mA = []

        @self.block.sync
        def _(sync):
            for t in range(NT):
                if t >= 2:
                    k.wait(sync, 's', sA0 + (t - 1))
                    k.wait(sync, 'v', vA0 + (t - 1))
                elif t == 0 and src_ready is not None:
                    k.wait(sync, src_ready[0], src_ready[1])
                mA.append(k.inc(sync.dma_start(
                    out=k.XF32[:, (t % 2) * 1024:(t % 2) * 1024 + 1024],
                    in_=src[t * 128:(t + 1) * 128, :]), 'wl%d' % (t % 2), 16))

        @self.block.scalar
        def _(sc):
            k.wait(sc, 'v', k.memset_v)
            for t in range(NT):
                k.wait(sc, 'wl%d' % (t % 2), mA[t])
                if NOSQ:
                    i = sc.activation(out=k.DMY[:, :],
                                      in_=k.XF32[:, (t % 2) * 1024:(t % 2) * 1024 + 1024],
                                      func=AF.Square)
                else:
                    i = sc.activation(out=k.DMY[:, :],
                                      in_=k.XF32[:, (t % 2) * 1024:(t % 2) * 1024 + 1024],
                                      func=AF.Square, accum_out=ST[:, 64 + t:65 + t])
                k.inc(i, 's', 1)
        sA = self.S['s']

        @self.block.vector
        def _(vec):
            for t in range(NT):
                k.wait(vec, 'wl%d' % (t % 2), mA[t])
                i = vec.reduce_sum(ST[:, t:t + 1],
                                   k.XF32[:, (t % 2) * 1024:(t % 2) * 1024 + 1024], AX)
                k.inc(i, 'v', 1)
            k.wait(vec, 's', sA)
            self._vfence(vec)
            vec.tensor_scalar(ST[:, 0:64], ST[:, 0:64], 1.0 / D, None, ALU.mult)
            vec.tensor_tensor(ST[:, 256:320], ST[:, 0:64], ST[:, 0:64], ALU.mult)
            vec.tensor_scalar(ST[:, 128:192], ST[:, 64:128], 1.0 / D, None, ALU.mult)
            vec.tensor_tensor(ST[:, 128:192], ST[:, 128:192], ST[:, 256:320],
                              ALU.subtract)
            vec.tensor_scalar(ST[:, 128:192], ST[:, 128:192], 1e-5, None, ALU.add)
            if NONEWT:
                vec.tensor_scalar(ST[:, 320:384], ST[:, 128:192], 0.0, 1.0,
                                  ALU.mult, ALU.add)
            else:
                self._newton_rsqrt(vec, 128, 320, 256, 0.6)
            vec.tensor_tensor(ST[:, 384:448], ST[:, 0:64], ST[:, 320:384], ALU.mult)
            i = vec.tensor_scalar(ST[:, 384:448], ST[:, 384:448], -1.0, None, ALU.mult)
            k.inc(i, 'v', 1)
        vstats = self.S['v']

        sB0 = self.snap('s')
        vB0 = self.snap('v')
        mB = []
        wt_base = (self.snap('wt0'), self.snap('wt1'))
        mst = {t: wt_base[t % 2] + (t // 2 + 1) * 16 for t in range(NT)}
        ssnap = [sB0 + t + 1 for t in range(NT)]

        def emit_store(sync, t):
            if kind == 'bf16':
                sync_gate = ('s', ssnap[t])
            else:
                sync_gate = ('v', vB0 + (t + 1) * 12)
            k.wait(sync, *sync_gate)
            mm_ = k.inc(sync.dma_start(
                out=dst[t * 128:(t + 1) * 128, :],
                in_=k.MED0[:, (t % 2) * 1024:(t % 2) * 1024 + 1024]),
                'wt%d' % (t % 2), 16)
            assert mm_ == mst[t]

        @self.block.sync
        def _(sync):
            for t in range(NT):
                if t >= 2:
                    k.wait(sync, 's', sB0 + (t - 1))
                    if kind == 'quant':
                        k.wait(sync, 'v', vB0 + (t - 1) * 12)
                elif t == 0:
                    k.wait(sync, 's', sA)
                    k.wait(sync, 'v', vstats)
                mB.append(k.inc(sync.dma_start(
                    out=k.XF32[:, (t % 2) * 1024:(t % 2) * 1024 + 1024],
                    in_=src[t * 128:(t + 1) * 128, :]), 'wl%d' % (t % 2), 16))
                if t >= 2:
                    emit_store(sync, t - 2)
            emit_store(sync, NT - 2)
            emit_store(sync, NT - 1)

        @self.block.scalar
        def _(sc):
            k.wait(sc, 'v', vstats)
            for t in range(NT):
                k.wait(sc, 'wl%d' % (t % 2), mB[t])
                if kind == 'bf16':
                    if t >= 2:
                        k.wait(sc, 'wt%d' % (t % 2), mst[t - 2])
                    o = k.MED0[:, (t % 2) * 1024:(t % 2) * 1024 + 1024]
                else:
                    o = k.XF32[:, 2048 + (t % 2) * 1024:2048 + (t % 2) * 1024 + 1024]
                if PLAIN:
                    i = sc.activation(out=o,
                                      in_=k.XF32[:, (t % 2) * 1024:(t % 2) * 1024 + 1024],
                                      func=AF.Identity)
                else:
                    i = sc.activation(out=o,
                                      in_=k.XF32[:, (t % 2) * 1024:(t % 2) * 1024 + 1024],
                                      func=AF.Identity, scale=ST[:, 320 + t:321 + t],
                                      bias=ST[:, 384 + t:385 + t])
                ssnap.append(k.inc(i, 's', 1))

        if kind == 'bf16':
            pass
        else:
            @self.block.vector
            def _(vec):
                for t in range(NT):
                    vb = k.S['v']
                    k.wait(vec, 's', ssnap[t])
                    if t >= 2:
                        k.wait(vec, 'wt%d' % (t % 2), mst[t - 2])
                    lnf = k.XF32[:, 2048 + (t % 2) * 1024:2048 + (t % 2) * 1024 + 1024]
                    i = vec.reduce_max(ST[:, 528:529], lnf, AX,
                                       apply_absolute_value=True)
                    self._vfence(vec, i)
                    vec.tensor_scalar(ST[:, 528:529], ST[:, 528:529], 1e-5, None,
                                      ALU.max)
                    i = vec.tensor_scalar(ST[:, 448 + t:449 + t], ST[:, 528:529],
                                          1.0 / 127.0, None, ALU.mult)
                    self._vfence(vec, i)
                    i = vec.reciprocal(ST[:, 544:545], ST[:, 528:529])
                    self._vfence(vec, i)
                    vec.tensor_scalar(ST[:, 544:545], ST[:, 544:545], 127.0, None,
                                      ALU.mult)
                    vec.tensor_scalar(ST[:, 464 + t:465 + t], ST[:, 448 + t:449 + t],
                                      ST[:, 592:593], None, ALU.mult)
                    vec.tensor_scalar(ST[:, 480 + t:481 + t], ST[:, 448 + t:449 + t],
                                      ST[:, 593:594], None, ALU.mult)
                    self._vfence(vec)
                    vec.tensor_scalar(lnf, lnf, ST[:, 544:545], RC, ALU.mult,
                                      ALU.add)
                    vec.tensor_scalar(lnf, lnf, -RC, None, ALU.add)
                    vec.tensor_copy(k.I32S[:, :], lnf)
                    i = vec.tensor_copy(k.MED0[:, (t % 2) * 1024:(t % 2) * 1024 + 1024],
                                        k.I32S[:, :])
                    k.inc(i, 'v', 1)
                    while k.S['v'] - vb < 12:
                        k.inc(vec.engine_nop(), 'v', 1)
                    assert k.S['v'] - vb == 12

        setattr(self, tag + "_done_w", (self.S['wt0'], self.S['wt1']))

    def phase_l0attn_A(self):
        k = self

        @self.block.sync
        def _(sync):
            k.wait(sync, 'gk', k.cast_key)
            k.wait(sync, 'wt0', k.ln0_done_w[0])
            k.wait(sync, 'wt1', k.ln0_done_w[1])
            for dc in range(8):
                k.inc(sync.dma_start(out=k.BIGA[:, dc * 4096:(dc + 1) * 4096],
                                     in_=k.key_bf[0:N, dc * 128:(dc + 1) * 128],
                                     transpose=True), 'w', 16)
        wkey = self.S['w']

        for h in range(2):
            tprev = self.snap('t')

            @self.block.sync
            def _(sync, h=h, tprev=tprev):
                if h == 1:
                    k.wait(sync, 't', tprev)
                for dc in range(8):
                    k.inc(sync.dma_start(
                        out=k.MED0[:, dc * 1024:(dc + 1) * 1024],
                        in_=k.act0[h * 1024:(h + 1) * 1024, dc * 128:(dc + 1) * 128],
                        transpose=True), 'w', 16)
            wln = self.S['w']

            e0 = self.snap('e')
            grp_t = []

            @self.block.tensor
            def _(ten, wln=wln, e0=e0):
                k.wait(ten, 'w', max(wkey, wln))
                k.wait(ten, 'e', e0)
                for tt in range(8):
                    for nc4 in range(8):
                        u = tt * 8 + nc4
                        b = u % 4
                        if u >= 4:
                            k.wait(ten, 'e', e0 + u - 4 + 1)
                        for dc in range(8):
                            i = ten.matmul(k.pb[b][:, :],
                                           k.MED0[:, dc * 1024 + tt * 128:
                                                  dc * 1024 + (tt + 1) * 128],
                                           k.BIGA[:, dc * 4096 + nc4 * 512:
                                                  dc * 4096 + (nc4 + 1) * 512],
                                           start=(dc == 0), stop=(dc == 7))
                        grp_t.append(k.inc(i, 't', 1))

            v0 = self.snap('v')
            sg = []
            gst = [wln + (t + 1) * 16 for t in range(8)]

            @self.block.scalar
            def _(sc, h=h, v0=v0):
                if h == 0:
                    sc.preload_act_table(AF.Gelu)

                def evac(tt):
                    for nc4 in range(8):
                        u = tt * 8 + nc4
                        k.wait(sc, 't', grp_t[u])
                        if nc4 == 0 and tt >= 2:
                            k.wait(sc, 'w', gst[tt - 2])
                        A = k.BIGB[:, (tt % 2) * 4096:(tt % 2) * 4096 + 4096]
                        i = sc.activation(out=A[:, nc4 * 512:(nc4 + 1) * 512],
                                          in_=k.pb[u % 4][:, :], func=AF.Copy)
                        k.inc(i, 'e', 1)

                def gelu(tt):
                    k.wait(sc, 'v', v0 + (tt + 1) * 7)
                    A = k.BIGB[:, (tt % 2) * 4096:(tt % 2) * 4096 + 4096]
                    i = sc.activation(out=A[:, :], in_=A[:, :], func=AF.Gelu,
                                      scale=k.STATS[:, 560 + tt:561 + tt])
                    sg.append(k.inc(i, 's', 1))

                evac(0)
                for tt in range(1, 8):
                    evac(tt)
                    gelu(tt - 1)
                gelu(7)

            @self.block.vector
            def _(vec, v0=v0, e0=e0):
                for tt in range(8):
                    vb = k.S['v']
                    k.wait(vec, 'e', e0 + (tt + 1) * 8)
                    A = k.BIGB[:, (tt % 2) * 4096:(tt % 2) * 4096 + 4096]
                    for sub in range(4):
                        vec.scalar_tensor_tensor(
                            k.DMY[:, :], A[:, sub * 1024:(sub + 1) * 1024], 1.0,
                            A[:, sub * 1024:(sub + 1) * 1024], ALU.mult, ALU.mult,
                            accum_out=k.STATS[:, 736 + sub:737 + sub])
                    self._vfence(vec)
                    i = vec.reduce_sum(k.STATS[:, 608:609], k.STATS[:, 736:740], AX)
                    self._vfence(vec, i)
                    self._newton_rsqrt(vec, 608, 672, 256, 0.02)
                    i = vec.tensor_scalar(k.STATS[:, 560 + tt:561 + tt],
                                          k.STATS[:, 672:673], 64.0, None, ALU.mult)
                    k.inc(i, 'v', 1)
                    while k.S['v'] - vb < 7:
                        k.inc(vec.engine_nop(), 'v', 1)
                    assert k.S['v'] - vb == 7

            @self.block.sync
            def _(sync, h=h):
                for tt in range(8):
                    k.wait(sync, 's', sg[tt])
                    A = k.BIGB[:, (tt % 2) * 4096:(tt % 2) * 4096 + 4096]
                    mm_ = k.inc(sync.dma_start(
                        out=k.act1[(h * 8 + tt) * 128:(h * 8 + tt + 1) * 128, :],
                        in_=A[:, :]), 'w', 16)
                    assert mm_ == gst[tt], (mm_, gst[tt])
        k.G_parked_w = self.S['w']
        k.A_all_t = self.S['t']

    def phase_l0attn_O(self):
        k = self
        sprev = self.snap('s')

        @self.block.gpsimd
        def _(gp, sprev=sprev):
            k.wait(gp, 's', sprev)
            k.wait(gp, 'w', k.G_parked_w)
            for nch in range(8):
                k.inc(gp.dma_start(out=k.BIGB[:, nch * D:(nch + 1) * D],
                                   in_=k.val_w[nch * 128:(nch + 1) * 128, :]), 'gv', 16)
        gval = self.S['gv']

        for h in range(2):
            t_half0 = self.snap('t')

            @self.block.sync
            def _(sync, h=h, t_half0=t_half0):
                k.wait(sync, 'w', k.G_parked_w)
                k.wait(sync, 't', k.A_all_t if h == 0 else t_half0)
                for nc_ in range(32):
                    k.inc(sync.dma_start(
                        out=k.BIGA[:, nc_ * 1024:(nc_ + 1) * 1024],
                        in_=k.act1[h * 1024:(h + 1) * 1024, nc_ * 128:(nc_ + 1) * 128],
                        transpose=True), 'w', 16)
            wgt = self.S['w']

            e0x = self.snap('e')
            mx = []

            @self.block.sync
            def _(sync, h=h, e0x=e0x):
                for tt in range(8):
                    if tt >= 2:
                        k.wait(sync, 'e', e0x + (tt - 1) * 2)
                    mx.append(k.inc(sync.dma_start(
                        out=k.XF32[:, (tt % 2) * 1024:(tt % 2) * 1024 + 1024],
                        in_=k.x_in[(h * 8 + tt) * 128:(h * 8 + tt + 1) * 128, :]),
                        'wx%d' % (tt % 2), 16))

            e0 = self.snap('e')
            grp_t = []

            @self.block.tensor
            def _(ten, wgt=wgt, e0=e0):
                k.wait(ten, 'w', wgt)
                k.wait(ten, 'gv', gval)
                k.wait(ten, 'e', e0)
                for tt in range(8):
                    for dh in range(2):
                        u = tt * 2 + dh
                        b = u % 4
                        if u >= 4:
                            k.wait(ten, 'e', e0 + u - 4 + 1)
                        for nc_ in range(32):
                            i = ten.matmul(
                                k.pb[b][:, :],
                                k.BIGA[:, nc_ * 1024 + tt * 128:nc_ * 1024 + (tt + 1) * 128],
                                k.BIGB[:, nc_ * 1024 + dh * 512:nc_ * 1024 + dh * 512 + 512],
                                start=(nc_ == 0), stop=(nc_ == 31))
                            if (nc_ + 1) % 8 == 0 and nc_ != 31:
                                k.inc(i, 't', 1)
                        grp_t.append(k.inc(i, 't', 1))

            gsb = self.snap('g')

            @self.block.vector
            def _(vec, gsb=gsb):
                for tt in range(8):
                    for dh in range(2):
                        u = tt * 2 + dh
                        k.wait(vec, 't', grp_t[u])
                        if dh == 0:
                            k.wait(vec, 'wx%d' % (tt % 2), mx[tt])
                            if tt >= 2:
                                k.wait(vec, 'g', gsb + tt * 16)
                        i = vec.scalar_tensor_tensor(
                            k.XF32[:, 2048 + (tt % 2) * 1024 + dh * 512:
                                   2048 + (tt % 2) * 1024 + dh * 512 + 512],
                            k.pb[u % 4][:, :], 1.0,
                            k.XF32[:, (tt % 2) * 1024 + dh * 512:
                                   (tt % 2) * 1024 + dh * 512 + 512],
                            ALU.mult, ALU.add)
                        k.inc(i, 'e', 1)
            er0 = self.S['e'] - 16

            @self.block.gpsimd
            def _(gp, h=h, er0=er0, gsb=gsb):
                for tt in range(8):
                    k.wait(gp, 'e', er0 + (tt + 1) * 2)
                    mm_ = k.inc(gp.dma_start(
                        out=k.xa[(h * 8 + tt) * 128:(h * 8 + tt + 1) * 128, :],
                        in_=k.XF32[:, 2048 + (tt % 2) * 1024:2048 + (tt % 2) * 1024 + 1024]),
                        'g', 16)
                    assert mm_ == gsb + (tt + 1) * 16

    def phase_l0ffn(self):
        k = self
        ST = self.STATS

        for h in range(2):
            tprev = self.snap('t')

            @self.block.sync
            def _(sync, h=h, tprev=tprev):
                k.wait(sync, 'gq0', k.stage_gate)
                k.wait(sync, 't', tprev)
                for dc in range(8):
                    k.inc(sync.dma_start(out=k.BIGA[:, dc * 4096:(dc + 1) * 4096],
                                         in_=k.gq_bf[0:N, dc * 128:(dc + 1) * 128],
                                         transpose=True), 'w', 16)
                k.wait(sync, 'wt0', k.ln1_done_w[0])
                k.wait(sync, 'wt1', k.ln1_done_w[1])
                for dc in range(8):
                    k.inc(sync.dma_start(
                        out=k.MED0[:, dc * 1024:(dc + 1) * 1024],
                        in_=k.act0[h * 1024:(h + 1) * 1024, dc * 128:(dc + 1) * 128],
                        transpose=True), 'w', 16)
            wg = self.S['w']

            sprev_v = self.snap('v')
            e0 = self.snap('e')
            grp_t = []

            @self.block.tensor
            def _(ten, wg=wg, e0=e0):
                k.wait(ten, 'w', wg)
                k.wait(ten, 'e', e0)
                for tt in range(8):
                    for nc4 in range(8):
                        u = tt * 8 + nc4
                        b = u % 4
                        if u >= 4:
                            k.wait(ten, 'e', e0 + u - 4 + 1)
                        for dc in range(8):
                            i = ten.matmul(k.pb[b][:, :],
                                           k.MED0[:, dc * 1024 + tt * 128:
                                                  dc * 1024 + (tt + 1) * 128],
                                           k.BIGA[:, dc * 4096 + nc4 * 512:
                                                  dc * 4096 + (nc4 + 1) * 512],
                                           start=(dc == 0), stop=(dc == 7))
                        grp_t.append(k.inc(i, 't', 1))
            tgate = self.S['t']

            @self.block.scalar
            def _(sc, h=h):
                if h == 0:
                    sc.preload_act_table(AF.Silu)
                for tt in range(8):
                    for nc4 in range(8):
                        u = tt * 8 + nc4
                        k.wait(sc, 't', grp_t[u])
                        i = sc.activation(
                            out=k.BIGB[:, tt * 4096 + nc4 * 512:tt * 4096 + nc4 * 512 + 512],
                            in_=k.pb[u % 4][:, :], func=AF.Silu,
                            scale=ST[:, 464 + h * 8 + tt:465 + h * 8 + tt])
                        k.inc(i, 'e', 1)

            @self.block.sync
            def _(sync, tgate=tgate):
                k.wait(sync, 'gq1', k.stage_up)
                k.wait(sync, 't', tgate)
                for dc in range(8):
                    k.inc(sync.dma_start(out=k.BIGA[:, dc * 4096:(dc + 1) * 4096],
                                         in_=k.uq_bf[0:N, dc * 128:(dc + 1) * 128],
                                         transpose=True), 'w', 16)
            wu = self.S['w']

            e0u = self.snap('e')
            grp_tu = []

            @self.block.tensor
            def _(ten, wu=wu, e0u=e0u):
                k.wait(ten, 'w', wu)
                k.wait(ten, 'e', e0u)
                for tt in range(8):
                    for nc4 in range(8):
                        u = tt * 8 + nc4
                        b = u % 4
                        if u >= 4:
                            k.wait(ten, 'e', e0u + u - 4 + 1)
                        for dc in range(8):
                            i = ten.matmul(k.pb[b][:, :],
                                           k.MED0[:, dc * 1024 + tt * 128:
                                                  dc * 1024 + (tt + 1) * 128],
                                           k.BIGA[:, dc * 4096 + nc4 * 512:
                                                  dc * 4096 + (nc4 + 1) * 512],
                                           start=(dc == 0), stop=(dc == 7))
                        grp_tu.append(k.inc(i, 't', 1))
            tup = self.S['t']

            vq_marks = []

            @self.block.vector
            def _(vec, h=h, e0=e0):
                for tt in range(8):
                    for nc4 in range(8):
                        u = tt * 8 + nc4
                        k.wait(vec, 't', grp_tu[u])
                        k.wait(vec, 'e', e0 + u + 1)
                        sl = slice(tt * 4096 + nc4 * 512, tt * 4096 + nc4 * 512 + 512)
                        i = vec.scalar_tensor_tensor(
                            k.BIGB[:, sl], k.pb[u % 4][:, :],
                            ST[:, 480 + h * 8 + tt:481 + h * 8 + tt],
                            k.BIGB[:, sl], ALU.mult, ALU.mult)
                        k.inc(i, 'e', 1)
                    H = k.BIGB[:, tt * 4096:(tt + 1) * 4096]
                    i = vec.reduce_max(ST[:, 528:529], H, AX, apply_absolute_value=True)
                    self._vfence(vec, i)
                    vec.tensor_scalar(ST[:, 528:529], ST[:, 528:529], 1e-5, None,
                                      ALU.max)
                    i = vec.tensor_scalar(ST[:, 556:557], ST[:, 528:529],
                                          1.0 / 127.0, None, ALU.mult)
                    self._vfence(vec, i)
                    vec.tensor_scalar(ST[:, 496 + h * 8 + tt:497 + h * 8 + tt],
                                      ST[:, 556:557], ST[:, 594:595], None, ALU.mult)
                    i = vec.reciprocal(ST[:, 544:545], ST[:, 528:529])
                    self._vfence(vec, i)
                    i = vec.tensor_scalar(ST[:, 544:545], ST[:, 544:545], 127.0,
                                          None, ALU.mult)
                    self._vfence(vec, i)
                    for sub in range(4):
                        ssl = slice(tt * 4096 + sub * 1024, tt * 4096 + sub * 1024 + 1024)
                        vec.tensor_scalar(k.DMY[:, :], k.BIGB[:, ssl],
                                          ST[:, 544:545], RC, ALU.mult, ALU.add)
                        vec.tensor_scalar(k.DMY[:, :], k.DMY[:, :], -RC, None,
                                          ALU.add)
                        vec.tensor_copy(k.I32S[:, :], k.DMY[:, :])
                        i = vec.tensor_copy(k.BIGB[:, ssl], k.I32S[:, :])
                    vq_marks.append(k.inc(i, 'v', 1))

            @self.block.gpsimd
            def _(gp, h=h):
                for tt in range(8):
                    k.wait(gp, 'v', vq_marks[tt])
                    k.inc(gp.dma_start(
                        out=k.act1[(h * 8 + tt) * 128:(h * 8 + tt + 1) * 128, :],
                        in_=k.BIGB[:, tt * 4096:(tt + 1) * 4096]), 'g', 16)
            gc2 = self.S['g']

            @self.block.sync
            def _(sync, h=h, tup=tup, gc2=gc2):
                k.wait(sync, 'gq2', k.stage_down)
                k.wait(sync, 'g', gc2)
                k.wait(sync, 't', tup)
                for nc_ in range(32):
                    k.inc(sync.dma_start(out=k.BIGA[:, nc_ * 1024:(nc_ + 1) * 1024],
                                         in_=k.dq_bf[0:D, nc_ * 128:(nc_ + 1) * 128],
                                         transpose=True), 'w', 16)
                for nc_ in range(32):
                    k.inc(sync.dma_start(
                        out=k.BIGB[:, nc_ * 1024:(nc_ + 1) * 1024],
                        in_=k.act1[h * 1024:(h + 1) * 1024, nc_ * 128:(nc_ + 1) * 128],
                        transpose=True), 'w', 16)
            wd = self.S['w']

            e0d = self.snap('e')

            mx = []

            @self.block.sync
            def _(sync, h=h, e0d=e0d, gc2=gc2):
                for tt in range(8):
                    if tt >= 2:
                        k.wait(sync, 'e', e0d + (tt - 1) * 2)
                    elif tt == 0:
                        k.wait(sync, 'g', gc2)
                    mx.append(k.inc(sync.dma_start(
                        out=k.XF32[:, (tt % 2) * 1024:(tt % 2) * 1024 + 1024],
                        in_=k.xa[(h * 8 + tt) * 128:(h * 8 + tt + 1) * 128, :]),
                        'wx%d' % (tt % 2), 16))
            grp_td = []

            @self.block.tensor
            def _(ten, wd=wd, e0d=e0d):
                k.wait(ten, 'w', wd)
                k.wait(ten, 'e', e0d)
                for tt in range(8):
                    for dh in range(2):
                        u = tt * 2 + dh
                        b = u % 4
                        if u >= 4:
                            k.wait(ten, 'e', e0d + u - 4 + 1)
                        for nc_ in range(32):
                            i = ten.matmul(
                                k.pb[b][:, :],
                                k.BIGB[:, nc_ * 1024 + tt * 128:nc_ * 1024 + (tt + 1) * 128],
                                k.BIGA[:, nc_ * 1024 + dh * 512:nc_ * 1024 + dh * 512 + 512],
                                start=(nc_ == 0), stop=(nc_ == 31))
                            if (nc_ + 1) % 8 == 0 and nc_ != 31:
                                k.inc(i, 't', 1)
                        grp_td.append(k.inc(i, 't', 1))

            gsb = self.snap('g')

            @self.block.vector
            def _(vec, h=h, gsb=gsb):
                for tt in range(8):
                    for dh in range(2):
                        u = tt * 2 + dh
                        k.wait(vec, 't', grp_td[u])
                        if dh == 0:
                            k.wait(vec, 'wx%d' % (tt % 2), mx[tt])
                            if tt >= 2:
                                k.wait(vec, 'g', gsb + tt * 16)
                        i = vec.scalar_tensor_tensor(
                            k.XF32[:, 2048 + (tt % 2) * 1024 + dh * 512:
                                   2048 + (tt % 2) * 1024 + dh * 512 + 512],
                            k.pb[u % 4][:, :],
                            ST[:, 496 + h * 8 + tt:497 + h * 8 + tt],
                            k.XF32[:, (tt % 2) * 1024 + dh * 512:
                                   (tt % 2) * 1024 + dh * 512 + 512],
                            ALU.mult, ALU.add)
                        k.inc(i, 'e', 1)
            er0 = self.S['e'] - 16

            @self.block.gpsimd
            def _(gp, h=h, er0=er0, gsb=gsb):
                for tt in range(8):
                    k.wait(gp, 'e', er0 + (tt + 1) * 2)
                    mm_ = k.inc(gp.dma_start(
                        out=k.xb[(h * 8 + tt) * 128:(h * 8 + tt + 1) * 128, :],
                        in_=k.XF32[:, 2048 + (tt % 2) * 1024:2048 + (tt % 2) * 1024 + 1024]),
                        'g', 16)
                    assert mm_ == gsb + (tt + 1) * 16

    def phase_l1attn_qkv(self):
        k = self

        @self.block.gpsimd
        def _(gp):
            for dc in range(8):
                k.inc(gp.dma_start(out=k.MED1[:, dc * 128:(dc + 1) * 128],
                                   in_=k.wq[dc * 128:(dc + 1) * 128, :]), 'g', 16)
                k.inc(gp.dma_start(out=k.MED1[:, 1024 + dc * 128:1024 + (dc + 1) * 128],
                                   in_=k.wk[dc * 128:(dc + 1) * 128, :]), 'g', 16)
                k.inc(gp.dma_start(out=k.MED1[:, 2048 + dc * 128:2048 + (dc + 1) * 128],
                                   in_=k.wv[dc * 128:(dc + 1) * 128, :]), 'g', 16)
            k.inc(gp.dma_start(out=k.MED0[:, 2048:3072], in_=k.wo[:, :]), 'g', 16)
        gqkv = self.S['g']

        tprev = self.snap('t')

        @self.block.sync
        def _(sync, tprev=tprev):
            k.wait(sync, 'wt0', k.ln2_done_w[0])
            k.wait(sync, 'wt1', k.ln2_done_w[1])
            k.wait(sync, 't', tprev)
            for dc in range(8):
                k.inc(sync.dma_start(
                    out=k.BIGB[:, dc * 2048:(dc + 1) * 2048],
                    in_=k.act0[0:TOK, dc * 128:(dc + 1) * 128], transpose=True),
                    'w', 16)
        wln2 = self.S['w']

        e0 = self.snap('e')
        grp_t = []

        @self.block.tensor
        def _(ten, e0=e0):
            k.wait(ten, 'w', wln2)
            k.wait(ten, 'g', gqkv)
            k.wait(ten, 'e', e0)
            u = 0
            for woff in (0, 1024):
                for qc4 in range(4):
                    b = u % 4
                    if u >= 4:
                        k.wait(ten, 'e', e0 + u - 4 + 1)
                    for dc in range(8):
                        i = ten.matmul(k.pb[b][:, :],
                                       k.MED1[:, woff + dc * 128:woff + (dc + 1) * 128],
                                       k.BIGB[:, dc * 2048 + qc4 * 512:
                                              dc * 2048 + (qc4 + 1) * 512],
                                       start=(dc == 0), stop=(dc == 7))
                    grp_t.append(k.inc(i, 't', 1))
                    u += 1
            for tt in range(NT):
                b = u % 4
                if u >= 4:
                    k.wait(ten, 'e', e0 + u - 4 + 1)
                for dc in range(8):
                    i = ten.matmul(k.pb[b][:, 0:128],
                                   k.BIGB[:, dc * 2048 + tt * 128:dc * 2048 + (tt + 1) * 128],
                                   k.MED1[:, 2048 + dc * 128:2048 + (dc + 1) * 128],
                                   start=(dc == 0), stop=(dc == 7))
                grp_t.append(k.inc(i, 't', 1))
                u += 1

        @self.block.scalar
        def _(sc):
            u = 0
            for which in range(2):
                for qc4 in range(4):
                    k.wait(sc, 't', grp_t[u])
                    if which == 0:
                        o = k.BIGB[:, 16384 + qc4 * 512:16384 + (qc4 + 1) * 512]
                    else:
                        o = k.MED0[:, qc4 * 512:(qc4 + 1) * 512]
                    i = sc.activation(out=o, in_=k.pb[u % 4][:, :], func=AF.Copy)
                    k.inc(i, 'e', 1)
                    u += 1
            for tt in range(NT):
                k.wait(sc, 't', grp_t[u])
                i = sc.activation(out=k.BIGB[:, 18432 + tt * 128:18432 + (tt + 1) * 128],
                                  in_=k.pb[u % 4][:, 0:128], func=AF.Copy)
                k.inc(i, 'e', 1)
                u += 1
        eqkv = self.S['e']

        @self.block.gpsimd
        def _(gp, eqkv=eqkv):
            k.wait(gp, 'e', eqkv)
            k.inc(gp.dma_start(out=k.kv_in.rearrange("r (p f) -> r p f", p=128)[0, :, :],
                               in_=k.MED0[:, 0:2048]), 'g', 16)
            k.inc(gp.dma_start(out=k.kv_in.rearrange("r (p f) -> r p f", p=128)[1, :, :],
                               in_=k.BIGB[:, 18432:20480]), 'g', 16)
            k.wait(gp, 'g', k.S['g'])
            k.inc(gp.collective_compute(
                "AllGather", ALU.bypass, replica_groups=[list(range(NCORE))],
                ins=[k.kv_in.opt()], outs=[k.kv_out.opt()]), 'c', 1)
        cagg = self.S['c']

        tqkv = self.snap('t')

        @self.block.sync
        def _(sync, tqkv=tqkv, cagg=cagg):
            k.wait(sync, 'c', cagg)
            k.wait(sync, 't', tqkv)
            for cc in range(8):
                k.inc(sync.dma_start(
                    out=k.BIGA[:, cc * 2048:(cc + 1) * 2048],
                    in_=k.kv_out.rearrange("r (p f) -> r p f", p=128)[2 * cc, :, :]),
                    'w', 16)
                k.inc(sync.dma_start(
                    out=k.BIGA[:, 16384 + cc * 2048:16384 + (cc + 1) * 2048],
                    in_=k.kv_out.rearrange("r (p f) -> r p f", p=128)[2 * cc + 1, :, :]),
                    'w', 16)
            k.wait(sync, 'e', k.snap('e'))
            k.wait(sync, 'gq3', k.stage_w3)
            k.inc(sync.dma_start(out=k.MED1[:, 0:4096], in_=k.cmask_in[:, :]), 'w', 16)
        k.attn_w = self.S['w']

    def phase_l1attn(self):
        k = self
        ST = self.STATS
        SC = f32(1.0 / np.sqrt(128.0))
        BI = f32(-4.0)
        e_attn0 = self.snap('e')
        g_xb = self.snap('g')
        k.ht_ms = []
        k.gs0 = self.snap('gs')

        LOOKAHEAD = 4
        for qc in range(8):
            nk = 16 * (qc + 1)
            masked0 = 16 * qc
            t0 = self.snap('t')
            s0 = self.snap('s')
            v0 = self.snap('v')
            w0 = self.snap('w')
            t_S = [0] * nk
            t_PV = [0] * nk
            _t = t0
            for jj in range(nk + LOOKAHEAD):
                if jj >= LOOKAHEAD:
                    _t += 1
                    t_PV[jj - LOOKAHEAD] = _t
                if jj < nk:
                    _t += 1
                    t_S[jj] = _t
            s_exp = [s0 + j + 1 for j in range(nk)]
            v_after = []
            vv = v0
            for j in range(nk):
                if j >= masked0:
                    vv += 1
                vv += 1
                v_after.append(vv)
            t_lT = t_PV[-1] + 2
            t_wo = [t_lT + 1 + uu for uu in range(4)]

            mxa = []

            @self.block.sync
            def _(sync, qc=qc):
                k.wait(sync, 'w', k.attn_w)
                k.wait(sync, 'g', g_xb)
                if qc > 0:
                    k.wait(sync, 'e', e_attn0 + qc * 5)
                for u in range(2):
                    tt = 2 * qc + u
                    mxa.append(k.inc(sync.dma_start(
                        out=k.XF32[:, u * 1024:(u + 1) * 1024],
                        in_=k.xb[tt * 128:(tt + 1) * 128, :]), 'wx%d' % u, 16))

            @self.block.tensor
            def _(ten, qc=qc, nk=nk, masked0=masked0):
                k.wait(ten, 'w', k.attn_w)
                Sslots = [k.pS[:, 0:256], k.pb[4][:, 0:256],
                          k.pb[0][:, 0:256], k.pb[1][:, 0:256]]
                for jj in range(nk + LOOKAHEAD):
                    if jj >= LOOKAHEAD:
                        j = jj - LOOKAHEAD
                        t2, c2 = j // 8, j % 8
                        if j >= masked0:
                            k.wait(ten, 'v', v_after[j])
                        else:
                            k.wait(ten, 's', s_exp[j])
                        if j == 0 and qc > 0:
                            k.wait(ten, 'e', e_attn0 + (qc - 1) * 5 + 1)
                        i = ten.matmul(k.pP[:, 0:256],
                                       k.BIGA[:, 16384 + (c2 * 16 + t2) * 128:
                                              16384 + (c2 * 16 + t2 + 1) * 128],
                                       k.MED0[:, 4096 + (j % 8) * 256:
                                              4096 + (j % 8) * 256 + 256],
                                       start=(j == 0), stop=(j == nk - 1))
                        assert k.inc(i, 't', 1) == t_PV[j]
                    if jj < nk:
                        t2, c2 = jj // 8, jj % 8
                        if jj >= 4:
                            k.wait(ten, 's', s_exp[jj - 4])
                        elif qc > 0 and jj in (2, 3):
                            k.wait(ten, 'e', e_attn0 + (qc - 1) * 5 + jj)
                        i = ten.matmul(Sslots[jj % 4],
                                       k.BIGA[:, c2 * 2048 + t2 * 128:
                                              c2 * 2048 + (t2 + 1) * 128],
                                       k.BIGB[:, 16384 + qc * 256:16384 + (qc + 1) * 256],
                                       start=True, stop=True)
                        assert k.inc(i, 't', 1) == t_S[jj]
                k.wait(ten, 'v', v_after[nk - 1])
                i = ten.transpose(k.pT[:, 0:128], ST[:, 768:896], k.EYE[:])
                k.inc(i, 't', 1)
                i = ten.transpose(k.pT[:, 128:256], ST[:, 896:1024], k.EYE[:])
                k.inc(i, 't', 1)
                k.wait(ten, 'e', e_attn0 + qc * 5 + 1)
                for u in range(2):
                    for dh in range(2):
                        if qc > 0:
                            k.wait(ten, 'e', e_attn0 + qc * 5)
                        i = ten.matmul(k.pb[u * 2 + dh][:, :],
                                       k.MED0[:, 6144 + u * 128:6144 + (u + 1) * 128],
                                       k.MED0[:, 2048 + dh * 512:2048 + dh * 512 + 512],
                                       start=True, stop=True)
                        k.inc(i, 't', 1)
                assert k.S['t'] == t_wo[-1], (k.S['t'], t_wo)

            @self.block.scalar
            def _(sc, qc=qc, nk=nk):
                if qc == 0:
                    sc.preload_act_table(AF.Exp)
                for j in range(nk):
                    k.wait(sc, 't', t_S[j])
                    if j >= 8:
                        k.wait(sc, 't', t_PV[j - 8])
                        k.wait(sc, 'v', v_after[j - 8])
                    Sslot = [k.pS[:, 0:256], k.pb[4][:, 0:256],
                             k.pb[0][:, 0:256], k.pb[1][:, 0:256]][j % 4]
                    i = sc.activation(
                        out=k.MED0[:, 4096 + (j % 8) * 256:4096 + (j % 8) * 256 + 256],
                        in_=Sslot, func=AF.Exp, scale=SC, bias=BI)
                    k.inc(i, 's', 1)
                k.wait(sc, 't', t_PV[nk - 1])
                i = sc.activation(out=k.MED0[:, 6144:6400], in_=k.pP[:, 0:256],
                                  func=AF.Copy)
                k.inc(i, 'e', 1)

            @self.block.vector
            def _(vec, qc=qc, nk=nk, masked0=masked0):
                for j in range(nk):
                    k.wait(vec, 's', s_exp[j])
                    P = k.MED0[:, 4096 + (j % 8) * 256:4096 + (j % 8) * 256 + 256]
                    if j >= masked0:
                        mi = j - masked0
                        i = vec.tensor_tensor(P, P, k.MED1[:, mi * 256:(mi + 1) * 256],
                                              ALU.mult)
                        k.inc(i, 'v', 1)
                    if j == 0:
                        i = vec.tensor_scalar(ST[:, 768:1024], P, 1.0, None, ALU.mult)
                    else:
                        i = vec.tensor_tensor(ST[:, 768:1024], ST[:, 768:1024], P,
                                              ALU.add)
                    k.inc(i, 'v', 1)
                    assert k.S['v'] == v_after[j]
                k.wait(vec, 't', t_lT)
                vec.reduce_sum(ST[:, 512:513], k.pT[:, 0:128], AX)
                i = vec.reduce_sum(ST[:, 513:514], k.pT[:, 128:256], AX)
                self._vfence(vec, i)
                i = vec.reciprocal(ST[:, 514:516], ST[:, 512:514])
                self._vfence(vec, i)
                for u in range(2):
                    k.wait(vec, 'wx%d' % u, mxa[u])
                    if qc > 0:
                        k.wait(vec, 'g', g_xb + qc * 32)
                    for dh in range(2):
                        k.wait(vec, 't', t_wo[u * 2 + dh])
                        i = vec.scalar_tensor_tensor(
                            k.XF32[:, 2048 + u * 1024 + dh * 512:
                                   2048 + u * 1024 + dh * 512 + 512],
                            k.pb[u * 2 + dh][:, :],
                            ST[:, 514 + u:515 + u],
                            k.XF32[:, u * 1024 + dh * 512:u * 1024 + dh * 512 + 512],
                            ALU.mult, ALU.add)
                        k.inc(i, 'e', 1)
            ewo0 = self.S['e'] - 4

            @self.block.gpsimd
            def _(gp, qc=qc, ewo0=ewo0):
                for u in range(2):
                    k.wait(gp, 'e', ewo0 + (u + 1) * 2)
                    tt = 2 * qc + u
                    mm_ = k.inc(gp.dma_start(out=k.xa[tt * 128:(tt + 1) * 128, :],
                                             in_=k.XF32[:, 2048 + u * 1024:2048 + (u + 1) * 1024]),
                                'g', 16)
                    assert mm_ == g_xb + qc * 32 + (u + 1) * 16

            _gsz = [2, 4, 6, 8, 10, 10, 12, 12]
            _g0 = sum(_gsz[:qc])
            self._ht_stage_gap(range(_g0, _g0 + _gsz[qc]))
        k.stage_ht = self.S['gs']

    def _ht_stage_gap(self, strips):
        k = self

        @self.block.sync
        def _(sync, strips=strips):
            for ni in strips:
                wsrc = k.h1_bf if ni < 32 else k.h2_bf
                col = (ni % 32) * 128
                if ni == 0:
                    k.wait(sync, 'gh1', k.cast_h1)
                    k.wait(sync, 'gq0', k.stage_gate)
                    k.wait(sync, 'gq1', k.stage_up)
                    k.wait(sync, 'gq2', k.stage_down)
                if ni == 32:
                    k.wait(sync, 'gh2', k.cast_h2)
                if ni >= 1:
                    k.wait(sync, 'gs', k.gs0 + ni * 16)
                k.ht_ms.append(k.inc(sync.dma_start(
                    out=k.MED1[:, 4096:8192],
                    in_=wsrc[0:N, col:col + 128], transpose=True), 'ws0', 16))

        @self.block.gpsimd
        def _(gp, strips=strips):
            for ni in strips:
                wdst = k.h1t if ni < 32 else k.h2t
                col = (ni % 32) * 128
                k.wait(gp, 'ws0', k.ht_ms[ni])
                k.inc(gp.dma_start(
                    out=wdst[:, col:col + 128, :].rearrange("g p c -> p g c"),
                    in_=k.MED1[:, 4096:8192].rearrange("p (g c) -> p g c", g=8)),
                    'gs', 16)

    def phase_l1ffn(self):
        k = self

        for h in range(2):
            for wi, wbf in enumerate((k.w1_bf, k.w2_bf)):
                tprev = self.snap('t')
                vprev = self.snap('v')

                @self.block.sync
                def _(sync, wbf=wbf, tprev=tprev, wi=wi, h=h, vprev=vprev):
                    k.wait(sync, 'gw', k.cast_w12)
                    k.wait(sync, 't', tprev)
                    if wi == 0:
                        k.wait(sync, 'v', vprev)
                    for dc in range(8):
                        k.inc(sync.dma_start(out=k.BIGA[:, dc * 4096:(dc + 1) * 4096],
                                             in_=wbf[0:N, dc * 128:(dc + 1) * 128],
                                             transpose=True), 'w', 16)
                    if wi == 0:
                        k.wait(sync, 'wt0', k.ln3_done_w[0])
                        k.wait(sync, 'wt1', k.ln3_done_w[1])
                        for dc in range(8):
                            k.inc(sync.dma_start(
                                out=k.MED0[:, dc * 1024:(dc + 1) * 1024],
                                in_=k.act0[h * 1024:(h + 1) * 1024,
                                           dc * 128:(dc + 1) * 128],
                                transpose=True), 'w', 16)
                ww = self.S['w']

                e0 = self.snap('e')
                grp_t = []

                @self.block.tensor
                def _(ten, ww=ww, e0=e0):
                    k.wait(ten, 'w', ww)
                    k.wait(ten, 'e', e0)
                    u = 0
                    for no in range(32):
                        for tc in range(2):
                            b = u % 4
                            if u >= 4:
                                k.wait(ten, 'e', e0 + u - 4 + 1)
                            for dc in range(8):
                                i = ten.matmul(
                                    k.pb[b][:, :],
                                    k.BIGA[:, dc * 4096 + no * 128:dc * 4096 + (no + 1) * 128],
                                    k.MED0[:, dc * 1024 + tc * 512:dc * 1024 + tc * 512 + 512],
                                    start=(dc == 0), stop=(dc == 7))
                            grp_t.append(k.inc(i, 't', 1))
                            u += 1

                if wi == 0:
                    k.w1_e0 = self.snap('e')

                    @self.block.scalar
                    def _(sc, h=h):
                        if h == 0:
                            sc.preload_act_table(AF.Silu)
                        u = 0
                        for no in range(32):
                            for tc in range(2):
                                k.wait(sc, 't', grp_t[u])
                                i = sc.activation(
                                    out=k.BIGB[:, no * 1024 + tc * 512:
                                               no * 1024 + tc * 512 + 512],
                                    in_=k.pb[u % 4][:, :], func=AF.Silu,
                                    scale=k.SVEC[:, no:no + 1])
                                k.inc(i, 'e', 1)
                                u += 1
                else:
                    @self.block.vector
                    def _(vec, e0=e0):
                        u = 0
                        for no in range(32):
                            for tc in range(2):
                                k.wait(vec, 't', grp_t[u])
                                k.wait(vec, 'e', k.w1_e0 + u + 1)
                                sl = slice(no * 1024 + tc * 512,
                                           no * 1024 + tc * 512 + 512)
                                i = vec.scalar_tensor_tensor(
                                    k.BIGB[:, sl], k.pb[u % 4][:, :],
                                    k.SVEC[:, 32 + no:33 + no],
                                    k.BIGB[:, sl], ALU.mult, ALU.mult)
                                k.inc(i, 'e', 1)
                                u += 1

            for li, (bandsrc, park, scol) in enumerate(
                    ((k.h1t, k.gpark, 64), (k.h2t, k.kpark, 96))):
                if li == 1:
                    tprev = self.snap('t')
                    gprev = self.snap('g')

                    @self.block.sync
                    def _(sync, h=h, tprev=tprev, gprev=gprev):
                        k.wait(sync, 't', tprev)
                        k.wait(sync, 'g', gprev)
                        for nc_ in range(32):
                            k.inc(sync.dma_start(
                                out=k.BIGB[:, nc_ * 1024:(nc_ + 1) * 1024],
                                in_=k.gpark[nc_ * 128:(nc_ + 1) * 128,
                                            h * 1024:(h + 1) * 1024]), 'w', 16)
                wh = self.S['w']
                t_b0 = self.snap('t')
                tprev2 = self.snap('t')

                mb = []

                @self.block.sync
                def _(sync, bandsrc=bandsrc, wh=wh, t_b0=t_b0):
                    k.wait(sync, 'gs', k.stage_ht)
                    k.wait(sync, 'w', wh)
                    k.wait(sync, 't', t_b0)
                    for bg in range(8):
                        if bg >= 2:
                            k.wait(sync, 't', t_b0 + (bg - 1) * 32)
                        mb.append(k.inc(sync.dma_start(
                            out=k.BIGA[:, (bg % 2) * 16384:(bg % 2) * 16384 + 16384].rearrange(
                                "p (n c) -> p n c", c=512),
                            in_=bandsrc.rearrange("b (n p) c -> b p n c", p=128)[bg, :, :, :]),
                            'wb%d' % (bg % 2), 16))

                e0 = self.snap('e')
                g_park0 = self.snap('g')
                grp_t = []

                @self.block.tensor
                def _(ten, e0=e0):
                    k.wait(ten, 'e', e0)
                    u = 0
                    for bg in range(8):
                        k.wait(ten, 'wb%d' % (bg % 2), mb[bg])
                        for no_l in range(4):
                            for tc in range(2):
                                b = u % 4
                                if u >= 4:
                                    k.wait(ten, 'e', e0 + u - 4 + 1)
                                for ni in range(32):
                                    i = ten.matmul(
                                        k.pb[b][:, :],
                                        k.BIGA[:, (bg % 2) * 16384 + ni * 512 + no_l * 128:
                                               (bg % 2) * 16384 + ni * 512 + (no_l + 1) * 128],
                                        k.BIGB[:, ni * 1024 + tc * 512:
                                               ni * 1024 + tc * 512 + 512],
                                        start=(ni == 0), stop=(ni == 31))
                                    if (ni + 1) % 8 == 0 and ni != 31:
                                        k.inc(i, 't', 1)
                                grp_t.append(k.inc(i, 't', 1))
                                u += 1
                assert self.S['t'] - t_b0 == 8 * 8 * 4

                @self.block.scalar
                def _(sc, h=h, li=li, scol=scol, g_park0=g_park0):
                    if li == 0 and h == 0:
                        sc.preload_act_table(AF.Gelu)
                    u = 0
                    for bg in range(8):
                        for no_l in range(4):
                            for tc in range(2):
                                k.wait(sc, 't', grp_t[u])
                                if u >= 2:
                                    k.wait(sc, 'g', g_park0 + u * 16)
                                i = sc.activation(
                                    out=k.DMY[:, (u % 2) * 512:(u % 2) * 512 + 512],
                                    in_=k.pb[u % 4][:, :], func=AF.Gelu,
                                    scale=k.SVEC[:, scol + bg * 4 + no_l:
                                                 scol + bg * 4 + no_l + 1])
                                k.inc(i, 'e', 1)
                                u += 1
                e_ev0 = self.S['e'] - 64

                @self.block.gpsimd
                def _(gp, h=h, park=park, e_ev0=e_ev0):
                    u = 0
                    for bg in range(8):
                        for no_l in range(4):
                            for tc in range(2):
                                k.wait(gp, 'e', e_ev0 + u + 1)
                                no = bg * 4 + no_l
                                k.inc(gp.dma_start(
                                    out=park[no * 128:(no + 1) * 128,
                                             h * 1024 + tc * 512:h * 1024 + tc * 512 + 512],
                                    in_=k.DMY[:, (u % 2) * 512:(u % 2) * 512 + 512]),
                                    'g', 16)
                                u += 1

            tprev = self.snap('t')
            gprev = self.snap('g')

            @self.block.sync
            def _(sync, h=h, tprev=tprev, gprev=gprev):
                k.wait(sync, 'gq3', k.stage_w3)
                k.wait(sync, 'g', gprev)
                k.wait(sync, 't', tprev)
                for nc_ in range(32):
                    k.inc(sync.dma_start(out=k.BIGA[:, nc_ * 1024:(nc_ + 1) * 1024],
                                         in_=k.w3s_bf[0:D, nc_ * 128:(nc_ + 1) * 128],
                                         transpose=True), 'w', 16)
                for nc_ in range(32):
                    k.inc(sync.dma_start(
                        out=k.BIGB[:, nc_ * 1024:(nc_ + 1) * 1024],
                        in_=k.kpark[nc_ * 128:(nc_ + 1) * 128,
                                    h * 1024:(h + 1) * 1024]), 'w', 16)
            w3w = self.S['w']

            e0w = self.snap('e')
            g_xa3 = self.snap('g')

            mx = []

            @self.block.sync
            def _(sync, h=h, e0w=e0w, g_xa3=g_xa3):
                for tt in range(8):
                    if tt >= 2:
                        k.wait(sync, 'e', e0w + (tt - 1) * 2)
                    elif tt == 0:
                        k.wait(sync, 'g', g_xa3)
                    mx.append(k.inc(sync.dma_start(
                        out=k.XF32[:, (tt % 2) * 1024:(tt % 2) * 1024 + 1024],
                        in_=k.xa[(h * 8 + tt) * 128:(h * 8 + tt + 1) * 128, :]),
                        'wx%d' % (tt % 2), 16))
            grp_tw = []

            @self.block.tensor
            def _(ten, w3w=w3w, e0w=e0w):
                k.wait(ten, 'w', w3w)
                k.wait(ten, 'e', e0w)
                for tt in range(8):
                    for dh in range(2):
                        u = tt * 2 + dh
                        b = u % 4
                        if u >= 4:
                            k.wait(ten, 'e', e0w + u - 4 + 1)
                        for nc_ in range(32):
                            i = ten.matmul(
                                k.pb[b][:, :],
                                k.BIGB[:, nc_ * 1024 + tt * 128:nc_ * 1024 + (tt + 1) * 128],
                                k.BIGA[:, nc_ * 1024 + dh * 512:nc_ * 1024 + dh * 512 + 512],
                                start=(nc_ == 0), stop=(nc_ == 31))
                            if (nc_ + 1) % 8 == 0 and nc_ != 31:
                                k.inc(i, 't', 1)
                        grp_tw.append(k.inc(i, 't', 1))

            gsb = self.snap('g')

            @self.block.vector
            def _(vec, gsb=gsb):
                for tt in range(8):
                    for dh in range(2):
                        u = tt * 2 + dh
                        k.wait(vec, 't', grp_tw[u])
                        if dh == 0:
                            k.wait(vec, 'wx%d' % (tt % 2), mx[tt])
                            if tt >= 2:
                                k.wait(vec, 'g', gsb + tt * 16)
                        i = vec.scalar_tensor_tensor(
                            k.XF32[:, 2048 + (tt % 2) * 1024 + dh * 512:
                                   2048 + (tt % 2) * 1024 + dh * 512 + 512],
                            k.pb[u % 4][:, :], 1.0,
                            k.XF32[:, (tt % 2) * 1024 + dh * 512:
                                   (tt % 2) * 1024 + dh * 512 + 512],
                            ALU.mult, ALU.add)
                        k.inc(i, 'e', 1)
            er0 = self.S['e'] - 16

            @self.block.gpsimd
            def _(gp, h=h, er0=er0, gsb=gsb):
                for tt in range(8):
                    k.wait(gp, 'e', er0 + (tt + 1) * 2)
                    mm_ = k.inc(gp.dma_start(
                        out=k.out[(h * 8 + tt) * 128:(h * 8 + tt + 1) * 128, :],
                        in_=k.XF32[:, 2048 + (tt % 2) * 1024:2048 + (tt % 2) * 1024 + 1024]),
                        'g', 16)
                    assert mm_ == gsb + (tt + 1) * 16

    def phase_finish(self):
        k = self

        @self.block.gpsimd
        def _(gp):
            k.wait(gp, 'g')

        @self.block.sync
        def _(sync):
            k.wait(sync, 'w')


_NC_CACHE = {}


def _build(debug=False):
    import os
    key = (debug, os.environ.get("KPHASES", "99"), os.environ.get("KLN_NOSQ"), os.environ.get("KLN_NONEWT"), os.environ.get("KLN_PLAIN"))
    if key not in _NC_CACHE:
        kb = KB(debug=debug)
        nc = kb.build()
        _NC_CACHE[key] = nc
    return _NC_CACHE[key]


def _make_consts(core):
    eye = np.eye(128, dtype=np.float32)
    ones = np.ones((128, 128), dtype=np.float32)
    tri = np.triu(np.ones((128, 128), dtype=np.float32))
    cm = np.zeros((16, 128, 256), dtype=np.float32)
    for c2 in range(8):
        blk = np.zeros((128, 128), np.float32)
        if c2 < core:
            blk[:] = 1.0
        elif c2 == core:
            blk[:] = tri
        cm[c2, :, 0:128] = blk
        cm[c2, :, 128:256] = 1.0
        cm[8 + c2, :, 128:256] = blk
    cmask = np.ascontiguousarray(
        cm.transpose(1, 0, 2).reshape(128, 4096)).astype(ml_dtypes.bfloat16)
    return eye, ones, cmask


def _make_svec(w1_s, w2_s, h1_s, h2_s, w3_s):
    sv = np.zeros((128, 144), dtype=np.float32)
    sv[:, 0:32] = np.ascontiguousarray(w1_s.reshape(32, 128).T)
    sv[:, 32:64] = np.ascontiguousarray(w2_s.reshape(32, 128).T)
    sv[:, 64:96] = np.ascontiguousarray(h1_s.reshape(32, 128).T)
    sv[:, 96:128] = np.ascontiguousarray(h2_s.reshape(32, 128).T)
    sv[:, 128:136] = np.ascontiguousarray(w3_s.reshape(8, 128).T)
    return sv


def kernel(L0_attn_inputs, L0_attn_key_param_tokens, L0_attn_value_param_tokens,
           L0_ffn_w_gate, L0_ffn_w_up, L0_ffn_w_down,
           L1_attn_Wq, L1_attn_Wk, L1_attn_Wv, L1_attn_Wo,
           L1_ffn_w1_w, L1_ffn_w1_s, L1_ffn_w2_w, L1_ffn_w2_s,
           L1_ffn_h1_w, L1_ffn_h1_s, L1_ffn_h2_w, L1_ffn_h2_s,
           L1_ffn_w3_w, L1_ffn_w3_s, _debug=False):
    nc = _build(debug=_debug)
    x = np.ascontiguousarray(np.asarray(L0_attn_inputs, dtype=np.float32)
                             .reshape(128, 128, D))
    svec = _make_svec(np.asarray(L1_ffn_w1_s, np.float32),
                      np.asarray(L1_ffn_w2_s, np.float32),
                      np.asarray(L1_ffn_h1_s, np.float32),
                      np.asarray(L1_ffn_h2_s, np.float32),
                      np.asarray(L1_ffn_w3_s, np.float32))
    shared = {
        "L0_attn_key_param_tokens": np.ascontiguousarray(L0_attn_key_param_tokens),
        "L0_attn_value_param_tokens": np.ascontiguousarray(L0_attn_value_param_tokens),
        "L0_ffn_w_gate": np.ascontiguousarray(L0_ffn_w_gate),
        "L0_ffn_w_up": np.ascontiguousarray(L0_ffn_w_up),
        "L0_ffn_w_down": np.ascontiguousarray(L0_ffn_w_down),
        "L1_attn_Wq": np.ascontiguousarray(L1_attn_Wq),
        "L1_attn_Wk": np.ascontiguousarray(L1_attn_Wk),
        "L1_attn_Wv": np.ascontiguousarray(L1_attn_Wv),
        "L1_attn_Wo": np.ascontiguousarray(L1_attn_Wo),
        "L1_ffn_w1_w": np.ascontiguousarray(L1_ffn_w1_w),
        "L1_ffn_w2_w": np.ascontiguousarray(L1_ffn_w2_w),
        "L1_ffn_h1_w": np.ascontiguousarray(L1_ffn_h1_w),
        "L1_ffn_h2_w": np.ascontiguousarray(L1_ffn_h2_w),
        "L1_ffn_w3_w": np.ascontiguousarray(L1_ffn_w3_w),
        "cst_svec": svec,
    }
    in_maps = []
    for c in range(NCORE):
        eye, ones, cmask = _make_consts(c)
        m = dict(shared)
        m["L0_attn_inputs"] = np.ascontiguousarray(x[c::8].reshape(TOK, D))
        m["cst_eye"] = eye
        m["cst_ones"] = ones
        m["cst_cmask"] = cmask
        in_maps.append(m)
    res = run_bass_kernel_spmd(nc, in_maps, core_ids=list(range(NCORE)))
    full = np.empty((128, 128, D), dtype=np.float32)
    for c in range(NCORE):
        full[c::8] = res.results[c]["out"].reshape(NT, 128, D)
    out = full.reshape(1, 128 * 128, D)
    if _debug:
        dbg = []
        for name in ("dbg_x1", "dbg_x2", "dbg_x3"):
            fb = np.empty((128, 128, D), dtype=np.float32)
            for c in range(NCORE):
                fb[c::8] = res.results[c][name].reshape(NT, 128, D)
            dbg.append(fb.reshape(1, 128 * 128, D))
        raw = {n: [np.asarray(res.results[c][n]) for c in range(NCORE)]
               for n in ("act0", "act1", "gpark", "kpark", "h1t", "h2t",
                         "w3s_bf")}
        return out, dbg, raw
    return out


# version 07
# speedup vs baseline: 1.0452x; 1.0452x over previous
import numpy as np
import ml_dtypes
import concourse.bass as bass
import concourse.mybir as mybir
from concourse.bass_utils import run_bass_kernel_spmd

F32 = mybir.dt.float32
BF16 = mybir.dt.bfloat16
I32 = mybir.dt.int32
AF = mybir.ActivationFunctionType
ALU = mybir.AluOpType
AX = mybir.AxisListType.X
f32 = np.float32

D = 1024
N = 4096
DK = 128
NT = 16
TOK = 2048
NCORE = 8
RSQRT_ITERS = 9
RC = 12582912.0


class KB:
    def __init__(self, debug=False):
        self.debug = debug
        nc = bass.Bass("TRN2", target_bir_lowering=False,
                       detect_race_conditions=False)
        self.nc = nc
        self.S = {}
        self._sem_h = {}
        self._waits = []

        dt = nc.dram_tensor
        self.x_in = dt("L0_attn_inputs", [TOK, D], F32, kind="ExternalInput").ap()
        self.key_w = dt("L0_attn_key_param_tokens", [N, D], F32, kind="ExternalInput").ap()
        self.val_w = dt("L0_attn_value_param_tokens", [N, D], F32, kind="ExternalInput").ap()
        self.gate_w = dt("L0_ffn_w_gate", [N, D], F32, kind="ExternalInput").ap()
        self.up_w = dt("L0_ffn_w_up", [N, D], F32, kind="ExternalInput").ap()
        self.down_w = dt("L0_ffn_w_down", [D, N], F32, kind="ExternalInput").ap()
        self.wq = dt("L1_attn_Wq", [D, DK], F32, kind="ExternalInput").ap()
        self.wk = dt("L1_attn_Wk", [D, DK], F32, kind="ExternalInput").ap()
        self.wv = dt("L1_attn_Wv", [D, DK], F32, kind="ExternalInput").ap()
        self.wo = dt("L1_attn_Wo", [DK, D], F32, kind="ExternalInput").ap()
        self.w1_w = dt("L1_ffn_w1_w", [N, D], F32, kind="ExternalInput").ap()
        self.w2_w = dt("L1_ffn_w2_w", [N, D], F32, kind="ExternalInput").ap()
        self.h1_w = dt("L1_ffn_h1_w", [N, N], F32, kind="ExternalInput").ap()
        self.h2_w = dt("L1_ffn_h2_w", [N, N], F32, kind="ExternalInput").ap()
        self.w3_w = dt("L1_ffn_w3_w", [D, N], F32, kind="ExternalInput").ap()
        self.eye_in = dt("cst_eye", [128, 128], F32, kind="ExternalInput").ap()
        self.ones_in = dt("cst_ones", [128, 128], F32, kind="ExternalInput").ap()
        self.svec_in = dt("cst_svec", [128, 144], F32, kind="ExternalInput").ap()
        self.cmask_in = dt("cst_cmask", [128, 4096], BF16, kind="ExternalInput").ap()
        self.out = dt("out", [TOK, D], F32, kind="ExternalOutput").ap()
        if debug:
            self.dbg_x1 = dt("dbg_x1", [TOK, D], F32, kind="ExternalOutput").ap()
            self.dbg_x2 = dt("dbg_x2", [TOK, D], F32, kind="ExternalOutput").ap()
            self.dbg_x3 = dt("dbg_x3", [TOK, D], F32, kind="ExternalOutput").ap()

        dkind = dict(kind="ExternalOutput") if debug else {}
        self.key_bf = dt("key_bf", [N, D], BF16).ap()
        self.w1_bf = dt("w1_bf", [N, D], BF16).ap()
        self.w2_bf = dt("w2_bf", [N, D], BF16).ap()
        self.h1_bf = dt("h1_bf", [N, N], BF16).ap()
        self.h2_bf = dt("h2_bf", [N, N], BF16).ap()
        self.gq_bf = dt("gq_bf", [N, D], BF16).ap()
        self.uq_bf = dt("uq_bf", [N, D], BF16).ap()
        self.dq_bf = dt("dq_bf", [D, N], BF16).ap()
        self.w3s_bf = dt("w3s_bf", [D, N], BF16, **dkind).ap()
        self.h1t = dt("h1t", [8, N, 512], BF16, **dkind).ap()
        self.h2t = dt("h2t", [8, N, 512], BF16, **dkind).ap()
        self.act0 = dt("act0", [TOK, D], BF16, **dkind).ap()
        self.act1 = dt("act1", [TOK, N], BF16, **dkind).ap()
        self.gpark = dt("gpark", [N, TOK], BF16, **dkind).ap()
        self.kpark = dt("kpark", [N, TOK], BF16, **dkind).ap()
        self.xa = dt("xa", [TOK, D], F32).ap()
        self.xb = dt("xb", [TOK, D], F32).ap()
        self.kv_in = dt("kv_in", [2, 128 * TOK], BF16).ap()
        self.kv_out = dt("kv_out", [16, 128 * TOK], BF16).ap()
        self.gqt = dt("gqt", [128, 32768], BF16).ap()
        self.uqt = dt("uqt", [128, 32768], BF16).ap()
        self.dqt = dt("dqt", [128, 32768], BF16).ap()
        self.w1tt = dt("w1tt", [128, 32768], BF16).ap()
        self.w2tt = dt("w2tt", [128, 32768], BF16).ap()
        self.w3tt = dt("w3tt", [128, 32768], BF16).ap()

    def sem(self, name):
        if name not in self._sem_h:
            self._sem_h[name] = self.ctx.enter_context(self.nc.semaphore(name))
            self.S[name] = 0
        return self._sem_h[name]

    def inc(self, instr, name, n=1):
        instr.then_inc(self.sem(name), n)
        self.S[name] += n
        return self.S[name]

    def wait(self, eng, name, val=None):
        v = self.S[name] if val is None else val
        self._waits.append((name, v))
        eng.wait_ge(self.sem(name), v)

    def snap(self, name):
        self.sem(name)
        return self.S[name]

    def _vfence(self, vec, instr=None):
        if instr is None:
            instr = vec.engine_nop()
        self.inc(instr, 'v', 1)
        self.wait(vec, 'v')

    def _newton_rsqrt(self, vec, vin_off, y_off, t_off, seed):
        ST = self.STATS
        V = ST[:, vin_off:vin_off + 64]
        Y = ST[:, y_off:y_off + 64]
        T = ST[:, t_off:t_off + 64]
        vec.tensor_scalar(Y, V, 0.0, seed, ALU.mult, ALU.add)
        for _ in range(RSQRT_ITERS):
            vec.tensor_tensor(T, Y, Y, ALU.mult)
            vec.tensor_tensor(T, T, V, ALU.mult)
            vec.tensor_scalar(T, T, -0.5, 1.5, ALU.mult, ALU.add)
            vec.tensor_tensor(Y, Y, T, ALU.mult)

    def build(self):
        nc = self.nc
        from contextlib import ExitStack
        self.ctx = ExitStack()
        e = self.ctx.enter_context
        self.BIGA = e(nc.sbuf_tensor("BIGA", [128, 32768], BF16))
        self.BIGB = e(nc.sbuf_tensor("BIGB", [128, 32768], BF16))
        self.MED0 = e(nc.sbuf_tensor("MED0", [128, 8192], BF16))
        self.MED1 = e(nc.sbuf_tensor("MED1", [128, 8192], BF16))
        self.XF32 = e(nc.sbuf_tensor("XF32", [128, 4096], F32))
        self.STATS = e(nc.sbuf_tensor("STATS", [128, 1024], F32))
        self.DMY = e(nc.sbuf_tensor("DMY", [128, 1024], F32))
        self.I32S = e(nc.sbuf_tensor("I32S", [128, 1024], I32))
        self.QF32 = e(nc.sbuf_tensor("QF32", [128, 4096], F32))
        self.EYE = e(nc.sbuf_tensor("EYEc", [128, 128], F32))
        self.ONES = e(nc.sbuf_tensor("ONESc", [128, 128], F32))
        self.SVEC = e(nc.sbuf_tensor("SVECc", [128, 144], F32))
        self.pb = [e(nc.psum_tensor(f"pb{i}", [128, 512], F32)) for i in range(5)]
        self.pP = e(nc.psum_tensor("pP", [128, 256], F32))
        self.pT = e(nc.psum_tensor("pT", [128, 512], F32))
        self.pS = e(nc.psum_tensor("pS", [128, 256], F32))
        for s in ('w', 'g', 't', 's', 'v', 'e', 'c'):
            pass
        self.block = e(nc.Block())
        for s in ('w', 'g', 't', 's', 'v', 'e', 'c'):
            self.sem(s)

        import os
        kmax = int(os.environ.get("KPHASES", "99"))
        phases = [
            self.phase_consts,
            self.phase_casts,
            lambda: self.phase_ln(src=self.x_in, dst=self.act0, kind='bf16',
                                  tag='ln0', src_ready=None),
            self.phase_l0attn_A,
            self.phase_l0attn_O,
            self.phase_qprep,
            lambda: self.dbg_copy('dbg_x1', self.xa),
            lambda: self.phase_ln(src=self.xa, dst=self.act0, kind='quant',
                                  tag='ln1', src_ready=('g', self.snap('g'))),
            self.phase_l0ffn,
            lambda: self.dbg_copy('dbg_x2', self.xb),
            lambda: self.phase_ln(src=self.xb, dst=self.act0, kind='bf16',
                                  tag='ln2', src_ready=('g', self.snap('g'))),
            self.phase_l1attn_qkv,
            self.phase_l1attn,
            lambda: self.dbg_copy('dbg_x3', self.xa),
            lambda: self.phase_ln(src=self.xa, dst=self.act0, kind='bf16',
                                  tag='ln3', src_ready=('g', self.snap('g'))),
            self.phase_l1ffn,
        ]
        for pi, ph in enumerate(phases):
            if pi < kmax:
                ph()
        self.phase_finish()
        bad = [(n, v) for (n, v) in self._waits if v > self.S[n]]
        assert not bad, f"waits beyond final counts: {bad[:10]} finals={self.S}"
        return nc

    def _biga_store(self, tdram):
        k = self
        wta_now = self.snap('wta')

        @self.block.gpsimd
        def _(gp, wta_now=wta_now, tdram=tdram):
            k.wait(gp, 'wta', wta_now)
            k.inc(gp.dma_start(out=tdram[:, :], in_=k.BIGA[:, :]), 'gt', 16)

    def _biga_reload(self, sync, tdram):
        for q4 in range(4):
            self.inc(sync.dma_start(
                out=self.BIGA[:, q4 * 8192:(q4 + 1) * 8192],
                in_=tdram[:, q4 * 8192:(q4 + 1) * 8192]), 'w', 16)

    def dbg_copy(self, name, src):
        if not self.debug:
            return
        k = self
        dst = getattr(self, name)
        gsnap = self.snap('g')

        @self.block.gpsimd
        def _(gp):
            k.wait(gp, 'g', gsnap)
            k.inc(gp.dma_start(out=dst[:, :], in_=src[:, :]), 'g', 16)

    def phase_consts(self):
        k = self

        @self.block.sync
        def _(sync):
            k.inc(sync.dma_start(out=k.EYE[:], in_=k.eye_in[:]), 'wc', 16)
            k.inc(sync.dma_start(out=k.ONES[:], in_=k.ones_in[:]), 'wc', 16)
            k.inc(sync.dma_start(out=k.SVEC[:], in_=k.svec_in[:]), 'wc', 16)
        self.const_w = self.S['wc']

        @self.block.vector
        def _(vec):
            k.inc(vec.memset(k.STATS[:, :], 1.0), 'v', 1)
        self.memset_v = self.S['v']

    def phase_casts(self):
        k = self

        @self.block.gpsimd
        def _(gp):
            k.inc(gp.dma_start(out=k.key_bf[:, :], in_=k.key_w[:, :]), 'gk', 16)
            k.cast_key = k.S['gk']
            for nch in range(8, 32):
                k.inc(gp.dma_start(out=k.BIGB[:, nch * D:(nch + 1) * D],
                                   in_=k.val_w[nch * 128:(nch + 1) * 128, :]), 'gv', 16)
            k.cast_val_hi = k.S['gv']
            k.inc(gp.dma_start(out=k.w1_bf[:, :], in_=k.w1_w[:, :]), 'gw', 16)
            k.inc(gp.dma_start(out=k.w2_bf[:, :], in_=k.w2_w[:, :]), 'gw', 16)
            k.cast_w12 = k.S['gw']
            k.inc(gp.dma_start(out=k.h1_bf[:, :], in_=k.h1_w[:, :]), 'gh1', 16)
            k.cast_h1 = k.S['gh1']
            k.inc(gp.dma_start(out=k.h2_bf[:, :], in_=k.h2_w[:, :]), 'gh2', 16)
            k.cast_h2 = k.S['gh2']

    def phase_qprep(self):
        k = self
        ST = self.STATS
        def chunk_ap(t, cc):
            R, C = t.shape
            if C == 1024:
                return t[cc * 256:(cc + 1) * 256, :].rearrange(
                    "(a p) c -> p a c", p=128)
            else:
                r, j = cc // 2, cc % 2
                return t[r * 128:(r + 1) * 128, j * 2048:(j + 1) * 2048]

        def slot_ap(buf, cc, C):
            s = buf[:, (cc % 2) * 2048:(cc % 2) * 2048 + 2048]
            if C == 1024:
                return s.rearrange("p (a c) -> p a c", c=1024)
            return s

        specs = [(k.gate_w, N * D, k.gq_bf, 592, 'gq0'),
                 (k.up_w, N * D, k.uq_bf, 593, 'gq1'),
                 (k.down_w, D * N, k.dq_bf, 594, 'gq2')]
        for wi, (src, numel, dst, mcol, gq) in enumerate(specs):
            nch = numel // (128 * 2048)
            rpc = (128 * 2048) // src.shape[1]

            v1_0 = self.snap('v')

            m1 = []

            @self.block.sync
            def _(sync, src=src, nch=nch, rpc=rpc, v1_0=v1_0):
                for cc in range(nch):
                    if cc >= 2:
                        k.wait(sync, 'v', v1_0 + (cc - 1))
                    elif cc == 0:
                        k.wait(sync, 'v', v1_0)
                    m1.append(k.inc(sync.dma_start(
                        out=slot_ap(k.QF32, cc, src.shape[1]),
                        in_=chunk_ap(src, cc)), 'wl%d' % (cc % 2), 16))

            @self.block.vector
            def _(vec, nch=nch):
                for cc in range(nch):
                    k.wait(vec, 'wl%d' % (cc % 2), m1[cc])
                    i = vec.reduce_sum(ST[:, 576 + cc:577 + cc],
                                       k.QF32[:, (cc % 2) * 2048:(cc % 2) * 2048 + 2048],
                                       AX, apply_absolute_value=True)
                    k.inc(i, 'v', 1)
                self._vfence(vec)
                i = vec.reduce_sum(ST[:, 560:561], ST[:, 576:576 + nch], AX)
                k.inc(i, 'v', 1)
            vpart = self.S['v']

            eprev = self.snap('e')

            @self.block.tensor
            def _(ten, vpart=vpart, eprev=eprev):
                k.wait(ten, 'wc', k.const_w)
                k.wait(ten, 'v', vpart)
                k.wait(ten, 'e', eprev)
                k.inc(ten.matmul(k.pT[0:1, 0:1], ST[:, 560:561], k.ONES[:, 0:1],
                                 start=True, stop=True), 't', 1)
            tsum = self.S['t']

            @self.block.vector
            def _(vec, numel=numel, tsum=tsum):
                k.wait(vec, 't', tsum)
                i = vec.tensor_scalar(ST[0:1, 561:562], k.pT[0:1, 0:1],
                                      1.0 / numel, 1e-5, ALU.mult, ALU.max)
                self._vfence(vec, i)
                i = vec.reciprocal(ST[0:1, 562:563], ST[0:1, 561:562])
                self._vfence(vec, i)
            vrecip = self.S['v']

            @self.block.tensor
            def _(ten, vrecip=vrecip):
                k.wait(ten, 'v', vrecip)
                k.inc(ten.matmul(k.pT[:, 256:257], k.ONES[0:1, :], ST[0:1, 561:562],
                                 start=True, stop=True), 't', 1)
                k.inc(ten.matmul(k.pT[:, 257:258], k.ONES[0:1, :], ST[0:1, 562:563],
                                 start=True, stop=True), 't', 1)
            tbc = self.S['t']

            @self.block.vector
            def _(vec, mcol=mcol, tbc=tbc):
                k.wait(vec, 't', tbc)
                vec.tensor_copy(ST[:, mcol:mcol + 1], k.pT[:, 256:257])
                i = vec.tensor_copy(ST[:, 595:596], k.pT[:, 257:258])
                self._vfence(vec, i)
                k.inc(vec.engine_nop(), 'e', 1)
            vsb = self.S['v']

            v2_0 = self.snap('v')

            m2 = []

            @self.block.sync
            def _(sync, src=src, nch=nch, rpc=rpc, v2_0=v2_0, vpart=vpart):
                for cc in range(nch):
                    if cc >= 2:
                        k.wait(sync, 'v', v2_0 + (cc - 1) * 2)
                    elif cc == 0:
                        k.wait(sync, 'v', vpart)
                    m2.append(k.inc(sync.dma_start(
                        out=slot_ap(k.QF32, cc, src.shape[1]),
                        in_=chunk_ap(src, cc)), 'wl%d' % (cc % 2), 16))

            @self.block.vector
            def _(vec, nch=nch, vsb=vsb, gq=gq):
                k.wait(vec, 'v', vsb)
                for cc in range(nch):
                    k.wait(vec, 'wl%d' % (cc % 2), m2[cc])
                    if cc >= 2:
                        k.wait(vec, gq, cc * 16)
                    for sub in range(2):
                        qsl = slice((cc % 2) * 2048 + sub * 1024,
                                    (cc % 2) * 2048 + sub * 1024 + 1024)
                        msl = slice((cc % 2) * 2048 + sub * 1024,
                                    (cc % 2) * 2048 + sub * 1024 + 1024)
                        vec.tensor_scalar(k.QF32[:, qsl], k.QF32[:, qsl],
                                          ST[:, 595:596], RC, ALU.mult, ALU.add)
                        vec.tensor_scalar(k.QF32[:, qsl], k.QF32[:, qsl],
                                          -RC, None, ALU.add)
                        vec.tensor_copy(k.I32S[:, :], k.QF32[:, qsl])
                        i = vec.tensor_scalar(k.MED1[:, msl], k.I32S[:, :],
                                              -1.0, 1.0, ALU.max, ALU.min)
                        k.inc(i, 'v', 1)
            vB = self.S['v'] - nch * 2

            @self.block.gpsimd
            def _(gp, dst=dst, nch=nch, rpc=rpc, vB=vB, gq=gq):
                for cc in range(nch):
                    k.wait(gp, 'v', vB + (cc + 1) * 2)
                    k.inc(gp.dma_start(
                        out=chunk_ap(dst, cc),
                        in_=slot_ap(k.MED1, cc, dst.shape[1])),
                        gq, 16)
            if wi == 0:
                k.stage_gate = self.S[gq]
            elif wi == 1:
                k.stage_up = self.S[gq]
            else:
                k.stage_down = self.S[gq]

        v3_0 = self.snap('v')

        m3 = []

        @self.block.sync
        def _(sync, v3_0=v3_0):
            for cc in range(8):
                k.wait(sync, 'v', v3_0 + cc)
                m3.append(k.inc(sync.dma_start(
                    out=k.QF32[:, :],
                    in_=k.w3_w[cc * 128:(cc + 1) * 128, :]), 'wl%d' % (cc % 2), 16))

        @self.block.vector
        def _(vec):
            k.wait(vec, 'wc', k.const_w)
            for cc in range(8):
                k.wait(vec, 'wl%d' % (cc % 2), m3[cc])
                if cc >= 1:
                    k.wait(vec, 'gq3', cc * 16)
                for sub in range(4):
                    i = vec.tensor_scalar(
                        k.MED1[:, sub * 1024:sub * 1024 + 1024],
                        k.QF32[:, sub * 1024:sub * 1024 + 1024],
                        k.SVEC[:, 128 + cc:129 + cc], None, ALU.mult)
                k.inc(i, 'v', 1)
        v3B = self.S['v'] - 8

        @self.block.gpsimd
        def _(gp, v3B=v3B):
            for cc in range(8):
                k.wait(gp, 'v', v3B + cc + 1)
                k.inc(gp.dma_start(out=k.w3s_bf[cc * 128:(cc + 1) * 128, :],
                                   in_=k.MED1[:, 0:4096]), 'gq3', 16)
        k.stage_w3 = self.S['gq3']

    def phase_ln(self, src, dst, kind, tag, src_ready):
        import os
        k = self
        ST = self.STATS
        NOSQ = os.environ.get("KLN_NOSQ") == "1"
        NONEWT = os.environ.get("KLN_NONEWT") == "1"
        PLAIN = os.environ.get("KLN_PLAIN") == "1"

        sA0 = self.snap('s')
        vA0 = self.snap('v')
        mA = []

        @self.block.sync
        def _(sync):
            for t in range(NT):
                if t >= 2:
                    k.wait(sync, 's', sA0 + (t - 1))
                    k.wait(sync, 'v', vA0 + (t - 1))
                elif t == 0 and src_ready is not None:
                    k.wait(sync, src_ready[0], src_ready[1])
                mA.append(k.inc(sync.dma_start(
                    out=k.XF32[:, (t % 2) * 1024:(t % 2) * 1024 + 1024],
                    in_=src[t * 128:(t + 1) * 128, :]), 'wl%d' % (t % 2), 16))

        @self.block.scalar
        def _(sc):
            k.wait(sc, 'v', k.memset_v)
            for t in range(NT):
                k.wait(sc, 'wl%d' % (t % 2), mA[t])
                if NOSQ:
                    i = sc.activation(out=k.DMY[:, :],
                                      in_=k.XF32[:, (t % 2) * 1024:(t % 2) * 1024 + 1024],
                                      func=AF.Square)
                else:
                    i = sc.activation(out=k.DMY[:, :],
                                      in_=k.XF32[:, (t % 2) * 1024:(t % 2) * 1024 + 1024],
                                      func=AF.Square, accum_out=ST[:, 64 + t:65 + t])
                k.inc(i, 's', 1)
        sA = self.S['s']

        @self.block.vector
        def _(vec):
            for t in range(NT):
                k.wait(vec, 'wl%d' % (t % 2), mA[t])
                i = vec.reduce_sum(ST[:, t:t + 1],
                                   k.XF32[:, (t % 2) * 1024:(t % 2) * 1024 + 1024], AX)
                k.inc(i, 'v', 1)
            k.wait(vec, 's', sA)
            self._vfence(vec)
            vec.tensor_scalar(ST[:, 0:64], ST[:, 0:64], 1.0 / D, None, ALU.mult)
            vec.tensor_tensor(ST[:, 256:320], ST[:, 0:64], ST[:, 0:64], ALU.mult)
            vec.tensor_scalar(ST[:, 128:192], ST[:, 64:128], 1.0 / D, None, ALU.mult)
            vec.tensor_tensor(ST[:, 128:192], ST[:, 128:192], ST[:, 256:320],
                              ALU.subtract)
            vec.tensor_scalar(ST[:, 128:192], ST[:, 128:192], 1e-5, None, ALU.add)
            if NONEWT:
                vec.tensor_scalar(ST[:, 320:384], ST[:, 128:192], 0.0, 1.0,
                                  ALU.mult, ALU.add)
            else:
                self._newton_rsqrt(vec, 128, 320, 256, 0.6)
            vec.tensor_tensor(ST[:, 384:448], ST[:, 0:64], ST[:, 320:384], ALU.mult)
            i = vec.tensor_scalar(ST[:, 384:448], ST[:, 384:448], -1.0, None, ALU.mult)
            k.inc(i, 'v', 1)
        vstats = self.S['v']

        sB0 = self.snap('s')
        vB0 = self.snap('v')
        mB = []
        wt_base = (self.snap('wt0'), self.snap('wt1'))
        mst = {t: wt_base[t % 2] + (t // 2 + 1) * 16 for t in range(NT)}
        ssnap = [sB0 + t + 1 for t in range(NT)]

        def emit_store(sync, t):
            if kind == 'bf16':
                sync_gate = ('s', ssnap[t])
            else:
                sync_gate = ('v', vB0 + (t + 1) * 12)
            k.wait(sync, *sync_gate)
            mm_ = k.inc(sync.dma_start(
                out=dst[t * 128:(t + 1) * 128, :],
                in_=k.MED0[:, (t % 2) * 1024:(t % 2) * 1024 + 1024]),
                'wt%d' % (t % 2), 16)
            assert mm_ == mst[t]

        @self.block.sync
        def _(sync):
            for t in range(NT):
                if t >= 2:
                    k.wait(sync, 's', sB0 + (t - 1))
                    if kind == 'quant':
                        k.wait(sync, 'v', vB0 + (t - 1) * 12)
                elif t == 0:
                    k.wait(sync, 's', sA)
                    k.wait(sync, 'v', vstats)
                mB.append(k.inc(sync.dma_start(
                    out=k.XF32[:, (t % 2) * 1024:(t % 2) * 1024 + 1024],
                    in_=src[t * 128:(t + 1) * 128, :]), 'wl%d' % (t % 2), 16))
                if t >= 2:
                    emit_store(sync, t - 2)
            emit_store(sync, NT - 2)
            emit_store(sync, NT - 1)

        @self.block.scalar
        def _(sc):
            k.wait(sc, 'v', vstats)
            for t in range(NT):
                k.wait(sc, 'wl%d' % (t % 2), mB[t])
                if kind == 'bf16':
                    if t >= 2:
                        k.wait(sc, 'wt%d' % (t % 2), mst[t - 2])
                    o = k.MED0[:, (t % 2) * 1024:(t % 2) * 1024 + 1024]
                else:
                    o = k.XF32[:, 2048 + (t % 2) * 1024:2048 + (t % 2) * 1024 + 1024]
                if PLAIN:
                    i = sc.activation(out=o,
                                      in_=k.XF32[:, (t % 2) * 1024:(t % 2) * 1024 + 1024],
                                      func=AF.Identity)
                else:
                    i = sc.activation(out=o,
                                      in_=k.XF32[:, (t % 2) * 1024:(t % 2) * 1024 + 1024],
                                      func=AF.Identity, scale=ST[:, 320 + t:321 + t],
                                      bias=ST[:, 384 + t:385 + t])
                ssnap.append(k.inc(i, 's', 1))

        if kind == 'bf16':
            pass
        else:
            @self.block.vector
            def _(vec):
                for t in range(NT):
                    vb = k.S['v']
                    k.wait(vec, 's', ssnap[t])
                    if t >= 2:
                        k.wait(vec, 'wt%d' % (t % 2), mst[t - 2])
                    lnf = k.XF32[:, 2048 + (t % 2) * 1024:2048 + (t % 2) * 1024 + 1024]
                    i = vec.reduce_max(ST[:, 528:529], lnf, AX,
                                       apply_absolute_value=True)
                    self._vfence(vec, i)
                    vec.tensor_scalar(ST[:, 528:529], ST[:, 528:529], 1e-5, None,
                                      ALU.max)
                    i = vec.tensor_scalar(ST[:, 448 + t:449 + t], ST[:, 528:529],
                                          1.0 / 127.0, None, ALU.mult)
                    self._vfence(vec, i)
                    i = vec.reciprocal(ST[:, 544:545], ST[:, 528:529])
                    self._vfence(vec, i)
                    vec.tensor_scalar(ST[:, 544:545], ST[:, 544:545], 127.0, None,
                                      ALU.mult)
                    vec.tensor_scalar(ST[:, 464 + t:465 + t], ST[:, 448 + t:449 + t],
                                      ST[:, 592:593], None, ALU.mult)
                    vec.tensor_scalar(ST[:, 480 + t:481 + t], ST[:, 448 + t:449 + t],
                                      ST[:, 593:594], None, ALU.mult)
                    self._vfence(vec)
                    vec.tensor_scalar(lnf, lnf, ST[:, 544:545], RC, ALU.mult,
                                      ALU.add)
                    vec.tensor_scalar(lnf, lnf, -RC, None, ALU.add)
                    vec.tensor_copy(k.I32S[:, :], lnf)
                    i = vec.tensor_copy(k.MED0[:, (t % 2) * 1024:(t % 2) * 1024 + 1024],
                                        k.I32S[:, :])
                    k.inc(i, 'v', 1)
                    while k.S['v'] - vb < 12:
                        k.inc(vec.engine_nop(), 'v', 1)
                    assert k.S['v'] - vb == 12

        setattr(self, tag + "_done_w", (self.S['wt0'], self.S['wt1']))

    def phase_l0attn_A(self):
        k = self

        @self.block.sync
        def _(sync):
            k.wait(sync, 'gk', k.cast_key)
            k.wait(sync, 'wt0', k.ln0_done_w[0])
            k.wait(sync, 'wt1', k.ln0_done_w[1])
            for dc in range(8):
                k.inc(sync.dma_start(out=k.BIGA[:, dc * 4096:(dc + 1) * 4096],
                                     in_=k.key_bf[0:N, dc * 128:(dc + 1) * 128],
                                     transpose=True), 'w', 16)
        wkey = self.S['w']

        for h in range(2):
            tprev = self.snap('t')

            @self.block.sync
            def _(sync, h=h, tprev=tprev):
                if h == 1:
                    k.wait(sync, 't', tprev)
                for dc in range(8):
                    k.inc(sync.dma_start(
                        out=k.MED0[:, dc * 1024:(dc + 1) * 1024],
                        in_=k.act0[h * 1024:(h + 1) * 1024, dc * 128:(dc + 1) * 128],
                        transpose=True), 'w', 16)
            wln = self.S['w']

            e0 = self.snap('e')
            grp_t = []

            @self.block.tensor
            def _(ten, wln=wln, e0=e0):
                k.wait(ten, 'w', max(wkey, wln))
                k.wait(ten, 'e', e0)
                for tt in range(8):
                    for nc4 in range(8):
                        u = tt * 8 + nc4
                        b = u % 4
                        if u >= 4:
                            k.wait(ten, 'e', e0 + u - 4 + 1)
                        for dc in range(8):
                            i = ten.matmul(k.pb[b][:, :],
                                           k.MED0[:, dc * 1024 + tt * 128:
                                                  dc * 1024 + (tt + 1) * 128],
                                           k.BIGA[:, dc * 4096 + nc4 * 512:
                                                  dc * 4096 + (nc4 + 1) * 512],
                                           start=(dc == 0), stop=(dc == 7))
                        grp_t.append(k.inc(i, 't', 1))

            v0 = self.snap('v')
            sg = []
            gst = [wln + (t + 1) * 16 for t in range(8)]

            @self.block.scalar
            def _(sc, h=h, v0=v0):
                if h == 0:
                    sc.preload_act_table(AF.Gelu)

                def evac(tt):
                    for nc4 in range(8):
                        u = tt * 8 + nc4
                        k.wait(sc, 't', grp_t[u])
                        if nc4 == 0 and tt >= 2:
                            k.wait(sc, 'w', gst[tt - 2])
                        A = k.BIGB[:, (tt % 2) * 4096:(tt % 2) * 4096 + 4096]
                        i = sc.activation(out=A[:, nc4 * 512:(nc4 + 1) * 512],
                                          in_=k.pb[u % 4][:, :], func=AF.Copy)
                        k.inc(i, 'e', 1)

                def gelu(tt):
                    k.wait(sc, 'v', v0 + (tt + 1) * 7)
                    A = k.BIGB[:, (tt % 2) * 4096:(tt % 2) * 4096 + 4096]
                    i = sc.activation(out=A[:, :], in_=A[:, :], func=AF.Gelu,
                                      scale=k.STATS[:, 560 + tt:561 + tt])
                    sg.append(k.inc(i, 's', 1))

                evac(0)
                for tt in range(1, 8):
                    evac(tt)
                    gelu(tt - 1)
                gelu(7)

            @self.block.vector
            def _(vec, v0=v0, e0=e0):
                for tt in range(8):
                    vb = k.S['v']
                    k.wait(vec, 'e', e0 + (tt + 1) * 8)
                    A = k.BIGB[:, (tt % 2) * 4096:(tt % 2) * 4096 + 4096]
                    for sub in range(4):
                        vec.scalar_tensor_tensor(
                            k.DMY[:, :], A[:, sub * 1024:(sub + 1) * 1024], 1.0,
                            A[:, sub * 1024:(sub + 1) * 1024], ALU.mult, ALU.mult,
                            accum_out=k.STATS[:, 736 + sub:737 + sub])
                    self._vfence(vec)
                    i = vec.reduce_sum(k.STATS[:, 608:609], k.STATS[:, 736:740], AX)
                    self._vfence(vec, i)
                    self._newton_rsqrt(vec, 608, 672, 256, 0.02)
                    i = vec.tensor_scalar(k.STATS[:, 560 + tt:561 + tt],
                                          k.STATS[:, 672:673], 64.0, None, ALU.mult)
                    k.inc(i, 'v', 1)
                    while k.S['v'] - vb < 7:
                        k.inc(vec.engine_nop(), 'v', 1)
                    assert k.S['v'] - vb == 7

            @self.block.sync
            def _(sync, h=h):
                for tt in range(8):
                    k.wait(sync, 's', sg[tt])
                    A = k.BIGB[:, (tt % 2) * 4096:(tt % 2) * 4096 + 4096]
                    mm_ = k.inc(sync.dma_start(
                        out=k.act1[(h * 8 + tt) * 128:(h * 8 + tt + 1) * 128, :],
                        in_=A[:, :]), 'w', 16)
                    assert mm_ == gst[tt], (mm_, gst[tt])
        k.G_parked_w = self.S['w']
        k.A_all_t = self.S['t']

    def phase_l0attn_O(self):
        k = self
        sprev = self.snap('s')

        @self.block.gpsimd
        def _(gp, sprev=sprev):
            k.wait(gp, 's', sprev)
            k.wait(gp, 'w', k.G_parked_w)
            for nch in range(8):
                k.inc(gp.dma_start(out=k.BIGB[:, nch * D:(nch + 1) * D],
                                   in_=k.val_w[nch * 128:(nch + 1) * 128, :]), 'gv', 16)
        gval = self.S['gv']

        for h in range(2):
            t_half0 = self.snap('t')

            @self.block.sync
            def _(sync, h=h, t_half0=t_half0):
                k.wait(sync, 'w', k.G_parked_w)
                k.wait(sync, 't', k.A_all_t if h == 0 else t_half0)
                for nc_ in range(32):
                    k.inc(sync.dma_start(
                        out=k.BIGA[:, nc_ * 1024:(nc_ + 1) * 1024],
                        in_=k.act1[h * 1024:(h + 1) * 1024, nc_ * 128:(nc_ + 1) * 128],
                        transpose=True), 'w', 16)
            wgt = self.S['w']

            e0x = self.snap('e')
            mx = []

            @self.block.sync
            def _(sync, h=h, e0x=e0x):
                for tt in range(8):
                    if tt >= 2:
                        k.wait(sync, 'e', e0x + (tt - 1) * 2)
                    mx.append(k.inc(sync.dma_start(
                        out=k.XF32[:, (tt % 2) * 1024:(tt % 2) * 1024 + 1024],
                        in_=k.x_in[(h * 8 + tt) * 128:(h * 8 + tt + 1) * 128, :]),
                        'wx%d' % (tt % 2), 16))

            e0 = self.snap('e')
            grp_t = []

            @self.block.tensor
            def _(ten, wgt=wgt, e0=e0):
                k.wait(ten, 'w', wgt)
                k.wait(ten, 'gv', gval)
                k.wait(ten, 'e', e0)
                for tt in range(8):
                    for dh in range(2):
                        u = tt * 2 + dh
                        b = u % 4
                        if u >= 4:
                            k.wait(ten, 'e', e0 + u - 4 + 1)
                        for nc_ in range(32):
                            i = ten.matmul(
                                k.pb[b][:, :],
                                k.BIGA[:, nc_ * 1024 + tt * 128:nc_ * 1024 + (tt + 1) * 128],
                                k.BIGB[:, nc_ * 1024 + dh * 512:nc_ * 1024 + dh * 512 + 512],
                                start=(nc_ == 0), stop=(nc_ == 31))
                            if (nc_ + 1) % 8 == 0 and nc_ != 31:
                                k.inc(i, 't', 1)
                        grp_t.append(k.inc(i, 't', 1))

            gsb = self.snap('g')

            @self.block.vector
            def _(vec, gsb=gsb):
                for tt in range(8):
                    for dh in range(2):
                        u = tt * 2 + dh
                        k.wait(vec, 't', grp_t[u])
                        if dh == 0:
                            k.wait(vec, 'wx%d' % (tt % 2), mx[tt])
                            if tt >= 2:
                                k.wait(vec, 'g', gsb + tt * 16)
                        i = vec.scalar_tensor_tensor(
                            k.XF32[:, 2048 + (tt % 2) * 1024 + dh * 512:
                                   2048 + (tt % 2) * 1024 + dh * 512 + 512],
                            k.pb[u % 4][:, :], 1.0,
                            k.XF32[:, (tt % 2) * 1024 + dh * 512:
                                   (tt % 2) * 1024 + dh * 512 + 512],
                            ALU.mult, ALU.add)
                        k.inc(i, 'e', 1)
            er0 = self.S['e'] - 16

            @self.block.gpsimd
            def _(gp, h=h, er0=er0, gsb=gsb):
                for tt in range(8):
                    k.wait(gp, 'e', er0 + (tt + 1) * 2)
                    mm_ = k.inc(gp.dma_start(
                        out=k.xa[(h * 8 + tt) * 128:(h * 8 + tt + 1) * 128, :],
                        in_=k.XF32[:, 2048 + (tt % 2) * 1024:2048 + (tt % 2) * 1024 + 1024]),
                        'g', 16)
                    assert mm_ == gsb + (tt + 1) * 16

    def phase_l0ffn(self):
        k = self
        ST = self.STATS

        for h in range(2):
            tprev = self.snap('t')

            @self.block.sync
            def _(sync, h=h, tprev=tprev):
                k.wait(sync, 'gq0', k.stage_gate)
                k.wait(sync, 't', tprev)
                k.wait(sync, 'gt', k.snap('gt'))
                if h == 0:
                    for dc in range(8):
                        k.inc(sync.dma_start(out=k.BIGA[:, dc * 4096:(dc + 1) * 4096],
                                             in_=k.gq_bf[0:N, dc * 128:(dc + 1) * 128],
                                             transpose=True), 'wta', 16)
                else:
                    k._biga_reload(sync, k.gqt)
                k.wait(sync, 'wt0', k.ln1_done_w[0])
                k.wait(sync, 'wt1', k.ln1_done_w[1])
                for dc in range(8):
                    k.inc(sync.dma_start(
                        out=k.MED0[:, dc * 1024:(dc + 1) * 1024],
                        in_=k.act0[h * 1024:(h + 1) * 1024, dc * 128:(dc + 1) * 128],
                        transpose=True), 'w', 16)
            wg = self.S['w']
            wtag = self.snap('wta')
            if h == 0:
                self._biga_store(k.gqt)

            sprev_v = self.snap('v')
            e0 = self.snap('e')
            grp_t = []

            @self.block.tensor
            def _(ten, wg=wg, e0=e0, wtag=wtag):
                k.wait(ten, 'w', wg)
                k.wait(ten, 'wta', wtag)
                k.wait(ten, 'e', e0)
                for tt in range(8):
                    for nc4 in range(8):
                        u = tt * 8 + nc4
                        b = u % 4
                        if u >= 4:
                            k.wait(ten, 'e', e0 + u - 4 + 1)
                        for dc in range(8):
                            i = ten.matmul(k.pb[b][:, :],
                                           k.MED0[:, dc * 1024 + tt * 128:
                                                  dc * 1024 + (tt + 1) * 128],
                                           k.BIGA[:, dc * 4096 + nc4 * 512:
                                                  dc * 4096 + (nc4 + 1) * 512],
                                           start=(dc == 0), stop=(dc == 7))
                        grp_t.append(k.inc(i, 't', 1))
            tgate = self.S['t']

            @self.block.scalar
            def _(sc, h=h):
                if h == 0:
                    sc.preload_act_table(AF.Silu)
                for tt in range(8):
                    for nc4 in range(8):
                        u = tt * 8 + nc4
                        k.wait(sc, 't', grp_t[u])
                        i = sc.activation(
                            out=k.BIGB[:, tt * 4096 + nc4 * 512:tt * 4096 + nc4 * 512 + 512],
                            in_=k.pb[u % 4][:, :], func=AF.Silu,
                            scale=ST[:, 464 + h * 8 + tt:465 + h * 8 + tt])
                        k.inc(i, 'e', 1)

            @self.block.sync
            def _(sync, h=h, tgate=tgate):
                k.wait(sync, 'gq1', k.stage_up)
                k.wait(sync, 't', tgate)
                k.wait(sync, 'gt', k.snap('gt'))
                if h == 0:
                    for dc in range(8):
                        k.inc(sync.dma_start(out=k.BIGA[:, dc * 4096:(dc + 1) * 4096],
                                             in_=k.uq_bf[0:N, dc * 128:(dc + 1) * 128],
                                             transpose=True), 'wta', 16)
                else:
                    k._biga_reload(sync, k.uqt)
            wu = self.S['w']
            wtau = self.snap('wta')
            if h == 0:
                self._biga_store(k.uqt)

            e0u = self.snap('e')
            grp_tu = []

            @self.block.tensor
            def _(ten, wu=wu, e0u=e0u, wtau=wtau):
                k.wait(ten, 'w', wu)
                k.wait(ten, 'wta', wtau)
                k.wait(ten, 'e', e0u)
                for tt in range(8):
                    for nc4 in range(8):
                        u = tt * 8 + nc4
                        b = u % 4
                        if u >= 4:
                            k.wait(ten, 'e', e0u + u - 4 + 1)
                        for dc in range(8):
                            i = ten.matmul(k.pb[b][:, :],
                                           k.MED0[:, dc * 1024 + tt * 128:
                                                  dc * 1024 + (tt + 1) * 128],
                                           k.BIGA[:, dc * 4096 + nc4 * 512:
                                                  dc * 4096 + (nc4 + 1) * 512],
                                           start=(dc == 0), stop=(dc == 7))
                        grp_tu.append(k.inc(i, 't', 1))
            tup = self.S['t']

            vq_marks = []

            @self.block.vector
            def _(vec, h=h, e0=e0):
                for tt in range(8):
                    for nc4 in range(8):
                        u = tt * 8 + nc4
                        k.wait(vec, 't', grp_tu[u])
                        k.wait(vec, 'e', e0 + u + 1)
                        sl = slice(tt * 4096 + nc4 * 512, tt * 4096 + nc4 * 512 + 512)
                        i = vec.scalar_tensor_tensor(
                            k.BIGB[:, sl], k.pb[u % 4][:, :],
                            ST[:, 480 + h * 8 + tt:481 + h * 8 + tt],
                            k.BIGB[:, sl], ALU.mult, ALU.mult)
                        k.inc(i, 'e', 1)
                    H = k.BIGB[:, tt * 4096:(tt + 1) * 4096]
                    i = vec.reduce_max(ST[:, 528:529], H, AX, apply_absolute_value=True)
                    self._vfence(vec, i)
                    vec.tensor_scalar(ST[:, 528:529], ST[:, 528:529], 1e-5, None,
                                      ALU.max)
                    i = vec.tensor_scalar(ST[:, 556:557], ST[:, 528:529],
                                          1.0 / 127.0, None, ALU.mult)
                    self._vfence(vec, i)
                    vec.tensor_scalar(ST[:, 496 + h * 8 + tt:497 + h * 8 + tt],
                                      ST[:, 556:557], ST[:, 594:595], None, ALU.mult)
                    i = vec.reciprocal(ST[:, 544:545], ST[:, 528:529])
                    self._vfence(vec, i)
                    i = vec.tensor_scalar(ST[:, 544:545], ST[:, 544:545], 127.0,
                                          None, ALU.mult)
                    self._vfence(vec, i)
                    for sub in range(4):
                        ssl = slice(tt * 4096 + sub * 1024, tt * 4096 + sub * 1024 + 1024)
                        vec.tensor_scalar(k.DMY[:, :], k.BIGB[:, ssl],
                                          ST[:, 544:545], RC, ALU.mult, ALU.add)
                        vec.tensor_scalar(k.DMY[:, :], k.DMY[:, :], -RC, None,
                                          ALU.add)
                        vec.tensor_copy(k.I32S[:, :], k.DMY[:, :])
                        i = vec.tensor_copy(k.BIGB[:, ssl], k.I32S[:, :])
                    vq_marks.append(k.inc(i, 'v', 1))

            @self.block.gpsimd
            def _(gp, h=h):
                for tt in range(8):
                    k.wait(gp, 'v', vq_marks[tt])
                    k.inc(gp.dma_start(
                        out=k.act1[(h * 8 + tt) * 128:(h * 8 + tt + 1) * 128, :],
                        in_=k.BIGB[:, tt * 4096:(tt + 1) * 4096]), 'g', 16)
            gc2 = self.S['g']

            @self.block.sync
            def _(sync, h=h, tup=tup, gc2=gc2):
                k.wait(sync, 'gq2', k.stage_down)
                k.wait(sync, 'g', gc2)
                k.wait(sync, 't', tup)
                k.wait(sync, 'gt', k.snap('gt'))
                if h == 0:
                    for nc_ in range(32):
                        k.inc(sync.dma_start(out=k.BIGA[:, nc_ * 1024:(nc_ + 1) * 1024],
                                             in_=k.dq_bf[0:D, nc_ * 128:(nc_ + 1) * 128],
                                             transpose=True), 'wta', 16)
                else:
                    k._biga_reload(sync, k.dqt)
                for nc_ in range(32):
                    k.inc(sync.dma_start(
                        out=k.BIGB[:, nc_ * 1024:(nc_ + 1) * 1024],
                        in_=k.act1[h * 1024:(h + 1) * 1024, nc_ * 128:(nc_ + 1) * 128],
                        transpose=True), 'w', 16)
            wd = self.S['w']
            wtad = self.snap('wta')
            if h == 0:
                self._biga_store(k.dqt)

            e0d = self.snap('e')

            mx = []

            @self.block.sync
            def _(sync, h=h, e0d=e0d, gc2=gc2):
                for tt in range(8):
                    if tt >= 2:
                        k.wait(sync, 'e', e0d + (tt - 1) * 2)
                    elif tt == 0:
                        k.wait(sync, 'g', gc2)
                    mx.append(k.inc(sync.dma_start(
                        out=k.XF32[:, (tt % 2) * 1024:(tt % 2) * 1024 + 1024],
                        in_=k.xa[(h * 8 + tt) * 128:(h * 8 + tt + 1) * 128, :]),
                        'wx%d' % (tt % 2), 16))
            grp_td = []

            @self.block.tensor
            def _(ten, wd=wd, e0d=e0d, wtad=wtad):
                k.wait(ten, 'w', wd)
                k.wait(ten, 'wta', wtad)
                k.wait(ten, 'e', e0d)
                for tt in range(8):
                    for dh in range(2):
                        u = tt * 2 + dh
                        b = u % 4
                        if u >= 4:
                            k.wait(ten, 'e', e0d + u - 4 + 1)
                        for nc_ in range(32):
                            i = ten.matmul(
                                k.pb[b][:, :],
                                k.BIGB[:, nc_ * 1024 + tt * 128:nc_ * 1024 + (tt + 1) * 128],
                                k.BIGA[:, nc_ * 1024 + dh * 512:nc_ * 1024 + dh * 512 + 512],
                                start=(nc_ == 0), stop=(nc_ == 31))
                            if (nc_ + 1) % 8 == 0 and nc_ != 31:
                                k.inc(i, 't', 1)
                        grp_td.append(k.inc(i, 't', 1))

            gsb = self.snap('g')

            @self.block.vector
            def _(vec, h=h, gsb=gsb):
                for tt in range(8):
                    for dh in range(2):
                        u = tt * 2 + dh
                        k.wait(vec, 't', grp_td[u])
                        if dh == 0:
                            k.wait(vec, 'wx%d' % (tt % 2), mx[tt])
                            if tt >= 2:
                                k.wait(vec, 'g', gsb + tt * 16)
                        i = vec.scalar_tensor_tensor(
                            k.XF32[:, 2048 + (tt % 2) * 1024 + dh * 512:
                                   2048 + (tt % 2) * 1024 + dh * 512 + 512],
                            k.pb[u % 4][:, :],
                            ST[:, 496 + h * 8 + tt:497 + h * 8 + tt],
                            k.XF32[:, (tt % 2) * 1024 + dh * 512:
                                   (tt % 2) * 1024 + dh * 512 + 512],
                            ALU.mult, ALU.add)
                        k.inc(i, 'e', 1)
            er0 = self.S['e'] - 16

            @self.block.gpsimd
            def _(gp, h=h, er0=er0, gsb=gsb):
                for tt in range(8):
                    k.wait(gp, 'e', er0 + (tt + 1) * 2)
                    mm_ = k.inc(gp.dma_start(
                        out=k.xb[(h * 8 + tt) * 128:(h * 8 + tt + 1) * 128, :],
                        in_=k.XF32[:, 2048 + (tt % 2) * 1024:2048 + (tt % 2) * 1024 + 1024]),
                        'g', 16)
                    assert mm_ == gsb + (tt + 1) * 16

    def phase_l1attn_qkv(self):
        k = self

        @self.block.gpsimd
        def _(gp):
            for dc in range(8):
                k.inc(gp.dma_start(out=k.MED1[:, dc * 128:(dc + 1) * 128],
                                   in_=k.wq[dc * 128:(dc + 1) * 128, :]), 'g', 16)
                k.inc(gp.dma_start(out=k.MED1[:, 1024 + dc * 128:1024 + (dc + 1) * 128],
                                   in_=k.wk[dc * 128:(dc + 1) * 128, :]), 'g', 16)
                k.inc(gp.dma_start(out=k.MED1[:, 2048 + dc * 128:2048 + (dc + 1) * 128],
                                   in_=k.wv[dc * 128:(dc + 1) * 128, :]), 'g', 16)
            k.inc(gp.dma_start(out=k.MED0[:, 2048:3072], in_=k.wo[:, :]), 'g', 16)
        gqkv = self.S['g']

        tprev = self.snap('t')

        @self.block.sync
        def _(sync, tprev=tprev):
            k.wait(sync, 'wt0', k.ln2_done_w[0])
            k.wait(sync, 'wt1', k.ln2_done_w[1])
            k.wait(sync, 't', tprev)
            for dc in range(8):
                k.inc(sync.dma_start(
                    out=k.BIGB[:, dc * 2048:(dc + 1) * 2048],
                    in_=k.act0[0:TOK, dc * 128:(dc + 1) * 128], transpose=True),
                    'w', 16)
        wln2 = self.S['w']

        e0 = self.snap('e')
        grp_t = []

        @self.block.tensor
        def _(ten, e0=e0):
            k.wait(ten, 'w', wln2)
            k.wait(ten, 'g', gqkv)
            k.wait(ten, 'e', e0)
            u = 0
            for woff in (0, 1024):
                for qc4 in range(4):
                    b = u % 4
                    if u >= 4:
                        k.wait(ten, 'e', e0 + u - 4 + 1)
                    for dc in range(8):
                        i = ten.matmul(k.pb[b][:, :],
                                       k.MED1[:, woff + dc * 128:woff + (dc + 1) * 128],
                                       k.BIGB[:, dc * 2048 + qc4 * 512:
                                              dc * 2048 + (qc4 + 1) * 512],
                                       start=(dc == 0), stop=(dc == 7))
                    grp_t.append(k.inc(i, 't', 1))
                    u += 1
            for tt in range(NT):
                b = u % 4
                if u >= 4:
                    k.wait(ten, 'e', e0 + u - 4 + 1)
                for dc in range(8):
                    i = ten.matmul(k.pb[b][:, 0:128],
                                   k.BIGB[:, dc * 2048 + tt * 128:dc * 2048 + (tt + 1) * 128],
                                   k.MED1[:, 2048 + dc * 128:2048 + (dc + 1) * 128],
                                   start=(dc == 0), stop=(dc == 7))
                grp_t.append(k.inc(i, 't', 1))
                u += 1

        @self.block.scalar
        def _(sc):
            u = 0
            for which in range(2):
                for qc4 in range(4):
                    k.wait(sc, 't', grp_t[u])
                    if which == 0:
                        o = k.BIGB[:, 16384 + qc4 * 512:16384 + (qc4 + 1) * 512]
                    else:
                        o = k.MED0[:, qc4 * 512:(qc4 + 1) * 512]
                    i = sc.activation(out=o, in_=k.pb[u % 4][:, :], func=AF.Copy)
                    k.inc(i, 'e', 1)
                    u += 1
            for tt in range(NT):
                k.wait(sc, 't', grp_t[u])
                i = sc.activation(out=k.BIGB[:, 18432 + tt * 128:18432 + (tt + 1) * 128],
                                  in_=k.pb[u % 4][:, 0:128], func=AF.Copy)
                k.inc(i, 'e', 1)
                u += 1
        eqkv = self.S['e']

        @self.block.gpsimd
        def _(gp, eqkv=eqkv):
            k.wait(gp, 'e', eqkv)
            k.inc(gp.dma_start(out=k.kv_in.rearrange("r (p f) -> r p f", p=128)[0, :, :],
                               in_=k.MED0[:, 0:2048]), 'g', 16)
            k.inc(gp.dma_start(out=k.kv_in.rearrange("r (p f) -> r p f", p=128)[1, :, :],
                               in_=k.BIGB[:, 18432:20480]), 'g', 16)
            k.wait(gp, 'g', k.S['g'])
            k.inc(gp.collective_compute(
                "AllGather", ALU.bypass, replica_groups=[list(range(NCORE))],
                ins=[k.kv_in.opt()], outs=[k.kv_out.opt()]), 'c', 1)
        cagg = self.S['c']

        tqkv = self.snap('t')

        @self.block.sync
        def _(sync, tqkv=tqkv, cagg=cagg):
            k.wait(sync, 'c', cagg)
            k.wait(sync, 't', tqkv)
            k.wait(sync, 'gt', k.snap('gt'))
            for cc in range(8):
                k.inc(sync.dma_start(
                    out=k.BIGA[:, cc * 2048:(cc + 1) * 2048],
                    in_=k.kv_out.rearrange("r (p f) -> r p f", p=128)[2 * cc, :, :]),
                    'w', 16)
                k.inc(sync.dma_start(
                    out=k.BIGA[:, 16384 + cc * 2048:16384 + (cc + 1) * 2048],
                    in_=k.kv_out.rearrange("r (p f) -> r p f", p=128)[2 * cc + 1, :, :]),
                    'w', 16)
            k.wait(sync, 'e', k.snap('e'))
            k.wait(sync, 'gq3', k.stage_w3)
            k.inc(sync.dma_start(out=k.MED1[:, 0:4096], in_=k.cmask_in[:, :]), 'w', 16)
        k.attn_w = self.S['w']

    def phase_l1attn(self):
        k = self
        ST = self.STATS
        SC = f32(1.0 / np.sqrt(128.0))
        BI = f32(-4.0)
        e_attn0 = self.snap('e')
        g_xb = self.snap('g')
        k.ht_ms = []
        k.gs0 = self.snap('gs')

        LOOKAHEAD = 4
        for qc in range(8):
            nk = 16 * (qc + 1)
            masked0 = 16 * qc
            t0 = self.snap('t')
            s0 = self.snap('s')
            v0 = self.snap('v')
            w0 = self.snap('w')
            t_S = [0] * nk
            t_PV = [0] * nk
            _t = t0
            for jj in range(nk + LOOKAHEAD):
                if jj >= LOOKAHEAD:
                    _t += 1
                    t_PV[jj - LOOKAHEAD] = _t
                if jj < nk:
                    _t += 1
                    t_S[jj] = _t
            s_exp = [s0 + j + 1 for j in range(nk)]
            v_after = []
            vv = v0
            for j in range(nk):
                if j >= masked0:
                    vv += 1
                vv += 1
                v_after.append(vv)
            t_lT = t_PV[-1] + 2
            t_wo = [t_lT + 1 + uu for uu in range(4)]

            mxa = []

            @self.block.sync
            def _(sync, qc=qc):
                k.wait(sync, 'w', k.attn_w)
                k.wait(sync, 'g', g_xb)
                if qc > 0:
                    k.wait(sync, 'e', e_attn0 + qc * 5)
                for u in range(2):
                    tt = 2 * qc + u
                    mxa.append(k.inc(sync.dma_start(
                        out=k.XF32[:, u * 1024:(u + 1) * 1024],
                        in_=k.xb[tt * 128:(tt + 1) * 128, :]), 'wx%d' % u, 16))

            @self.block.tensor
            def _(ten, qc=qc, nk=nk, masked0=masked0):
                k.wait(ten, 'w', k.attn_w)
                Sslots = [k.pS[:, 0:256], k.pb[4][:, 0:256],
                          k.pb[0][:, 0:256], k.pb[1][:, 0:256]]
                for jj in range(nk + LOOKAHEAD):
                    if jj >= LOOKAHEAD:
                        j = jj - LOOKAHEAD
                        t2, c2 = j // 8, j % 8
                        if j >= masked0:
                            k.wait(ten, 'v', v_after[j])
                        else:
                            k.wait(ten, 's', s_exp[j])
                        if j == 0 and qc > 0:
                            k.wait(ten, 'e', e_attn0 + (qc - 1) * 5 + 1)
                        i = ten.matmul(k.pP[:, 0:256],
                                       k.BIGA[:, 16384 + (c2 * 16 + t2) * 128:
                                              16384 + (c2 * 16 + t2 + 1) * 128],
                                       k.MED0[:, 4096 + (j % 8) * 256:
                                              4096 + (j % 8) * 256 + 256],
                                       start=(j == 0), stop=(j == nk - 1))
                        assert k.inc(i, 't', 1) == t_PV[j]
                    if jj < nk:
                        t2, c2 = jj // 8, jj % 8
                        if jj >= 4:
                            k.wait(ten, 's', s_exp[jj - 4])
                        elif qc > 0 and jj in (2, 3):
                            k.wait(ten, 'e', e_attn0 + (qc - 1) * 5 + jj)
                        i = ten.matmul(Sslots[jj % 4],
                                       k.BIGA[:, c2 * 2048 + t2 * 128:
                                              c2 * 2048 + (t2 + 1) * 128],
                                       k.BIGB[:, 16384 + qc * 256:16384 + (qc + 1) * 256],
                                       start=True, stop=True)
                        assert k.inc(i, 't', 1) == t_S[jj]
                k.wait(ten, 'v', v_after[nk - 1])
                i = ten.transpose(k.pT[:, 0:128], ST[:, 768:896], k.EYE[:])
                k.inc(i, 't', 1)
                i = ten.transpose(k.pT[:, 128:256], ST[:, 896:1024], k.EYE[:])
                k.inc(i, 't', 1)
                k.wait(ten, 'e', e_attn0 + qc * 5 + 1)
                for u in range(2):
                    for dh in range(2):
                        if qc > 0:
                            k.wait(ten, 'e', e_attn0 + qc * 5)
                        i = ten.matmul(k.pb[u * 2 + dh][:, :],
                                       k.MED0[:, 6144 + u * 128:6144 + (u + 1) * 128],
                                       k.MED0[:, 2048 + dh * 512:2048 + dh * 512 + 512],
                                       start=True, stop=True)
                        k.inc(i, 't', 1)
                assert k.S['t'] == t_wo[-1], (k.S['t'], t_wo)

            @self.block.scalar
            def _(sc, qc=qc, nk=nk):
                if qc == 0:
                    sc.preload_act_table(AF.Exp)
                for j in range(nk):
                    k.wait(sc, 't', t_S[j])
                    if j >= 8:
                        k.wait(sc, 't', t_PV[j - 8])
                        k.wait(sc, 'v', v_after[j - 8])
                    Sslot = [k.pS[:, 0:256], k.pb[4][:, 0:256],
                             k.pb[0][:, 0:256], k.pb[1][:, 0:256]][j % 4]
                    i = sc.activation(
                        out=k.MED0[:, 4096 + (j % 8) * 256:4096 + (j % 8) * 256 + 256],
                        in_=Sslot, func=AF.Exp, scale=SC, bias=BI)
                    k.inc(i, 's', 1)
                k.wait(sc, 't', t_PV[nk - 1])
                i = sc.activation(out=k.MED0[:, 6144:6400], in_=k.pP[:, 0:256],
                                  func=AF.Copy)
                k.inc(i, 'e', 1)

            @self.block.vector
            def _(vec, qc=qc, nk=nk, masked0=masked0):
                for j in range(nk):
                    k.wait(vec, 's', s_exp[j])
                    P = k.MED0[:, 4096 + (j % 8) * 256:4096 + (j % 8) * 256 + 256]
                    if j >= masked0:
                        mi = j - masked0
                        i = vec.tensor_tensor(P, P, k.MED1[:, mi * 256:(mi + 1) * 256],
                                              ALU.mult)
                        k.inc(i, 'v', 1)
                    if j == 0:
                        i = vec.tensor_scalar(ST[:, 768:1024], P, 1.0, None, ALU.mult)
                    else:
                        i = vec.tensor_tensor(ST[:, 768:1024], ST[:, 768:1024], P,
                                              ALU.add)
                    k.inc(i, 'v', 1)
                    assert k.S['v'] == v_after[j]
                k.wait(vec, 't', t_lT)
                vec.reduce_sum(ST[:, 512:513], k.pT[:, 0:128], AX)
                i = vec.reduce_sum(ST[:, 513:514], k.pT[:, 128:256], AX)
                self._vfence(vec, i)
                i = vec.reciprocal(ST[:, 514:516], ST[:, 512:514])
                self._vfence(vec, i)
                for u in range(2):
                    k.wait(vec, 'wx%d' % u, mxa[u])
                    if qc > 0:
                        k.wait(vec, 'g', g_xb + qc * 32)
                    for dh in range(2):
                        k.wait(vec, 't', t_wo[u * 2 + dh])
                        i = vec.scalar_tensor_tensor(
                            k.XF32[:, 2048 + u * 1024 + dh * 512:
                                   2048 + u * 1024 + dh * 512 + 512],
                            k.pb[u * 2 + dh][:, :],
                            ST[:, 514 + u:515 + u],
                            k.XF32[:, u * 1024 + dh * 512:u * 1024 + dh * 512 + 512],
                            ALU.mult, ALU.add)
                        k.inc(i, 'e', 1)
            ewo0 = self.S['e'] - 4

            @self.block.gpsimd
            def _(gp, qc=qc, ewo0=ewo0):
                for u in range(2):
                    k.wait(gp, 'e', ewo0 + (u + 1) * 2)
                    tt = 2 * qc + u
                    mm_ = k.inc(gp.dma_start(out=k.xa[tt * 128:(tt + 1) * 128, :],
                                             in_=k.XF32[:, 2048 + u * 1024:2048 + (u + 1) * 1024]),
                                'g', 16)
                    assert mm_ == g_xb + qc * 32 + (u + 1) * 16

            _gsz = [2, 4, 6, 8, 10, 10, 12, 12]
            _g0 = sum(_gsz[:qc])
            self._ht_stage_gap(range(_g0, _g0 + _gsz[qc]))
        k.stage_ht = self.S['gs']

    def _ht_stage_gap(self, strips):
        k = self

        @self.block.sync
        def _(sync, strips=strips):
            for ni in strips:
                wsrc = k.h1_bf if ni < 32 else k.h2_bf
                col = (ni % 32) * 128
                if ni == 0:
                    k.wait(sync, 'gh1', k.cast_h1)
                    k.wait(sync, 'gq0', k.stage_gate)
                    k.wait(sync, 'gq1', k.stage_up)
                    k.wait(sync, 'gq2', k.stage_down)
                if ni == 32:
                    k.wait(sync, 'gh2', k.cast_h2)
                if ni >= 1:
                    k.wait(sync, 'gs', k.gs0 + ni * 16)
                k.ht_ms.append(k.inc(sync.dma_start(
                    out=k.MED1[:, 4096:8192],
                    in_=wsrc[0:N, col:col + 128], transpose=True), 'ws0', 16))

        @self.block.gpsimd
        def _(gp, strips=strips):
            for ni in strips:
                wdst = k.h1t if ni < 32 else k.h2t
                col = (ni % 32) * 128
                k.wait(gp, 'ws0', k.ht_ms[ni])
                k.inc(gp.dma_start(
                    out=wdst[:, col:col + 128, :].rearrange("g p c -> p g c"),
                    in_=k.MED1[:, 4096:8192].rearrange("p (g c) -> p g c", g=8)),
                    'gs', 16)

    def phase_l1ffn(self):
        k = self

        for h in range(2):
            for wi, wbf in enumerate((k.w1_bf, k.w2_bf)):
                tprev = self.snap('t')
                vprev = self.snap('v')

                @self.block.sync
                def _(sync, wbf=wbf, tprev=tprev, wi=wi, h=h, vprev=vprev):
                    k.wait(sync, 'gw', k.cast_w12)
                    k.wait(sync, 't', tprev)
                    k.wait(sync, 'gt', k.snap('gt'))
                    if wi == 0:
                        k.wait(sync, 'v', vprev)
                    if h == 0:
                        for dc in range(8):
                            k.inc(sync.dma_start(
                                out=k.BIGA[:, dc * 4096:(dc + 1) * 4096],
                                in_=wbf[0:N, dc * 128:(dc + 1) * 128],
                                transpose=True), 'wta', 16)
                    else:
                        k._biga_reload(sync, k.w1tt if wi == 0 else k.w2tt)
                    if wi == 0:
                        k.wait(sync, 'wt0', k.ln3_done_w[0])
                        k.wait(sync, 'wt1', k.ln3_done_w[1])
                        for dc in range(8):
                            k.inc(sync.dma_start(
                                out=k.MED0[:, dc * 1024:(dc + 1) * 1024],
                                in_=k.act0[h * 1024:(h + 1) * 1024,
                                           dc * 128:(dc + 1) * 128],
                                transpose=True), 'w', 16)
                ww = self.S['w']
                wtaw = self.snap('wta')
                if h == 0:
                    self._biga_store(k.w1tt if wi == 0 else k.w2tt)

                e0 = self.snap('e')
                grp_t = []

                @self.block.tensor
                def _(ten, ww=ww, e0=e0, wtaw=wtaw):
                    k.wait(ten, 'w', ww)
                    k.wait(ten, 'wta', wtaw)
                    k.wait(ten, 'e', e0)
                    u = 0
                    for no in range(32):
                        for tc in range(2):
                            b = u % 4
                            if u >= 4:
                                k.wait(ten, 'e', e0 + u - 4 + 1)
                            for dc in range(8):
                                i = ten.matmul(
                                    k.pb[b][:, :],
                                    k.BIGA[:, dc * 4096 + no * 128:dc * 4096 + (no + 1) * 128],
                                    k.MED0[:, dc * 1024 + tc * 512:dc * 1024 + tc * 512 + 512],
                                    start=(dc == 0), stop=(dc == 7))
                            grp_t.append(k.inc(i, 't', 1))
                            u += 1

                if wi == 0:
                    k.w1_e0 = self.snap('e')

                    @self.block.scalar
                    def _(sc, h=h):
                        if h == 0:
                            sc.preload_act_table(AF.Silu)
                        u = 0
                        for no in range(32):
                            for tc in range(2):
                                k.wait(sc, 't', grp_t[u])
                                i = sc.activation(
                                    out=k.BIGB[:, no * 1024 + tc * 512:
                                               no * 1024 + tc * 512 + 512],
                                    in_=k.pb[u % 4][:, :], func=AF.Silu,
                                    scale=k.SVEC[:, no:no + 1])
                                k.inc(i, 'e', 1)
                                u += 1
                else:
                    @self.block.vector
                    def _(vec, e0=e0):
                        u = 0
                        for no in range(32):
                            for tc in range(2):
                                k.wait(vec, 't', grp_t[u])
                                k.wait(vec, 'e', k.w1_e0 + u + 1)
                                sl = slice(no * 1024 + tc * 512,
                                           no * 1024 + tc * 512 + 512)
                                i = vec.scalar_tensor_tensor(
                                    k.BIGB[:, sl], k.pb[u % 4][:, :],
                                    k.SVEC[:, 32 + no:33 + no],
                                    k.BIGB[:, sl], ALU.mult, ALU.mult)
                                k.inc(i, 'e', 1)
                                u += 1

            for li, (bandsrc, park, scol) in enumerate(
                    ((k.h1t, k.gpark, 64), (k.h2t, k.kpark, 96))):
                if li == 1:
                    tprev = self.snap('t')
                    gprev = self.snap('g')

                    @self.block.sync
                    def _(sync, h=h, tprev=tprev, gprev=gprev):
                        k.wait(sync, 't', tprev)
                        k.wait(sync, 'g', gprev)
                        for nc_ in range(32):
                            k.inc(sync.dma_start(
                                out=k.BIGB[:, nc_ * 1024:(nc_ + 1) * 1024],
                                in_=k.gpark[nc_ * 128:(nc_ + 1) * 128,
                                            h * 1024:(h + 1) * 1024]), 'w', 16)
                wh = self.S['w']
                t_b0 = self.snap('t')
                tprev2 = self.snap('t')

                mb = []

                @self.block.sync
                def _(sync, bandsrc=bandsrc, wh=wh, t_b0=t_b0):
                    k.wait(sync, 'gs', k.stage_ht)
                    k.wait(sync, 'w', wh)
                    k.wait(sync, 't', t_b0)
                    k.wait(sync, 'gt', k.snap('gt'))
                    for bg in range(8):
                        if bg >= 2:
                            k.wait(sync, 't', t_b0 + (bg - 1) * 32)
                        mb.append(k.inc(sync.dma_start(
                            out=k.BIGA[:, (bg % 2) * 16384:(bg % 2) * 16384 + 16384].rearrange(
                                "p (n c) -> p n c", c=512),
                            in_=bandsrc.rearrange("b (n p) c -> b p n c", p=128)[bg, :, :, :]),
                            'wb%d' % (bg % 2), 16))

                e0 = self.snap('e')
                g_park0 = self.snap('g')
                grp_t = []

                @self.block.tensor
                def _(ten, e0=e0):
                    k.wait(ten, 'e', e0)
                    u = 0
                    for bg in range(8):
                        k.wait(ten, 'wb%d' % (bg % 2), mb[bg])
                        for no_l in range(4):
                            for tc in range(2):
                                b = u % 4
                                if u >= 4:
                                    k.wait(ten, 'e', e0 + u - 4 + 1)
                                for ni in range(32):
                                    i = ten.matmul(
                                        k.pb[b][:, :],
                                        k.BIGA[:, (bg % 2) * 16384 + ni * 512 + no_l * 128:
                                               (bg % 2) * 16384 + ni * 512 + (no_l + 1) * 128],
                                        k.BIGB[:, ni * 1024 + tc * 512:
                                               ni * 1024 + tc * 512 + 512],
                                        start=(ni == 0), stop=(ni == 31))
                                    if (ni + 1) % 8 == 0 and ni != 31:
                                        k.inc(i, 't', 1)
                                grp_t.append(k.inc(i, 't', 1))
                                u += 1
                assert self.S['t'] - t_b0 == 8 * 8 * 4

                @self.block.scalar
                def _(sc, h=h, li=li, scol=scol, g_park0=g_park0):
                    if li == 0 and h == 0:
                        sc.preload_act_table(AF.Gelu)
                    u = 0
                    for bg in range(8):
                        for no_l in range(4):
                            for tc in range(2):
                                k.wait(sc, 't', grp_t[u])
                                if u >= 2:
                                    k.wait(sc, 'g', g_park0 + u * 16)
                                i = sc.activation(
                                    out=k.DMY[:, (u % 2) * 512:(u % 2) * 512 + 512],
                                    in_=k.pb[u % 4][:, :], func=AF.Gelu,
                                    scale=k.SVEC[:, scol + bg * 4 + no_l:
                                                 scol + bg * 4 + no_l + 1])
                                k.inc(i, 'e', 1)
                                u += 1
                e_ev0 = self.S['e'] - 64

                @self.block.gpsimd
                def _(gp, h=h, park=park, e_ev0=e_ev0):
                    u = 0
                    for bg in range(8):
                        for no_l in range(4):
                            for tc in range(2):
                                k.wait(gp, 'e', e_ev0 + u + 1)
                                no = bg * 4 + no_l
                                k.inc(gp.dma_start(
                                    out=park[no * 128:(no + 1) * 128,
                                             h * 1024 + tc * 512:h * 1024 + tc * 512 + 512],
                                    in_=k.DMY[:, (u % 2) * 512:(u % 2) * 512 + 512]),
                                    'g', 16)
                                u += 1

            tprev = self.snap('t')
            gprev = self.snap('g')

            @self.block.sync
            def _(sync, h=h, tprev=tprev, gprev=gprev):
                k.wait(sync, 'gq3', k.stage_w3)
                k.wait(sync, 'g', gprev)
                k.wait(sync, 't', tprev)
                k.wait(sync, 'gt', k.snap('gt'))
                if h == 0:
                    for nc_ in range(32):
                        k.inc(sync.dma_start(out=k.BIGA[:, nc_ * 1024:(nc_ + 1) * 1024],
                                             in_=k.w3s_bf[0:D, nc_ * 128:(nc_ + 1) * 128],
                                             transpose=True), 'wta', 16)
                else:
                    k._biga_reload(sync, k.w3tt)
                for nc_ in range(32):
                    k.inc(sync.dma_start(
                        out=k.BIGB[:, nc_ * 1024:(nc_ + 1) * 1024],
                        in_=k.kpark[nc_ * 128:(nc_ + 1) * 128,
                                    h * 1024:(h + 1) * 1024]), 'w', 16)
            w3w = self.S['w']
            wtaw3 = self.snap('wta')
            if h == 0:
                self._biga_store(k.w3tt)

            e0w = self.snap('e')
            g_xa3 = self.snap('g')

            mx = []

            @self.block.sync
            def _(sync, h=h, e0w=e0w, g_xa3=g_xa3):
                for tt in range(8):
                    if tt >= 2:
                        k.wait(sync, 'e', e0w + (tt - 1) * 2)
                    elif tt == 0:
                        k.wait(sync, 'g', g_xa3)
                    mx.append(k.inc(sync.dma_start(
                        out=k.XF32[:, (tt % 2) * 1024:(tt % 2) * 1024 + 1024],
                        in_=k.xa[(h * 8 + tt) * 128:(h * 8 + tt + 1) * 128, :]),
                        'wx%d' % (tt % 2), 16))
            grp_tw = []

            @self.block.tensor
            def _(ten, w3w=w3w, e0w=e0w, wtaw3=wtaw3):
                k.wait(ten, 'w', w3w)
                k.wait(ten, 'wta', wtaw3)
                k.wait(ten, 'e', e0w)
                for tt in range(8):
                    for dh in range(2):
                        u = tt * 2 + dh
                        b = u % 4
                        if u >= 4:
                            k.wait(ten, 'e', e0w + u - 4 + 1)
                        for nc_ in range(32):
                            i = ten.matmul(
                                k.pb[b][:, :],
                                k.BIGB[:, nc_ * 1024 + tt * 128:nc_ * 1024 + (tt + 1) * 128],
                                k.BIGA[:, nc_ * 1024 + dh * 512:nc_ * 1024 + dh * 512 + 512],
                                start=(nc_ == 0), stop=(nc_ == 31))
                            if (nc_ + 1) % 8 == 0 and nc_ != 31:
                                k.inc(i, 't', 1)
                        grp_tw.append(k.inc(i, 't', 1))

            gsb = self.snap('g')

            @self.block.vector
            def _(vec, gsb=gsb):
                for tt in range(8):
                    for dh in range(2):
                        u = tt * 2 + dh
                        k.wait(vec, 't', grp_tw[u])
                        if dh == 0:
                            k.wait(vec, 'wx%d' % (tt % 2), mx[tt])
                            if tt >= 2:
                                k.wait(vec, 'g', gsb + tt * 16)
                        i = vec.scalar_tensor_tensor(
                            k.XF32[:, 2048 + (tt % 2) * 1024 + dh * 512:
                                   2048 + (tt % 2) * 1024 + dh * 512 + 512],
                            k.pb[u % 4][:, :], 1.0,
                            k.XF32[:, (tt % 2) * 1024 + dh * 512:
                                   (tt % 2) * 1024 + dh * 512 + 512],
                            ALU.mult, ALU.add)
                        k.inc(i, 'e', 1)
            er0 = self.S['e'] - 16

            @self.block.gpsimd
            def _(gp, h=h, er0=er0, gsb=gsb):
                for tt in range(8):
                    k.wait(gp, 'e', er0 + (tt + 1) * 2)
                    mm_ = k.inc(gp.dma_start(
                        out=k.out[(h * 8 + tt) * 128:(h * 8 + tt + 1) * 128, :],
                        in_=k.XF32[:, 2048 + (tt % 2) * 1024:2048 + (tt % 2) * 1024 + 1024]),
                        'g', 16)
                    assert mm_ == gsb + (tt + 1) * 16

    def phase_finish(self):
        k = self

        @self.block.gpsimd
        def _(gp):
            k.wait(gp, 'g')

        @self.block.sync
        def _(sync):
            k.wait(sync, 'w')


_NC_CACHE = {}


def _build(debug=False):
    import os
    key = (debug, os.environ.get("KPHASES", "99"), os.environ.get("KLN_NOSQ"), os.environ.get("KLN_NONEWT"), os.environ.get("KLN_PLAIN"))
    if key not in _NC_CACHE:
        kb = KB(debug=debug)
        nc = kb.build()
        _NC_CACHE[key] = nc
    return _NC_CACHE[key]


def _make_consts(core):
    eye = np.eye(128, dtype=np.float32)
    ones = np.ones((128, 128), dtype=np.float32)
    tri = np.triu(np.ones((128, 128), dtype=np.float32))
    cm = np.zeros((16, 128, 256), dtype=np.float32)
    for c2 in range(8):
        blk = np.zeros((128, 128), np.float32)
        if c2 < core:
            blk[:] = 1.0
        elif c2 == core:
            blk[:] = tri
        cm[c2, :, 0:128] = blk
        cm[c2, :, 128:256] = 1.0
        cm[8 + c2, :, 128:256] = blk
    cmask = np.ascontiguousarray(
        cm.transpose(1, 0, 2).reshape(128, 4096)).astype(ml_dtypes.bfloat16)
    return eye, ones, cmask


def _make_svec(w1_s, w2_s, h1_s, h2_s, w3_s):
    sv = np.zeros((128, 144), dtype=np.float32)
    sv[:, 0:32] = np.ascontiguousarray(w1_s.reshape(32, 128).T)
    sv[:, 32:64] = np.ascontiguousarray(w2_s.reshape(32, 128).T)
    sv[:, 64:96] = np.ascontiguousarray(h1_s.reshape(32, 128).T)
    sv[:, 96:128] = np.ascontiguousarray(h2_s.reshape(32, 128).T)
    sv[:, 128:136] = np.ascontiguousarray(w3_s.reshape(8, 128).T)
    return sv


def kernel(L0_attn_inputs, L0_attn_key_param_tokens, L0_attn_value_param_tokens,
           L0_ffn_w_gate, L0_ffn_w_up, L0_ffn_w_down,
           L1_attn_Wq, L1_attn_Wk, L1_attn_Wv, L1_attn_Wo,
           L1_ffn_w1_w, L1_ffn_w1_s, L1_ffn_w2_w, L1_ffn_w2_s,
           L1_ffn_h1_w, L1_ffn_h1_s, L1_ffn_h2_w, L1_ffn_h2_s,
           L1_ffn_w3_w, L1_ffn_w3_s, _debug=False):
    nc = _build(debug=_debug)
    x = np.ascontiguousarray(np.asarray(L0_attn_inputs, dtype=np.float32)
                             .reshape(128, 128, D))
    svec = _make_svec(np.asarray(L1_ffn_w1_s, np.float32),
                      np.asarray(L1_ffn_w2_s, np.float32),
                      np.asarray(L1_ffn_h1_s, np.float32),
                      np.asarray(L1_ffn_h2_s, np.float32),
                      np.asarray(L1_ffn_w3_s, np.float32))
    shared = {
        "L0_attn_key_param_tokens": np.ascontiguousarray(L0_attn_key_param_tokens),
        "L0_attn_value_param_tokens": np.ascontiguousarray(L0_attn_value_param_tokens),
        "L0_ffn_w_gate": np.ascontiguousarray(L0_ffn_w_gate),
        "L0_ffn_w_up": np.ascontiguousarray(L0_ffn_w_up),
        "L0_ffn_w_down": np.ascontiguousarray(L0_ffn_w_down),
        "L1_attn_Wq": np.ascontiguousarray(L1_attn_Wq),
        "L1_attn_Wk": np.ascontiguousarray(L1_attn_Wk),
        "L1_attn_Wv": np.ascontiguousarray(L1_attn_Wv),
        "L1_attn_Wo": np.ascontiguousarray(L1_attn_Wo),
        "L1_ffn_w1_w": np.ascontiguousarray(L1_ffn_w1_w),
        "L1_ffn_w2_w": np.ascontiguousarray(L1_ffn_w2_w),
        "L1_ffn_h1_w": np.ascontiguousarray(L1_ffn_h1_w),
        "L1_ffn_h2_w": np.ascontiguousarray(L1_ffn_h2_w),
        "L1_ffn_w3_w": np.ascontiguousarray(L1_ffn_w3_w),
        "cst_svec": svec,
    }
    in_maps = []
    for c in range(NCORE):
        eye, ones, cmask = _make_consts(c)
        m = dict(shared)
        m["L0_attn_inputs"] = np.ascontiguousarray(x[c::8].reshape(TOK, D))
        m["cst_eye"] = eye
        m["cst_ones"] = ones
        m["cst_cmask"] = cmask
        in_maps.append(m)
    res = run_bass_kernel_spmd(nc, in_maps, core_ids=list(range(NCORE)))
    full = np.empty((128, 128, D), dtype=np.float32)
    for c in range(NCORE):
        full[c::8] = res.results[c]["out"].reshape(NT, 128, D)
    out = full.reshape(1, 128 * 128, D)
    if _debug:
        dbg = []
        for name in ("dbg_x1", "dbg_x2", "dbg_x3"):
            fb = np.empty((128, 128, D), dtype=np.float32)
            for c in range(NCORE):
                fb[c::8] = res.results[c][name].reshape(NT, 128, D)
            dbg.append(fb.reshape(1, 128 * 128, D))
        raw = {n: [np.asarray(res.results[c][n]) for c in range(NCORE)]
               for n in ("act0", "act1", "gpark", "kpark", "h1t", "h2t",
                         "w3s_bf")}
        return out, dbg, raw
    return out


# version 08
# speedup vs baseline: 1.0596x; 1.0138x over previous
import numpy as np
import ml_dtypes
import concourse.bass as bass
import concourse.mybir as mybir
from concourse.bass_utils import run_bass_kernel_spmd

F32 = mybir.dt.float32
BF16 = mybir.dt.bfloat16
I32 = mybir.dt.int32
AF = mybir.ActivationFunctionType
ALU = mybir.AluOpType
AX = mybir.AxisListType.X
f32 = np.float32

D = 1024
N = 4096
DK = 128
NT = 16
TOK = 2048
NCORE = 8
RSQRT_ITERS = 9
RC = 12582912.0


class KB:
    def __init__(self, debug=False):
        self.debug = debug
        nc = bass.Bass("TRN2", target_bir_lowering=False,
                       detect_race_conditions=False)
        self.nc = nc
        self.S = {}
        self._sem_h = {}
        self._waits = []

        dt = nc.dram_tensor
        self.x_in = dt("L0_attn_inputs", [TOK, D], F32, kind="ExternalInput").ap()
        self.key_w = dt("L0_attn_key_param_tokens", [N, D], F32, kind="ExternalInput").ap()
        self.val_w = dt("L0_attn_value_param_tokens", [N, D], F32, kind="ExternalInput").ap()
        self.gate_w = dt("L0_ffn_w_gate", [N, D], F32, kind="ExternalInput").ap()
        self.up_w = dt("L0_ffn_w_up", [N, D], F32, kind="ExternalInput").ap()
        self.down_w = dt("L0_ffn_w_down", [D, N], F32, kind="ExternalInput").ap()
        self.wq = dt("L1_attn_Wq", [D, DK], F32, kind="ExternalInput").ap()
        self.wk = dt("L1_attn_Wk", [D, DK], F32, kind="ExternalInput").ap()
        self.wv = dt("L1_attn_Wv", [D, DK], F32, kind="ExternalInput").ap()
        self.wo = dt("L1_attn_Wo", [DK, D], F32, kind="ExternalInput").ap()
        self.w1_w = dt("L1_ffn_w1_w", [N, D], F32, kind="ExternalInput").ap()
        self.w2_w = dt("L1_ffn_w2_w", [N, D], F32, kind="ExternalInput").ap()
        self.h1_w = dt("L1_ffn_h1_w", [N, N], F32, kind="ExternalInput").ap()
        self.h2_w = dt("L1_ffn_h2_w", [N, N], F32, kind="ExternalInput").ap()
        self.w3_w = dt("L1_ffn_w3_w", [D, N], F32, kind="ExternalInput").ap()
        self.eye_in = dt("cst_eye", [128, 128], F32, kind="ExternalInput").ap()
        self.ones_in = dt("cst_ones", [128, 128], F32, kind="ExternalInput").ap()
        self.svec_in = dt("cst_svec", [128, 144], F32, kind="ExternalInput").ap()
        self.cmask_in = dt("cst_cmask", [128, 4096], BF16, kind="ExternalInput").ap()
        self.out = dt("out", [TOK, D], F32, kind="ExternalOutput").ap()
        if debug:
            self.dbg_x1 = dt("dbg_x1", [TOK, D], F32, kind="ExternalOutput").ap()
            self.dbg_x2 = dt("dbg_x2", [TOK, D], F32, kind="ExternalOutput").ap()
            self.dbg_x3 = dt("dbg_x3", [TOK, D], F32, kind="ExternalOutput").ap()

        dkind = dict(kind="ExternalOutput") if debug else {}
        self.key_bf = dt("key_bf", [N, D], BF16).ap()
        self.w1_bf = dt("w1_bf", [N, D], BF16).ap()
        self.w2_bf = dt("w2_bf", [N, D], BF16).ap()
        self.h1_bf = dt("h1_bf", [N, N], BF16).ap()
        self.h2_bf = dt("h2_bf", [N, N], BF16).ap()
        self.gq_bf = dt("gq_bf", [N, D], BF16).ap()
        self.uq_bf = dt("uq_bf", [N, D], BF16).ap()
        self.dq_bf = dt("dq_bf", [D, N], BF16).ap()
        self.w3s_bf = dt("w3s_bf", [D, N], BF16, **dkind).ap()
        self.h1t = dt("h1t", [8, N, 512], BF16).ap()
        self.h2t = dt("h2t", [8, N, 512], BF16).ap()
        self.act0 = dt("act0", [TOK, D], BF16, **dkind).ap()
        self.act1 = dt("act1", [TOK, N], BF16, **dkind).ap()
        self.gpark = dt("gpark", [N, TOK], BF16, **dkind).ap()
        self.kpark = dt("kpark", [N, TOK], BF16, **dkind).ap()
        self.xa = dt("xa", [TOK, D], F32).ap()
        self.xb = dt("xb", [TOK, D], F32).ap()
        self.kv_in = dt("kv_in", [2, 128 * TOK], BF16).ap()
        self.kv_out = dt("kv_out", [16, 128 * TOK], BF16).ap()
        self.gqt = dt("gqt", [128, 32768], BF16).ap()
        self.uqt = dt("uqt", [128, 32768], BF16).ap()
        self.dqt = dt("dqt", [128, 32768], BF16).ap()
        self.w1tt = dt("w1tt", [128, 32768], BF16).ap()
        self.w2tt = dt("w2tt", [128, 32768], BF16).ap()
        self.w3tt = dt("w3tt", [128, 32768], BF16).ap()

    def sem(self, name):
        if name not in self._sem_h:
            self._sem_h[name] = self.ctx.enter_context(self.nc.semaphore(name))
            self.S[name] = 0
        return self._sem_h[name]

    def inc(self, instr, name, n=1):
        instr.then_inc(self.sem(name), n)
        self.S[name] += n
        return self.S[name]

    def wait(self, eng, name, val=None):
        v = self.S[name] if val is None else val
        self._waits.append((name, v))
        eng.wait_ge(self.sem(name), v)

    def snap(self, name):
        self.sem(name)
        return self.S[name]

    def _vfence(self, vec, instr=None):
        if instr is None:
            instr = vec.engine_nop()
        self.inc(instr, 'v', 1)
        self.wait(vec, 'v')

    def _newton_rsqrt(self, vec, vin_off, y_off, t_off, seed):
        ST = self.STATS
        V = ST[:, vin_off:vin_off + 64]
        Y = ST[:, y_off:y_off + 64]
        T = ST[:, t_off:t_off + 64]
        vec.tensor_scalar(Y, V, 0.0, seed, ALU.mult, ALU.add)
        for _ in range(RSQRT_ITERS):
            vec.tensor_tensor(T, Y, Y, ALU.mult)
            vec.tensor_tensor(T, T, V, ALU.mult)
            vec.tensor_scalar(T, T, -0.5, 1.5, ALU.mult, ALU.add)
            vec.tensor_tensor(Y, Y, T, ALU.mult)

    def build(self):
        nc = self.nc
        from contextlib import ExitStack
        self.ctx = ExitStack()
        e = self.ctx.enter_context
        self.BIGA = e(nc.sbuf_tensor("BIGA", [128, 32768], BF16))
        self.BIGB = e(nc.sbuf_tensor("BIGB", [128, 32768], BF16))
        self.MED0 = e(nc.sbuf_tensor("MED0", [128, 8192], BF16))
        self.MED1 = e(nc.sbuf_tensor("MED1", [128, 8192], BF16))
        self.XF32 = e(nc.sbuf_tensor("XF32", [128, 4096], F32))
        self.STATS = e(nc.sbuf_tensor("STATS", [128, 1024], F32))
        self.DMY = e(nc.sbuf_tensor("DMY", [128, 1024], F32))
        self.I32S = e(nc.sbuf_tensor("I32S", [128, 1024], I32))
        self.QF32 = e(nc.sbuf_tensor("QF32", [128, 4096], F32))
        self.EYE = e(nc.sbuf_tensor("EYEc", [128, 128], F32))
        self.ONES = e(nc.sbuf_tensor("ONESc", [128, 128], F32))
        self.SVEC = e(nc.sbuf_tensor("SVECc", [128, 144], F32))
        self.pb = [e(nc.psum_tensor(f"pb{i}", [128, 512], F32)) for i in range(5)]
        self.pP = e(nc.psum_tensor("pP", [128, 256], F32))
        self.pT = e(nc.psum_tensor("pT", [128, 512], F32))
        self.pS = e(nc.psum_tensor("pS", [128, 256], F32))
        for s in ('w', 'g', 't', 's', 'v', 'e', 'c'):
            pass
        self.block = e(nc.Block())
        for s in ('w', 'g', 't', 's', 'v', 'e', 'c'):
            self.sem(s)

        import os
        kmax = int(os.environ.get("KPHASES", "99"))
        phases = [
            self.phase_consts,
            self.phase_casts,
            lambda: self.phase_ln(src=self.x_in, dst=self.act0, kind='bf16',
                                  tag='ln0', src_ready=None),
            self.phase_l0attn_A,
            self.phase_l0attn_O,
            self.phase_qprep,
            lambda: self.dbg_copy('dbg_x1', self.xa),
            lambda: self.phase_ln(src=self.xa, dst=self.act0, kind='quant',
                                  tag='ln1', src_ready=('g', self.snap('g'))),
            self.phase_l0ffn,
            lambda: self.dbg_copy('dbg_x2', self.xb),
            lambda: self.phase_ln(src=self.xb, dst=self.act0, kind='bf16',
                                  tag='ln2', src_ready=('g', self.snap('g'))),
            self.phase_l1attn_qkv,
            self.phase_l1attn,
            lambda: self.dbg_copy('dbg_x3', self.xa),
            lambda: self.phase_ln(src=self.xa, dst=self.act0, kind='bf16',
                                  tag='ln3', src_ready=('g', self.snap('g'))),
            self.phase_l1ffn,
        ]
        for pi, ph in enumerate(phases):
            if pi < kmax:
                ph()
        self.phase_finish()
        bad = [(n, v) for (n, v) in self._waits if v > self.S[n]]
        assert not bad, f"waits beyond final counts: {bad[:10]} finals={self.S}"
        return nc

    def _biga_store(self, tdram):
        k = self
        wta_now = self.snap('wta')

        @self.block.gpsimd
        def _(gp, wta_now=wta_now, tdram=tdram):
            k.wait(gp, 'wta', wta_now)
            k.inc(gp.dma_start(out=tdram[:, :], in_=k.BIGA[:, :]), 'gt', 16)

    def _biga_reload(self, sync, tdram):
        for q4 in range(4):
            self.inc(sync.dma_start(
                out=self.BIGA[:, q4 * 8192:(q4 + 1) * 8192],
                in_=tdram[:, q4 * 8192:(q4 + 1) * 8192]), 'w', 16)

    def dbg_copy(self, name, src):
        if not self.debug:
            return
        k = self
        dst = getattr(self, name)
        gsnap = self.snap('g')

        @self.block.gpsimd
        def _(gp):
            k.wait(gp, 'g', gsnap)
            k.inc(gp.dma_start(out=dst[:, :], in_=src[:, :]), 'g', 16)

    def phase_consts(self):
        k = self

        @self.block.sync
        def _(sync):
            k.inc(sync.dma_start(out=k.EYE[:], in_=k.eye_in[:]), 'wc', 16)
            k.inc(sync.dma_start(out=k.ONES[:], in_=k.ones_in[:]), 'wc', 16)
            k.inc(sync.dma_start(out=k.SVEC[:], in_=k.svec_in[:]), 'wc', 16)
        self.const_w = self.S['wc']

        @self.block.vector
        def _(vec):
            k.inc(vec.memset(k.STATS[:, :], 1.0), 'v', 1)
        self.memset_v = self.S['v']

    def phase_casts(self):
        k = self

        @self.block.gpsimd
        def _(gp):
            k.inc(gp.dma_start(out=k.key_bf[:, :], in_=k.key_w[:, :]), 'gk', 16)
            k.cast_key = k.S['gk']
            for nch in range(8, 32):
                k.inc(gp.dma_start(out=k.BIGB[:, nch * D:(nch + 1) * D],
                                   in_=k.val_w[nch * 128:(nch + 1) * 128, :]), 'gv', 16)
            k.cast_val_hi = k.S['gv']
            k.inc(gp.dma_start(out=k.w1_bf[:, :], in_=k.w1_w[:, :]), 'gw', 16)
            k.inc(gp.dma_start(out=k.w2_bf[:, :], in_=k.w2_w[:, :]), 'gw', 16)
            k.cast_w12 = k.S['gw']
            k.inc(gp.dma_start(out=k.h1_bf[:, :], in_=k.h1_w[:, :]), 'gh1', 16)
            k.cast_h1 = k.S['gh1']
            k.inc(gp.dma_start(out=k.h2_bf[:, :], in_=k.h2_w[:, :]), 'gh2', 16)
            k.cast_h2 = k.S['gh2']

    def phase_qprep(self):
        k = self
        ST = self.STATS
        def chunk_ap(t, cc):
            R, C = t.shape
            if C == 1024:
                return t[cc * 256:(cc + 1) * 256, :].rearrange(
                    "(a p) c -> p a c", p=128)
            else:
                r, j = cc // 2, cc % 2
                return t[r * 128:(r + 1) * 128, j * 2048:(j + 1) * 2048]

        def slot_ap(buf, cc, C):
            s = buf[:, (cc % 2) * 2048:(cc % 2) * 2048 + 2048]
            if C == 1024:
                return s.rearrange("p (a c) -> p a c", c=1024)
            return s

        specs = [(k.gate_w, N * D, k.gq_bf, 592, 'gq0'),
                 (k.up_w, N * D, k.uq_bf, 593, 'gq1'),
                 (k.down_w, D * N, k.dq_bf, 594, 'gq2')]
        for wi, (src, numel, dst, mcol, gq) in enumerate(specs):
            nch = numel // (128 * 2048)
            rpc = (128 * 2048) // src.shape[1]

            v1_0 = self.snap('v')

            m1 = []

            @self.block.sync
            def _(sync, src=src, nch=nch, rpc=rpc, v1_0=v1_0):
                for cc in range(nch):
                    if cc >= 2:
                        k.wait(sync, 'v', v1_0 + (cc - 1))
                    elif cc == 0:
                        k.wait(sync, 'v', v1_0)
                    m1.append(k.inc(sync.dma_start(
                        out=slot_ap(k.QF32, cc, src.shape[1]),
                        in_=chunk_ap(src, cc)), 'wl%d' % (cc % 2), 16))

            @self.block.vector
            def _(vec, nch=nch):
                for cc in range(nch):
                    k.wait(vec, 'wl%d' % (cc % 2), m1[cc])
                    i = vec.reduce_sum(ST[:, 576 + cc:577 + cc],
                                       k.QF32[:, (cc % 2) * 2048:(cc % 2) * 2048 + 2048],
                                       AX, apply_absolute_value=True)
                    k.inc(i, 'v', 1)
                self._vfence(vec)
                i = vec.reduce_sum(ST[:, 560:561], ST[:, 576:576 + nch], AX)
                k.inc(i, 'v', 1)
            vpart = self.S['v']

            eprev = self.snap('e')

            @self.block.tensor
            def _(ten, vpart=vpart, eprev=eprev):
                k.wait(ten, 'wc', k.const_w)
                k.wait(ten, 'v', vpart)
                k.wait(ten, 'e', eprev)
                k.inc(ten.matmul(k.pT[0:1, 0:1], ST[:, 560:561], k.ONES[:, 0:1],
                                 start=True, stop=True), 't', 1)
            tsum = self.S['t']

            @self.block.vector
            def _(vec, numel=numel, tsum=tsum):
                k.wait(vec, 't', tsum)
                i = vec.tensor_scalar(ST[0:1, 561:562], k.pT[0:1, 0:1],
                                      1.0 / numel, 1e-5, ALU.mult, ALU.max)
                self._vfence(vec, i)
                i = vec.reciprocal(ST[0:1, 562:563], ST[0:1, 561:562])
                self._vfence(vec, i)
            vrecip = self.S['v']

            @self.block.tensor
            def _(ten, vrecip=vrecip):
                k.wait(ten, 'v', vrecip)
                k.inc(ten.matmul(k.pT[:, 256:257], k.ONES[0:1, :], ST[0:1, 561:562],
                                 start=True, stop=True), 't', 1)
                k.inc(ten.matmul(k.pT[:, 257:258], k.ONES[0:1, :], ST[0:1, 562:563],
                                 start=True, stop=True), 't', 1)
            tbc = self.S['t']

            @self.block.vector
            def _(vec, mcol=mcol, tbc=tbc):
                k.wait(vec, 't', tbc)
                vec.tensor_copy(ST[:, mcol:mcol + 1], k.pT[:, 256:257])
                i = vec.tensor_copy(ST[:, 595:596], k.pT[:, 257:258])
                self._vfence(vec, i)
                k.inc(vec.engine_nop(), 'e', 1)
            vsb = self.S['v']

            v2_0 = self.snap('v')

            m2 = []

            @self.block.sync
            def _(sync, src=src, nch=nch, rpc=rpc, v2_0=v2_0, vpart=vpart):
                for cc in range(nch):
                    if cc >= 2:
                        k.wait(sync, 'v', v2_0 + (cc - 1) * 2)
                    elif cc == 0:
                        k.wait(sync, 'v', vpart)
                    m2.append(k.inc(sync.dma_start(
                        out=slot_ap(k.QF32, cc, src.shape[1]),
                        in_=chunk_ap(src, cc)), 'wl%d' % (cc % 2), 16))

            @self.block.vector
            def _(vec, nch=nch, vsb=vsb, gq=gq):
                k.wait(vec, 'v', vsb)
                for cc in range(nch):
                    k.wait(vec, 'wl%d' % (cc % 2), m2[cc])
                    if cc >= 2:
                        k.wait(vec, gq, cc * 16)
                    for sub in range(2):
                        qsl = slice((cc % 2) * 2048 + sub * 1024,
                                    (cc % 2) * 2048 + sub * 1024 + 1024)
                        msl = slice((cc % 2) * 2048 + sub * 1024,
                                    (cc % 2) * 2048 + sub * 1024 + 1024)
                        vec.tensor_scalar(k.QF32[:, qsl], k.QF32[:, qsl],
                                          ST[:, 595:596], RC, ALU.mult, ALU.add)
                        vec.tensor_scalar(k.QF32[:, qsl], k.QF32[:, qsl],
                                          -RC, None, ALU.add)
                        vec.tensor_copy(k.I32S[:, :], k.QF32[:, qsl])
                        i = vec.tensor_scalar(k.MED1[:, msl], k.I32S[:, :],
                                              -1.0, 1.0, ALU.max, ALU.min)
                        k.inc(i, 'v', 1)
            vB = self.S['v'] - nch * 2

            @self.block.gpsimd
            def _(gp, dst=dst, nch=nch, rpc=rpc, vB=vB, gq=gq):
                for cc in range(nch):
                    k.wait(gp, 'v', vB + (cc + 1) * 2)
                    k.inc(gp.dma_start(
                        out=chunk_ap(dst, cc),
                        in_=slot_ap(k.MED1, cc, dst.shape[1])),
                        gq, 16)
            if wi == 0:
                k.stage_gate = self.S[gq]
            elif wi == 1:
                k.stage_up = self.S[gq]
            else:
                k.stage_down = self.S[gq]

        v3_0 = self.snap('v')

        m3 = []

        @self.block.sync
        def _(sync, v3_0=v3_0):
            for cc in range(8):
                k.wait(sync, 'v', v3_0 + cc)
                m3.append(k.inc(sync.dma_start(
                    out=k.QF32[:, :],
                    in_=k.w3_w[cc * 128:(cc + 1) * 128, :]), 'wl%d' % (cc % 2), 16))

        @self.block.vector
        def _(vec):
            k.wait(vec, 'wc', k.const_w)
            for cc in range(8):
                k.wait(vec, 'wl%d' % (cc % 2), m3[cc])
                if cc >= 1:
                    k.wait(vec, 'gq3', cc * 16)
                for sub in range(4):
                    i = vec.tensor_scalar(
                        k.MED1[:, sub * 1024:sub * 1024 + 1024],
                        k.QF32[:, sub * 1024:sub * 1024 + 1024],
                        k.SVEC[:, 128 + cc:129 + cc], None, ALU.mult)
                k.inc(i, 'v', 1)
        v3B = self.S['v'] - 8

        @self.block.gpsimd
        def _(gp, v3B=v3B):
            for cc in range(8):
                k.wait(gp, 'v', v3B + cc + 1)
                k.inc(gp.dma_start(out=k.w3s_bf[cc * 128:(cc + 1) * 128, :],
                                   in_=k.MED1[:, 0:4096]), 'gq3', 16)
        k.stage_w3 = self.S['gq3']

    def phase_ln(self, src, dst, kind, tag, src_ready):
        import os
        k = self
        ST = self.STATS
        NOSQ = os.environ.get("KLN_NOSQ") == "1"
        NONEWT = os.environ.get("KLN_NONEWT") == "1"
        PLAIN = os.environ.get("KLN_PLAIN") == "1"

        sA0 = self.snap('s')
        vA0 = self.snap('v')
        mA = []

        @self.block.sync
        def _(sync):
            for t in range(NT):
                if t >= 2:
                    k.wait(sync, 's', sA0 + (t - 1))
                    k.wait(sync, 'v', vA0 + (t - 1))
                elif t == 0 and src_ready is not None:
                    k.wait(sync, src_ready[0], src_ready[1])
                mA.append(k.inc(sync.dma_start(
                    out=k.XF32[:, (t % 2) * 1024:(t % 2) * 1024 + 1024],
                    in_=src[t * 128:(t + 1) * 128, :]), 'wl%d' % (t % 2), 16))

        @self.block.scalar
        def _(sc):
            k.wait(sc, 'v', k.memset_v)
            for t in range(NT):
                k.wait(sc, 'wl%d' % (t % 2), mA[t])
                if NOSQ:
                    i = sc.activation(out=k.DMY[:, :],
                                      in_=k.XF32[:, (t % 2) * 1024:(t % 2) * 1024 + 1024],
                                      func=AF.Square)
                else:
                    i = sc.activation(out=k.DMY[:, :],
                                      in_=k.XF32[:, (t % 2) * 1024:(t % 2) * 1024 + 1024],
                                      func=AF.Square, accum_out=ST[:, 64 + t:65 + t])
                k.inc(i, 's', 1)
        sA = self.S['s']

        @self.block.vector
        def _(vec):
            for t in range(NT):
                k.wait(vec, 'wl%d' % (t % 2), mA[t])
                i = vec.reduce_sum(ST[:, t:t + 1],
                                   k.XF32[:, (t % 2) * 1024:(t % 2) * 1024 + 1024], AX)
                k.inc(i, 'v', 1)
            k.wait(vec, 's', sA)
            self._vfence(vec)
            vec.tensor_scalar(ST[:, 0:64], ST[:, 0:64], 1.0 / D, None, ALU.mult)
            vec.tensor_tensor(ST[:, 256:320], ST[:, 0:64], ST[:, 0:64], ALU.mult)
            vec.tensor_scalar(ST[:, 128:192], ST[:, 64:128], 1.0 / D, None, ALU.mult)
            vec.tensor_tensor(ST[:, 128:192], ST[:, 128:192], ST[:, 256:320],
                              ALU.subtract)
            vec.tensor_scalar(ST[:, 128:192], ST[:, 128:192], 1e-5, None, ALU.add)
            if NONEWT:
                vec.tensor_scalar(ST[:, 320:384], ST[:, 128:192], 0.0, 1.0,
                                  ALU.mult, ALU.add)
            else:
                self._newton_rsqrt(vec, 128, 320, 256, 0.6)
            vec.tensor_tensor(ST[:, 384:448], ST[:, 0:64], ST[:, 320:384], ALU.mult)
            i = vec.tensor_scalar(ST[:, 384:448], ST[:, 384:448], -1.0, None, ALU.mult)
            k.inc(i, 'v', 1)
        vstats = self.S['v']

        sB0 = self.snap('s')
        vB0 = self.snap('v')
        mB = []
        wt_base = (self.snap('wt0'), self.snap('wt1'))
        mst = {t: wt_base[t % 2] + (t // 2 + 1) * 16 for t in range(NT)}
        ssnap = [sB0 + t + 1 for t in range(NT)]

        def emit_store(sync, t):
            if kind == 'bf16':
                sync_gate = ('s', ssnap[t])
            else:
                sync_gate = ('v', vB0 + (t + 1) * 12)
            k.wait(sync, *sync_gate)
            mm_ = k.inc(sync.dma_start(
                out=dst[t * 128:(t + 1) * 128, :],
                in_=k.MED0[:, (t % 2) * 1024:(t % 2) * 1024 + 1024]),
                'wt%d' % (t % 2), 16)
            assert mm_ == mst[t]

        @self.block.sync
        def _(sync):
            for t in range(NT):
                if t >= 2:
                    k.wait(sync, 's', sB0 + (t - 1))
                    if kind == 'quant':
                        k.wait(sync, 'v', vB0 + (t - 1) * 12)
                elif t == 0:
                    k.wait(sync, 's', sA)
                    k.wait(sync, 'v', vstats)
                mB.append(k.inc(sync.dma_start(
                    out=k.XF32[:, (t % 2) * 1024:(t % 2) * 1024 + 1024],
                    in_=src[t * 128:(t + 1) * 128, :]), 'wl%d' % (t % 2), 16))
                if t >= 2:
                    emit_store(sync, t - 2)
            emit_store(sync, NT - 2)
            emit_store(sync, NT - 1)

        @self.block.scalar
        def _(sc):
            k.wait(sc, 'v', vstats)
            for t in range(NT):
                k.wait(sc, 'wl%d' % (t % 2), mB[t])
                if kind == 'bf16':
                    if t >= 2:
                        k.wait(sc, 'wt%d' % (t % 2), mst[t - 2])
                    o = k.MED0[:, (t % 2) * 1024:(t % 2) * 1024 + 1024]
                else:
                    o = k.XF32[:, 2048 + (t % 2) * 1024:2048 + (t % 2) * 1024 + 1024]
                if PLAIN:
                    i = sc.activation(out=o,
                                      in_=k.XF32[:, (t % 2) * 1024:(t % 2) * 1024 + 1024],
                                      func=AF.Identity)
                else:
                    i = sc.activation(out=o,
                                      in_=k.XF32[:, (t % 2) * 1024:(t % 2) * 1024 + 1024],
                                      func=AF.Identity, scale=ST[:, 320 + t:321 + t],
                                      bias=ST[:, 384 + t:385 + t])
                ssnap.append(k.inc(i, 's', 1))

        if kind == 'bf16':
            pass
        else:
            @self.block.vector
            def _(vec):
                for t in range(NT):
                    vb = k.S['v']
                    k.wait(vec, 's', ssnap[t])
                    if t >= 2:
                        k.wait(vec, 'wt%d' % (t % 2), mst[t - 2])
                    lnf = k.XF32[:, 2048 + (t % 2) * 1024:2048 + (t % 2) * 1024 + 1024]
                    i = vec.reduce_max(ST[:, 528:529], lnf, AX,
                                       apply_absolute_value=True)
                    self._vfence(vec, i)
                    vec.tensor_scalar(ST[:, 528:529], ST[:, 528:529], 1e-5, None,
                                      ALU.max)
                    i = vec.tensor_scalar(ST[:, 448 + t:449 + t], ST[:, 528:529],
                                          1.0 / 127.0, None, ALU.mult)
                    self._vfence(vec, i)
                    i = vec.reciprocal(ST[:, 544:545], ST[:, 528:529])
                    self._vfence(vec, i)
                    vec.tensor_scalar(ST[:, 544:545], ST[:, 544:545], 127.0, None,
                                      ALU.mult)
                    vec.tensor_scalar(ST[:, 464 + t:465 + t], ST[:, 448 + t:449 + t],
                                      ST[:, 592:593], None, ALU.mult)
                    vec.tensor_scalar(ST[:, 480 + t:481 + t], ST[:, 448 + t:449 + t],
                                      ST[:, 593:594], None, ALU.mult)
                    self._vfence(vec)
                    vec.tensor_scalar(lnf, lnf, ST[:, 544:545], RC, ALU.mult,
                                      ALU.add)
                    vec.tensor_scalar(lnf, lnf, -RC, None, ALU.add)
                    vec.tensor_copy(k.I32S[:, :], lnf)
                    i = vec.tensor_copy(k.MED0[:, (t % 2) * 1024:(t % 2) * 1024 + 1024],
                                        k.I32S[:, :])
                    k.inc(i, 'v', 1)
                    while k.S['v'] - vb < 12:
                        k.inc(vec.engine_nop(), 'v', 1)
                    assert k.S['v'] - vb == 12

        setattr(self, tag + "_done_w", (self.S['wt0'], self.S['wt1']))

    def phase_l0attn_A(self):
        k = self

        @self.block.sync
        def _(sync):
            k.wait(sync, 'gk', k.cast_key)
            for dc in range(8):
                k.inc(sync.dma_start(out=k.BIGA[:, dc * 4096:(dc + 1) * 4096],
                                     in_=k.key_bf[0:N, dc * 128:(dc + 1) * 128],
                                     transpose=True), 'w', 16)
        wkey = self.S['w']

        for h in range(2):
            tprev = self.snap('t')

            @self.block.sync
            def _(sync, h=h, tprev=tprev):
                if h == 1:
                    k.wait(sync, 't', tprev)
                for dc in range(8):
                    k.inc(sync.dma_start(
                        out=k.MED0[:, dc * 1024:(dc + 1) * 1024],
                        in_=k.act0[h * 1024:(h + 1) * 1024, dc * 128:(dc + 1) * 128],
                        transpose=True), 'w', 16)
            wln = self.S['w']

            e0 = self.snap('e')
            grp_t = []

            @self.block.tensor
            def _(ten, wln=wln, e0=e0):
                k.wait(ten, 'w', max(wkey, wln))
                k.wait(ten, 'e', e0)
                for tt in range(8):
                    for nc4 in range(8):
                        u = tt * 8 + nc4
                        b = u % 4
                        if u >= 4:
                            k.wait(ten, 'e', e0 + u - 4 + 1)
                        for dc in range(8):
                            i = ten.matmul(k.pb[b][:, :],
                                           k.MED0[:, dc * 1024 + tt * 128:
                                                  dc * 1024 + (tt + 1) * 128],
                                           k.BIGA[:, dc * 4096 + nc4 * 512:
                                                  dc * 4096 + (nc4 + 1) * 512],
                                           start=(dc == 0), stop=(dc == 7))
                        grp_t.append(k.inc(i, 't', 1))

            v0 = self.snap('v')
            sg = []
            gst = [wln + (t + 1) * 16 for t in range(8)]

            @self.block.scalar
            def _(sc, h=h, v0=v0):
                if h == 0:
                    sc.preload_act_table(AF.Gelu)

                def evac(tt):
                    for nc4 in range(8):
                        u = tt * 8 + nc4
                        k.wait(sc, 't', grp_t[u])
                        if nc4 == 0 and tt >= 2:
                            k.wait(sc, 'w', gst[tt - 2])
                        A = k.BIGB[:, (tt % 2) * 4096:(tt % 2) * 4096 + 4096]
                        i = sc.activation(out=A[:, nc4 * 512:(nc4 + 1) * 512],
                                          in_=k.pb[u % 4][:, :], func=AF.Copy)
                        k.inc(i, 'e', 1)

                def gelu(tt):
                    k.wait(sc, 'v', v0 + (tt + 1) * 7)
                    A = k.BIGB[:, (tt % 2) * 4096:(tt % 2) * 4096 + 4096]
                    i = sc.activation(out=A[:, :], in_=A[:, :], func=AF.Gelu,
                                      scale=k.STATS[:, 560 + tt:561 + tt])
                    sg.append(k.inc(i, 's', 1))

                evac(0)
                for tt in range(1, 8):
                    evac(tt)
                    gelu(tt - 1)
                gelu(7)

            @self.block.vector
            def _(vec, v0=v0, e0=e0):
                for tt in range(8):
                    vb = k.S['v']
                    k.wait(vec, 'e', e0 + (tt + 1) * 8)
                    A = k.BIGB[:, (tt % 2) * 4096:(tt % 2) * 4096 + 4096]
                    for sub in range(4):
                        vec.scalar_tensor_tensor(
                            k.DMY[:, :], A[:, sub * 1024:(sub + 1) * 1024], 1.0,
                            A[:, sub * 1024:(sub + 1) * 1024], ALU.mult, ALU.mult,
                            accum_out=k.STATS[:, 736 + sub:737 + sub])
                    self._vfence(vec)
                    i = vec.reduce_sum(k.STATS[:, 608:609], k.STATS[:, 736:740], AX)
                    self._vfence(vec, i)
                    self._newton_rsqrt(vec, 608, 672, 256, 0.02)
                    i = vec.tensor_scalar(k.STATS[:, 560 + tt:561 + tt],
                                          k.STATS[:, 672:673], 64.0, None, ALU.mult)
                    k.inc(i, 'v', 1)
                    while k.S['v'] - vb < 7:
                        k.inc(vec.engine_nop(), 'v', 1)
                    assert k.S['v'] - vb == 7

            @self.block.sync
            def _(sync, h=h):
                for tt in range(8):
                    k.wait(sync, 's', sg[tt])
                    A = k.BIGB[:, (tt % 2) * 4096:(tt % 2) * 4096 + 4096]
                    mm_ = k.inc(sync.dma_start(
                        out=k.act1[(h * 8 + tt) * 128:(h * 8 + tt + 1) * 128, :],
                        in_=A[:, :]), 'w', 16)
                    assert mm_ == gst[tt], (mm_, gst[tt])
        k.G_parked_w = self.S['w']
        k.A_all_t = self.S['t']

    def phase_l0attn_O(self):
        k = self
        sprev = self.snap('s')

        @self.block.gpsimd
        def _(gp, sprev=sprev):
            k.wait(gp, 's', sprev)
            k.wait(gp, 'w', k.G_parked_w)
            for nch in range(8):
                k.inc(gp.dma_start(out=k.BIGB[:, nch * D:(nch + 1) * D],
                                   in_=k.val_w[nch * 128:(nch + 1) * 128, :]), 'gv', 16)
        gval = self.S['gv']

        for h in range(2):
            t_half0 = self.snap('t')

            @self.block.sync
            def _(sync, h=h, t_half0=t_half0):
                k.wait(sync, 'w', k.G_parked_w)
                k.wait(sync, 't', k.A_all_t if h == 0 else t_half0)
                for nc_ in range(32):
                    k.inc(sync.dma_start(
                        out=k.BIGA[:, nc_ * 1024:(nc_ + 1) * 1024],
                        in_=k.act1[h * 1024:(h + 1) * 1024, nc_ * 128:(nc_ + 1) * 128],
                        transpose=True), 'w', 16)
            wgt = self.S['w']

            e0x = self.snap('e')
            mx = []

            @self.block.sync
            def _(sync, h=h, e0x=e0x):
                for tt in range(8):
                    if tt >= 2:
                        k.wait(sync, 'e', e0x + (tt - 1) * 2)
                    mx.append(k.inc(sync.dma_start(
                        out=k.XF32[:, (tt % 2) * 1024:(tt % 2) * 1024 + 1024],
                        in_=k.x_in[(h * 8 + tt) * 128:(h * 8 + tt + 1) * 128, :]),
                        'wx%d' % (tt % 2), 16))

            e0 = self.snap('e')
            grp_t = []

            @self.block.tensor
            def _(ten, wgt=wgt, e0=e0):
                k.wait(ten, 'w', wgt)
                k.wait(ten, 'gv', gval)
                k.wait(ten, 'e', e0)
                for tt in range(8):
                    for dh in range(2):
                        u = tt * 2 + dh
                        b = u % 4
                        if u >= 4:
                            k.wait(ten, 'e', e0 + u - 4 + 1)
                        for nc_ in range(32):
                            i = ten.matmul(
                                k.pb[b][:, :],
                                k.BIGA[:, nc_ * 1024 + tt * 128:nc_ * 1024 + (tt + 1) * 128],
                                k.BIGB[:, nc_ * 1024 + dh * 512:nc_ * 1024 + dh * 512 + 512],
                                start=(nc_ == 0), stop=(nc_ == 31))
                            if (nc_ + 1) % 8 == 0 and nc_ != 31:
                                k.inc(i, 't', 1)
                        grp_t.append(k.inc(i, 't', 1))

            gsb = self.snap('g')

            @self.block.vector
            def _(vec, gsb=gsb):
                for tt in range(8):
                    for dh in range(2):
                        u = tt * 2 + dh
                        k.wait(vec, 't', grp_t[u])
                        if dh == 0:
                            k.wait(vec, 'wx%d' % (tt % 2), mx[tt])
                            if tt >= 2:
                                k.wait(vec, 'g', gsb + tt * 16)
                        i = vec.scalar_tensor_tensor(
                            k.XF32[:, 2048 + (tt % 2) * 1024 + dh * 512:
                                   2048 + (tt % 2) * 1024 + dh * 512 + 512],
                            k.pb[u % 4][:, :], 1.0,
                            k.XF32[:, (tt % 2) * 1024 + dh * 512:
                                   (tt % 2) * 1024 + dh * 512 + 512],
                            ALU.mult, ALU.add)
                        k.inc(i, 'e', 1)
            er0 = self.S['e'] - 16

            @self.block.gpsimd
            def _(gp, h=h, er0=er0, gsb=gsb):
                for tt in range(8):
                    k.wait(gp, 'e', er0 + (tt + 1) * 2)
                    mm_ = k.inc(gp.dma_start(
                        out=k.xa[(h * 8 + tt) * 128:(h * 8 + tt + 1) * 128, :],
                        in_=k.XF32[:, 2048 + (tt % 2) * 1024:2048 + (tt % 2) * 1024 + 1024]),
                        'g', 16)
                    assert mm_ == gsb + (tt + 1) * 16

    def phase_l0ffn(self):
        k = self
        ST = self.STATS

        for h in range(2):
            tprev = self.snap('t')

            @self.block.sync
            def _(sync, h=h, tprev=tprev):
                k.wait(sync, 'gq0', k.stage_gate)
                k.wait(sync, 't', tprev)
                k.wait(sync, 'gt', k.snap('gt'))
                if h == 0:
                    for dc in range(8):
                        k.inc(sync.dma_start(out=k.BIGA[:, dc * 4096:(dc + 1) * 4096],
                                             in_=k.gq_bf[0:N, dc * 128:(dc + 1) * 128],
                                             transpose=True), 'wta', 16)
                else:
                    k._biga_reload(sync, k.gqt)
                k.wait(sync, 'wt0', k.ln1_done_w[0])
                k.wait(sync, 'wt1', k.ln1_done_w[1])
                for dc in range(8):
                    k.inc(sync.dma_start(
                        out=k.MED0[:, dc * 1024:(dc + 1) * 1024],
                        in_=k.act0[h * 1024:(h + 1) * 1024, dc * 128:(dc + 1) * 128],
                        transpose=True), 'w', 16)
            wg = self.S['w']
            wtag = self.snap('wta')
            if h == 0:
                self._biga_store(k.gqt)

            sprev_v = self.snap('v')
            e0 = self.snap('e')
            grp_t = []

            @self.block.tensor
            def _(ten, wg=wg, e0=e0, wtag=wtag):
                k.wait(ten, 'w', wg)
                k.wait(ten, 'wta', wtag)
                k.wait(ten, 'e', e0)
                for tt in range(8):
                    for nc4 in range(8):
                        u = tt * 8 + nc4
                        b = u % 4
                        if u >= 4:
                            k.wait(ten, 'e', e0 + u - 4 + 1)
                        for dc in range(8):
                            i = ten.matmul(k.pb[b][:, :],
                                           k.MED0[:, dc * 1024 + tt * 128:
                                                  dc * 1024 + (tt + 1) * 128],
                                           k.BIGA[:, dc * 4096 + nc4 * 512:
                                                  dc * 4096 + (nc4 + 1) * 512],
                                           start=(dc == 0), stop=(dc == 7))
                        grp_t.append(k.inc(i, 't', 1))
            tgate = self.S['t']

            @self.block.scalar
            def _(sc, h=h):
                if h == 0:
                    sc.preload_act_table(AF.Silu)
                for tt in range(8):
                    for nc4 in range(8):
                        u = tt * 8 + nc4
                        k.wait(sc, 't', grp_t[u])
                        i = sc.activation(
                            out=k.BIGB[:, tt * 4096 + nc4 * 512:tt * 4096 + nc4 * 512 + 512],
                            in_=k.pb[u % 4][:, :], func=AF.Silu,
                            scale=ST[:, 464 + h * 8 + tt:465 + h * 8 + tt])
                        k.inc(i, 'e', 1)

            @self.block.sync
            def _(sync, h=h, tgate=tgate):
                k.wait(sync, 'gq1', k.stage_up)
                k.wait(sync, 't', tgate)
                k.wait(sync, 'gt', k.snap('gt'))
                if h == 0:
                    for dc in range(8):
                        k.inc(sync.dma_start(out=k.BIGA[:, dc * 4096:(dc + 1) * 4096],
                                             in_=k.uq_bf[0:N, dc * 128:(dc + 1) * 128],
                                             transpose=True), 'wta', 16)
                else:
                    k._biga_reload(sync, k.uqt)
            wu = self.S['w']
            wtau = self.snap('wta')
            if h == 0:
                self._biga_store(k.uqt)

            e0u = self.snap('e')
            grp_tu = []

            @self.block.tensor
            def _(ten, wu=wu, e0u=e0u, wtau=wtau):
                k.wait(ten, 'w', wu)
                k.wait(ten, 'wta', wtau)
                k.wait(ten, 'e', e0u)
                for tt in range(8):
                    for nc4 in range(8):
                        u = tt * 8 + nc4
                        b = u % 4
                        if u >= 4:
                            k.wait(ten, 'e', e0u + u - 4 + 1)
                        for dc in range(8):
                            i = ten.matmul(k.pb[b][:, :],
                                           k.MED0[:, dc * 1024 + tt * 128:
                                                  dc * 1024 + (tt + 1) * 128],
                                           k.BIGA[:, dc * 4096 + nc4 * 512:
                                                  dc * 4096 + (nc4 + 1) * 512],
                                           start=(dc == 0), stop=(dc == 7))
                        grp_tu.append(k.inc(i, 't', 1))
            tup = self.S['t']

            vq_marks = []

            @self.block.vector
            def _(vec, h=h, e0=e0):
                for tt in range(8):
                    for nc4 in range(8):
                        u = tt * 8 + nc4
                        k.wait(vec, 't', grp_tu[u])
                        k.wait(vec, 'e', e0 + u + 1)
                        sl = slice(tt * 4096 + nc4 * 512, tt * 4096 + nc4 * 512 + 512)
                        i = vec.scalar_tensor_tensor(
                            k.BIGB[:, sl], k.pb[u % 4][:, :],
                            ST[:, 480 + h * 8 + tt:481 + h * 8 + tt],
                            k.BIGB[:, sl], ALU.mult, ALU.mult)
                        k.inc(i, 'e', 1)
                    H = k.BIGB[:, tt * 4096:(tt + 1) * 4096]
                    i = vec.reduce_max(ST[:, 528:529], H, AX, apply_absolute_value=True)
                    self._vfence(vec, i)
                    vec.tensor_scalar(ST[:, 528:529], ST[:, 528:529], 1e-5, None,
                                      ALU.max)
                    i = vec.tensor_scalar(ST[:, 556:557], ST[:, 528:529],
                                          1.0 / 127.0, None, ALU.mult)
                    self._vfence(vec, i)
                    vec.tensor_scalar(ST[:, 496 + h * 8 + tt:497 + h * 8 + tt],
                                      ST[:, 556:557], ST[:, 594:595], None, ALU.mult)
                    i = vec.reciprocal(ST[:, 544:545], ST[:, 528:529])
                    self._vfence(vec, i)
                    i = vec.tensor_scalar(ST[:, 544:545], ST[:, 544:545], 127.0,
                                          None, ALU.mult)
                    self._vfence(vec, i)
                    for sub in range(4):
                        ssl = slice(tt * 4096 + sub * 1024, tt * 4096 + sub * 1024 + 1024)
                        vec.tensor_scalar(k.DMY[:, :], k.BIGB[:, ssl],
                                          ST[:, 544:545], RC, ALU.mult, ALU.add)
                        vec.tensor_scalar(k.DMY[:, :], k.DMY[:, :], -RC, None,
                                          ALU.add)
                        vec.tensor_copy(k.I32S[:, :], k.DMY[:, :])
                        i = vec.tensor_copy(k.BIGB[:, ssl], k.I32S[:, :])
                    vq_marks.append(k.inc(i, 'v', 1))

            @self.block.gpsimd
            def _(gp, h=h):
                for tt in range(8):
                    k.wait(gp, 'v', vq_marks[tt])
                    k.inc(gp.dma_start(
                        out=k.act1[(h * 8 + tt) * 128:(h * 8 + tt + 1) * 128, :],
                        in_=k.BIGB[:, tt * 4096:(tt + 1) * 4096]), 'g', 16)
            gc2 = self.S['g']

            @self.block.sync
            def _(sync, h=h, tup=tup, gc2=gc2):
                k.wait(sync, 'gq2', k.stage_down)
                k.wait(sync, 'g', gc2)
                k.wait(sync, 't', tup)
                k.wait(sync, 'gt', k.snap('gt'))
                if h == 0:
                    for nc_ in range(32):
                        k.inc(sync.dma_start(out=k.BIGA[:, nc_ * 1024:(nc_ + 1) * 1024],
                                             in_=k.dq_bf[0:D, nc_ * 128:(nc_ + 1) * 128],
                                             transpose=True), 'wta', 16)
                else:
                    k._biga_reload(sync, k.dqt)
                for nc_ in range(32):
                    k.inc(sync.dma_start(
                        out=k.BIGB[:, nc_ * 1024:(nc_ + 1) * 1024],
                        in_=k.act1[h * 1024:(h + 1) * 1024, nc_ * 128:(nc_ + 1) * 128],
                        transpose=True), 'w', 16)
            wd = self.S['w']
            wtad = self.snap('wta')
            if h == 0:
                self._biga_store(k.dqt)

            e0d = self.snap('e')

            mx = []

            @self.block.sync
            def _(sync, h=h, e0d=e0d, gc2=gc2):
                for tt in range(8):
                    if tt >= 2:
                        k.wait(sync, 'e', e0d + (tt - 1) * 2)
                    elif tt == 0:
                        k.wait(sync, 'g', gc2)
                    mx.append(k.inc(sync.dma_start(
                        out=k.XF32[:, (tt % 2) * 1024:(tt % 2) * 1024 + 1024],
                        in_=k.xa[(h * 8 + tt) * 128:(h * 8 + tt + 1) * 128, :]),
                        'wx%d' % (tt % 2), 16))
            grp_td = []

            @self.block.tensor
            def _(ten, wd=wd, e0d=e0d, wtad=wtad):
                k.wait(ten, 'w', wd)
                k.wait(ten, 'wta', wtad)
                k.wait(ten, 'e', e0d)
                for tt in range(8):
                    for dh in range(2):
                        u = tt * 2 + dh
                        b = u % 4
                        if u >= 4:
                            k.wait(ten, 'e', e0d + u - 4 + 1)
                        for nc_ in range(32):
                            i = ten.matmul(
                                k.pb[b][:, :],
                                k.BIGB[:, nc_ * 1024 + tt * 128:nc_ * 1024 + (tt + 1) * 128],
                                k.BIGA[:, nc_ * 1024 + dh * 512:nc_ * 1024 + dh * 512 + 512],
                                start=(nc_ == 0), stop=(nc_ == 31))
                            if (nc_ + 1) % 8 == 0 and nc_ != 31:
                                k.inc(i, 't', 1)
                        grp_td.append(k.inc(i, 't', 1))

            gsb = self.snap('g')

            @self.block.vector
            def _(vec, h=h, gsb=gsb):
                for tt in range(8):
                    for dh in range(2):
                        u = tt * 2 + dh
                        k.wait(vec, 't', grp_td[u])
                        if dh == 0:
                            k.wait(vec, 'wx%d' % (tt % 2), mx[tt])
                            if tt >= 2:
                                k.wait(vec, 'g', gsb + tt * 16)
                        i = vec.scalar_tensor_tensor(
                            k.XF32[:, 2048 + (tt % 2) * 1024 + dh * 512:
                                   2048 + (tt % 2) * 1024 + dh * 512 + 512],
                            k.pb[u % 4][:, :],
                            ST[:, 496 + h * 8 + tt:497 + h * 8 + tt],
                            k.XF32[:, (tt % 2) * 1024 + dh * 512:
                                   (tt % 2) * 1024 + dh * 512 + 512],
                            ALU.mult, ALU.add)
                        k.inc(i, 'e', 1)
            er0 = self.S['e'] - 16

            @self.block.gpsimd
            def _(gp, h=h, er0=er0, gsb=gsb):
                for tt in range(8):
                    k.wait(gp, 'e', er0 + (tt + 1) * 2)
                    mm_ = k.inc(gp.dma_start(
                        out=k.xb[(h * 8 + tt) * 128:(h * 8 + tt + 1) * 128, :],
                        in_=k.XF32[:, 2048 + (tt % 2) * 1024:2048 + (tt % 2) * 1024 + 1024]),
                        'g', 16)
                    assert mm_ == gsb + (tt + 1) * 16

    def phase_l1attn_qkv(self):
        k = self

        @self.block.gpsimd
        def _(gp):
            for dc in range(8):
                k.inc(gp.dma_start(out=k.MED1[:, dc * 128:(dc + 1) * 128],
                                   in_=k.wq[dc * 128:(dc + 1) * 128, :]), 'g', 16)
                k.inc(gp.dma_start(out=k.MED1[:, 1024 + dc * 128:1024 + (dc + 1) * 128],
                                   in_=k.wk[dc * 128:(dc + 1) * 128, :]), 'g', 16)
                k.inc(gp.dma_start(out=k.MED1[:, 2048 + dc * 128:2048 + (dc + 1) * 128],
                                   in_=k.wv[dc * 128:(dc + 1) * 128, :]), 'g', 16)
            k.inc(gp.dma_start(out=k.MED0[:, 2048:3072], in_=k.wo[:, :]), 'g', 16)
        gqkv = self.S['g']

        tprev = self.snap('t')

        @self.block.sync
        def _(sync, tprev=tprev):
            k.wait(sync, 'wt0', k.ln2_done_w[0])
            k.wait(sync, 'wt1', k.ln2_done_w[1])
            k.wait(sync, 't', tprev)
            for dc in range(8):
                k.inc(sync.dma_start(
                    out=k.BIGB[:, dc * 2048:(dc + 1) * 2048],
                    in_=k.act0[0:TOK, dc * 128:(dc + 1) * 128], transpose=True),
                    'w', 16)
        wln2 = self.S['w']

        e0 = self.snap('e')
        grp_t = []

        @self.block.tensor
        def _(ten, e0=e0):
            k.wait(ten, 'w', wln2)
            k.wait(ten, 'g', gqkv)
            k.wait(ten, 'e', e0)
            u = 0
            for woff in (0, 1024):
                for qc4 in range(4):
                    b = u % 4
                    if u >= 4:
                        k.wait(ten, 'e', e0 + u - 4 + 1)
                    for dc in range(8):
                        i = ten.matmul(k.pb[b][:, :],
                                       k.MED1[:, woff + dc * 128:woff + (dc + 1) * 128],
                                       k.BIGB[:, dc * 2048 + qc4 * 512:
                                              dc * 2048 + (qc4 + 1) * 512],
                                       start=(dc == 0), stop=(dc == 7))
                    grp_t.append(k.inc(i, 't', 1))
                    u += 1
            for tt in range(NT):
                b = u % 4
                if u >= 4:
                    k.wait(ten, 'e', e0 + u - 4 + 1)
                for dc in range(8):
                    i = ten.matmul(k.pb[b][:, 0:128],
                                   k.BIGB[:, dc * 2048 + tt * 128:dc * 2048 + (tt + 1) * 128],
                                   k.MED1[:, 2048 + dc * 128:2048 + (dc + 1) * 128],
                                   start=(dc == 0), stop=(dc == 7))
                grp_t.append(k.inc(i, 't', 1))
                u += 1

        @self.block.scalar
        def _(sc):
            u = 0
            for which in range(2):
                for qc4 in range(4):
                    k.wait(sc, 't', grp_t[u])
                    if which == 0:
                        o = k.BIGB[:, 16384 + qc4 * 512:16384 + (qc4 + 1) * 512]
                    else:
                        o = k.MED0[:, qc4 * 512:(qc4 + 1) * 512]
                    i = sc.activation(out=o, in_=k.pb[u % 4][:, :], func=AF.Copy)
                    k.inc(i, 'e', 1)
                    u += 1
            for tt in range(NT):
                k.wait(sc, 't', grp_t[u])
                i = sc.activation(out=k.BIGB[:, 18432 + tt * 128:18432 + (tt + 1) * 128],
                                  in_=k.pb[u % 4][:, 0:128], func=AF.Copy)
                k.inc(i, 'e', 1)
                u += 1
        eqkv = self.S['e']

        @self.block.gpsimd
        def _(gp, eqkv=eqkv):
            k.wait(gp, 'e', eqkv)
            k.inc(gp.dma_start(out=k.kv_in.rearrange("r (p f) -> r p f", p=128)[0, :, :],
                               in_=k.MED0[:, 0:2048]), 'g', 16)
            k.inc(gp.dma_start(out=k.kv_in.rearrange("r (p f) -> r p f", p=128)[1, :, :],
                               in_=k.BIGB[:, 18432:20480]), 'g', 16)
            k.wait(gp, 'g', k.S['g'])
            k.inc(gp.collective_compute(
                "AllGather", ALU.bypass, replica_groups=[list(range(NCORE))],
                ins=[k.kv_in.opt()], outs=[k.kv_out.opt()]), 'c', 1)
        cagg = self.S['c']

        tqkv = self.snap('t')

        @self.block.sync
        def _(sync, tqkv=tqkv, cagg=cagg):
            k.wait(sync, 'c', cagg)
            k.wait(sync, 't', tqkv)
            k.wait(sync, 'gt', k.snap('gt'))
            for cc in range(8):
                k.inc(sync.dma_start(
                    out=k.BIGA[:, cc * 2048:(cc + 1) * 2048],
                    in_=k.kv_out.rearrange("r (p f) -> r p f", p=128)[2 * cc, :, :]),
                    'w', 16)
                k.inc(sync.dma_start(
                    out=k.BIGA[:, 16384 + cc * 2048:16384 + (cc + 1) * 2048],
                    in_=k.kv_out.rearrange("r (p f) -> r p f", p=128)[2 * cc + 1, :, :]),
                    'w', 16)
            k.wait(sync, 'e', k.snap('e'))
            k.wait(sync, 'gq3', k.stage_w3)
            k.inc(sync.dma_start(out=k.MED1[:, 0:4096], in_=k.cmask_in[:, :]), 'w', 16)
        k.attn_w = self.S['w']

    def phase_l1attn(self):
        k = self
        ST = self.STATS
        SC = f32(1.0 / np.sqrt(128.0))
        BI = f32(-4.0)
        e_attn0 = self.snap('e')
        g_xb = self.snap('g')

        LOOKAHEAD = 4
        for qc in range(8):
            nk = 16 * (qc + 1)
            masked0 = 16 * qc
            t0 = self.snap('t')
            s0 = self.snap('s')
            v0 = self.snap('v')
            w0 = self.snap('w')
            t_S = [0] * nk
            t_PV = [0] * nk
            _t = t0
            for jj in range(nk + LOOKAHEAD):
                if jj >= LOOKAHEAD:
                    _t += 1
                    t_PV[jj - LOOKAHEAD] = _t
                if jj < nk:
                    _t += 1
                    t_S[jj] = _t
            s_exp = [s0 + j + 1 for j in range(nk)]
            v_after = []
            vv = v0
            for j in range(nk):
                if j >= masked0:
                    vv += 1
                vv += 1
                v_after.append(vv)
            t_lT = t_PV[-1] + 2
            t_wo = [t_lT + 1 + uu for uu in range(4)]

            mxa = []

            @self.block.sync
            def _(sync, qc=qc):
                k.wait(sync, 'w', k.attn_w)
                k.wait(sync, 'g', g_xb)
                if qc > 0:
                    k.wait(sync, 'e', e_attn0 + qc * 5)
                for u in range(2):
                    tt = 2 * qc + u
                    mxa.append(k.inc(sync.dma_start(
                        out=k.XF32[:, u * 1024:(u + 1) * 1024],
                        in_=k.xb[tt * 128:(tt + 1) * 128, :]), 'wx%d' % u, 16))

            @self.block.tensor
            def _(ten, qc=qc, nk=nk, masked0=masked0):
                k.wait(ten, 'w', k.attn_w)
                Sslots = [k.pS[:, 0:256], k.pb[4][:, 0:256],
                          k.pb[0][:, 0:256], k.pb[1][:, 0:256]]
                for jj in range(nk + LOOKAHEAD):
                    if jj >= LOOKAHEAD:
                        j = jj - LOOKAHEAD
                        t2, c2 = j // 8, j % 8
                        if j >= masked0:
                            k.wait(ten, 'v', v_after[j])
                        else:
                            k.wait(ten, 's', s_exp[j])
                        if j == 0 and qc > 0:
                            k.wait(ten, 'e', e_attn0 + (qc - 1) * 5 + 1)
                        i = ten.matmul(k.pP[:, 0:256],
                                       k.BIGA[:, 16384 + (c2 * 16 + t2) * 128:
                                              16384 + (c2 * 16 + t2 + 1) * 128],
                                       k.MED0[:, 4096 + (j % 8) * 256:
                                              4096 + (j % 8) * 256 + 256],
                                       start=(j == 0), stop=(j == nk - 1))
                        assert k.inc(i, 't', 1) == t_PV[j]
                    if jj < nk:
                        t2, c2 = jj // 8, jj % 8
                        if jj >= 4:
                            k.wait(ten, 's', s_exp[jj - 4])
                        elif qc > 0 and jj in (2, 3):
                            k.wait(ten, 'e', e_attn0 + (qc - 1) * 5 + jj)
                        i = ten.matmul(Sslots[jj % 4],
                                       k.BIGA[:, c2 * 2048 + t2 * 128:
                                              c2 * 2048 + (t2 + 1) * 128],
                                       k.BIGB[:, 16384 + qc * 256:16384 + (qc + 1) * 256],
                                       start=True, stop=True)
                        assert k.inc(i, 't', 1) == t_S[jj]
                k.wait(ten, 'v', v_after[nk - 1])
                i = ten.transpose(k.pT[:, 0:128], ST[:, 768:896], k.EYE[:])
                k.inc(i, 't', 1)
                i = ten.transpose(k.pT[:, 128:256], ST[:, 896:1024], k.EYE[:])
                k.inc(i, 't', 1)
                k.wait(ten, 'e', e_attn0 + qc * 5 + 1)
                for u in range(2):
                    for dh in range(2):
                        if qc > 0:
                            k.wait(ten, 'e', e_attn0 + qc * 5)
                        i = ten.matmul(k.pb[u * 2 + dh][:, :],
                                       k.MED0[:, 6144 + u * 128:6144 + (u + 1) * 128],
                                       k.MED0[:, 2048 + dh * 512:2048 + dh * 512 + 512],
                                       start=True, stop=True)
                        k.inc(i, 't', 1)
                assert k.S['t'] == t_wo[-1], (k.S['t'], t_wo)

            @self.block.scalar
            def _(sc, qc=qc, nk=nk):
                if qc == 0:
                    sc.preload_act_table(AF.Exp)
                for j in range(nk):
                    k.wait(sc, 't', t_S[j])
                    if j >= 8:
                        k.wait(sc, 't', t_PV[j - 8])
                        k.wait(sc, 'v', v_after[j - 8])
                    Sslot = [k.pS[:, 0:256], k.pb[4][:, 0:256],
                             k.pb[0][:, 0:256], k.pb[1][:, 0:256]][j % 4]
                    i = sc.activation(
                        out=k.MED0[:, 4096 + (j % 8) * 256:4096 + (j % 8) * 256 + 256],
                        in_=Sslot, func=AF.Exp, scale=SC, bias=BI)
                    k.inc(i, 's', 1)
                k.wait(sc, 't', t_PV[nk - 1])
                i = sc.activation(out=k.MED0[:, 6144:6400], in_=k.pP[:, 0:256],
                                  func=AF.Copy)
                k.inc(i, 'e', 1)

            @self.block.vector
            def _(vec, qc=qc, nk=nk, masked0=masked0):
                for j in range(nk):
                    k.wait(vec, 's', s_exp[j])
                    P = k.MED0[:, 4096 + (j % 8) * 256:4096 + (j % 8) * 256 + 256]
                    if j >= masked0:
                        mi = j - masked0
                        i = vec.tensor_tensor(P, P, k.MED1[:, mi * 256:(mi + 1) * 256],
                                              ALU.mult)
                        k.inc(i, 'v', 1)
                    if j == 0:
                        i = vec.tensor_scalar(ST[:, 768:1024], P, 1.0, None, ALU.mult)
                    else:
                        i = vec.tensor_tensor(ST[:, 768:1024], ST[:, 768:1024], P,
                                              ALU.add)
                    k.inc(i, 'v', 1)
                    assert k.S['v'] == v_after[j]
                k.wait(vec, 't', t_lT)
                vec.reduce_sum(ST[:, 512:513], k.pT[:, 0:128], AX)
                i = vec.reduce_sum(ST[:, 513:514], k.pT[:, 128:256], AX)
                self._vfence(vec, i)
                i = vec.reciprocal(ST[:, 514:516], ST[:, 512:514])
                self._vfence(vec, i)
                for u in range(2):
                    k.wait(vec, 'wx%d' % u, mxa[u])
                    if qc > 0:
                        k.wait(vec, 'g', g_xb + qc * 32)
                    for dh in range(2):
                        k.wait(vec, 't', t_wo[u * 2 + dh])
                        i = vec.scalar_tensor_tensor(
                            k.XF32[:, 2048 + u * 1024 + dh * 512:
                                   2048 + u * 1024 + dh * 512 + 512],
                            k.pb[u * 2 + dh][:, :],
                            ST[:, 514 + u:515 + u],
                            k.XF32[:, u * 1024 + dh * 512:u * 1024 + dh * 512 + 512],
                            ALU.mult, ALU.add)
                        k.inc(i, 'e', 1)
            ewo0 = self.S['e'] - 4

            @self.block.gpsimd
            def _(gp, qc=qc, ewo0=ewo0):
                for u in range(2):
                    k.wait(gp, 'e', ewo0 + (u + 1) * 2)
                    tt = 2 * qc + u
                    mm_ = k.inc(gp.dma_start(out=k.xa[tt * 128:(tt + 1) * 128, :],
                                             in_=k.XF32[:, 2048 + u * 1024:2048 + (u + 1) * 1024]),
                                'g', 16)
                    assert mm_ == g_xb + qc * 32 + (u + 1) * 16

    def phase_l1ffn(self):
        k = self

        for h in range(2):
            for wi, wbf in enumerate((k.w1_bf, k.w2_bf)):
                tprev = self.snap('t')
                vprev = self.snap('v')

                @self.block.sync
                def _(sync, wbf=wbf, tprev=tprev, wi=wi, h=h, vprev=vprev):
                    k.wait(sync, 'gw', k.cast_w12)
                    k.wait(sync, 't', tprev)
                    k.wait(sync, 'gt', k.snap('gt'))
                    if wi == 0:
                        k.wait(sync, 'v', vprev)
                    if h == 0:
                        for dc in range(8):
                            k.inc(sync.dma_start(
                                out=k.BIGA[:, dc * 4096:(dc + 1) * 4096],
                                in_=wbf[0:N, dc * 128:(dc + 1) * 128],
                                transpose=True), 'wta', 16)
                    else:
                        k._biga_reload(sync, k.w1tt if wi == 0 else k.w2tt)
                    if wi == 0:
                        k.wait(sync, 'wt0', k.ln3_done_w[0])
                        k.wait(sync, 'wt1', k.ln3_done_w[1])
                        for dc in range(8):
                            k.inc(sync.dma_start(
                                out=k.MED0[:, dc * 1024:(dc + 1) * 1024],
                                in_=k.act0[h * 1024:(h + 1) * 1024,
                                           dc * 128:(dc + 1) * 128],
                                transpose=True), 'w', 16)
                ww = self.S['w']
                wtaw = self.snap('wta')
                if h == 0:
                    self._biga_store(k.w1tt if wi == 0 else k.w2tt)

                e0 = self.snap('e')
                grp_t = []

                @self.block.tensor
                def _(ten, ww=ww, e0=e0, wtaw=wtaw):
                    k.wait(ten, 'w', ww)
                    k.wait(ten, 'wta', wtaw)
                    k.wait(ten, 'e', e0)
                    u = 0
                    for no in range(32):
                        for tc in range(2):
                            b = u % 4
                            if u >= 4:
                                k.wait(ten, 'e', e0 + u - 4 + 1)
                            for dc in range(8):
                                i = ten.matmul(
                                    k.pb[b][:, :],
                                    k.BIGA[:, dc * 4096 + no * 128:dc * 4096 + (no + 1) * 128],
                                    k.MED0[:, dc * 1024 + tc * 512:dc * 1024 + tc * 512 + 512],
                                    start=(dc == 0), stop=(dc == 7))
                            grp_t.append(k.inc(i, 't', 1))
                            u += 1

                if wi == 0:
                    k.w1_e0 = self.snap('e')

                    @self.block.scalar
                    def _(sc, h=h):
                        if h == 0:
                            sc.preload_act_table(AF.Silu)
                        u = 0
                        for no in range(32):
                            for tc in range(2):
                                k.wait(sc, 't', grp_t[u])
                                i = sc.activation(
                                    out=k.BIGB[:, no * 1024 + tc * 512:
                                               no * 1024 + tc * 512 + 512],
                                    in_=k.pb[u % 4][:, :], func=AF.Silu,
                                    scale=k.SVEC[:, no:no + 1])
                                k.inc(i, 'e', 1)
                                u += 1
                else:
                    @self.block.vector
                    def _(vec, e0=e0):
                        u = 0
                        for no in range(32):
                            for tc in range(2):
                                k.wait(vec, 't', grp_t[u])
                                k.wait(vec, 'e', k.w1_e0 + u + 1)
                                sl = slice(no * 1024 + tc * 512,
                                           no * 1024 + tc * 512 + 512)
                                i = vec.scalar_tensor_tensor(
                                    k.BIGB[:, sl], k.pb[u % 4][:, :],
                                    k.SVEC[:, 32 + no:33 + no],
                                    k.BIGB[:, sl], ALU.mult, ALU.mult)
                                k.inc(i, 'e', 1)
                                u += 1

            for li, (bandsrc, park, scol) in enumerate(
                    ((k.h1_bf, k.gpark, 64), (k.h2_bf, k.kpark, 96))):
                if li == 1:
                    tprev = self.snap('t')
                    gprev = self.snap('g')

                    @self.block.sync
                    def _(sync, h=h, tprev=tprev, gprev=gprev):
                        k.wait(sync, 't', tprev)
                        k.wait(sync, 'g', gprev)
                        for nc_ in range(32):
                            k.inc(sync.dma_start(
                                out=k.BIGB[:, nc_ * 1024:(nc_ + 1) * 1024],
                                in_=k.gpark[nc_ * 128:(nc_ + 1) * 128,
                                            h * 1024:(h + 1) * 1024]), 'w', 16)
                wh = self.S['w']
                t_b0 = self.snap('t')
                tprev2 = self.snap('t')

                mb = []

                @self.block.sync
                def _(sync, bandsrc=bandsrc, wh=wh, t_b0=t_b0, li=li):
                    k.wait(sync, 'gh%d' % (li + 1),
                           k.cast_h1 if li == 0 else k.cast_h2)
                    k.wait(sync, 'w', wh)
                    k.wait(sync, 't', t_b0)
                    k.wait(sync, 'gt', k.snap('gt'))
                    for bg in range(8):
                        if bg >= 2:
                            k.wait(sync, 't', t_b0 + (bg - 1) * 32)
                        for ni in range(32):
                            m = k.inc(sync.dma_start(
                                out=k.BIGA[:, (bg % 2) * 16384 + ni * 512:
                                           (bg % 2) * 16384 + (ni + 1) * 512],
                                in_=bandsrc[bg * 512:(bg + 1) * 512,
                                            ni * 128:(ni + 1) * 128],
                                transpose=True), 'wb%d' % (bg % 2), 16)
                        mb.append(m)

                e0 = self.snap('e')
                g_park0 = self.snap('g')
                grp_t = []

                @self.block.tensor
                def _(ten, e0=e0):
                    k.wait(ten, 'e', e0)
                    u = 0
                    for bg in range(8):
                        k.wait(ten, 'wb%d' % (bg % 2), mb[bg])
                        for no_l in range(4):
                            for tc in range(2):
                                b = u % 4
                                if u >= 4:
                                    k.wait(ten, 'e', e0 + u - 4 + 1)
                                for ni in range(32):
                                    i = ten.matmul(
                                        k.pb[b][:, :],
                                        k.BIGA[:, (bg % 2) * 16384 + ni * 512 + no_l * 128:
                                               (bg % 2) * 16384 + ni * 512 + (no_l + 1) * 128],
                                        k.BIGB[:, ni * 1024 + tc * 512:
                                               ni * 1024 + tc * 512 + 512],
                                        start=(ni == 0), stop=(ni == 31))
                                    if (ni + 1) % 8 == 0 and ni != 31:
                                        k.inc(i, 't', 1)
                                grp_t.append(k.inc(i, 't', 1))
                                u += 1
                assert self.S['t'] - t_b0 == 8 * 8 * 4

                @self.block.scalar
                def _(sc, h=h, li=li, scol=scol, g_park0=g_park0):
                    if li == 0 and h == 0:
                        sc.preload_act_table(AF.Gelu)
                    u = 0
                    for bg in range(8):
                        for no_l in range(4):
                            for tc in range(2):
                                k.wait(sc, 't', grp_t[u])
                                if u >= 2:
                                    k.wait(sc, 'g', g_park0 + u * 16)
                                i = sc.activation(
                                    out=k.DMY[:, (u % 2) * 512:(u % 2) * 512 + 512],
                                    in_=k.pb[u % 4][:, :], func=AF.Gelu,
                                    scale=k.SVEC[:, scol + bg * 4 + no_l:
                                                 scol + bg * 4 + no_l + 1])
                                k.inc(i, 'e', 1)
                                u += 1
                e_ev0 = self.S['e'] - 64

                @self.block.gpsimd
                def _(gp, h=h, park=park, e_ev0=e_ev0):
                    u = 0
                    for bg in range(8):
                        for no_l in range(4):
                            for tc in range(2):
                                k.wait(gp, 'e', e_ev0 + u + 1)
                                no = bg * 4 + no_l
                                k.inc(gp.dma_start(
                                    out=park[no * 128:(no + 1) * 128,
                                             h * 1024 + tc * 512:h * 1024 + tc * 512 + 512],
                                    in_=k.DMY[:, (u % 2) * 512:(u % 2) * 512 + 512]),
                                    'g', 16)
                                u += 1

            tprev = self.snap('t')
            gprev = self.snap('g')

            @self.block.sync
            def _(sync, h=h, tprev=tprev, gprev=gprev):
                k.wait(sync, 'gq3', k.stage_w3)
                k.wait(sync, 'g', gprev)
                k.wait(sync, 't', tprev)
                k.wait(sync, 'gt', k.snap('gt'))
                if h == 0:
                    for nc_ in range(32):
                        k.inc(sync.dma_start(out=k.BIGA[:, nc_ * 1024:(nc_ + 1) * 1024],
                                             in_=k.w3s_bf[0:D, nc_ * 128:(nc_ + 1) * 128],
                                             transpose=True), 'wta', 16)
                else:
                    k._biga_reload(sync, k.w3tt)
                for nc_ in range(32):
                    k.inc(sync.dma_start(
                        out=k.BIGB[:, nc_ * 1024:(nc_ + 1) * 1024],
                        in_=k.kpark[nc_ * 128:(nc_ + 1) * 128,
                                    h * 1024:(h + 1) * 1024]), 'w', 16)
            w3w = self.S['w']
            wtaw3 = self.snap('wta')
            if h == 0:
                self._biga_store(k.w3tt)

            e0w = self.snap('e')
            g_xa3 = self.snap('g')

            mx = []

            @self.block.sync
            def _(sync, h=h, e0w=e0w, g_xa3=g_xa3):
                for tt in range(8):
                    if tt >= 2:
                        k.wait(sync, 'e', e0w + (tt - 1) * 2)
                    elif tt == 0:
                        k.wait(sync, 'g', g_xa3)
                    mx.append(k.inc(sync.dma_start(
                        out=k.XF32[:, (tt % 2) * 1024:(tt % 2) * 1024 + 1024],
                        in_=k.xa[(h * 8 + tt) * 128:(h * 8 + tt + 1) * 128, :]),
                        'wx%d' % (tt % 2), 16))
            grp_tw = []

            @self.block.tensor
            def _(ten, w3w=w3w, e0w=e0w, wtaw3=wtaw3):
                k.wait(ten, 'w', w3w)
                k.wait(ten, 'wta', wtaw3)
                k.wait(ten, 'e', e0w)
                for tt in range(8):
                    for dh in range(2):
                        u = tt * 2 + dh
                        b = u % 4
                        if u >= 4:
                            k.wait(ten, 'e', e0w + u - 4 + 1)
                        for nc_ in range(32):
                            i = ten.matmul(
                                k.pb[b][:, :],
                                k.BIGB[:, nc_ * 1024 + tt * 128:nc_ * 1024 + (tt + 1) * 128],
                                k.BIGA[:, nc_ * 1024 + dh * 512:nc_ * 1024 + dh * 512 + 512],
                                start=(nc_ == 0), stop=(nc_ == 31))
                            if (nc_ + 1) % 8 == 0 and nc_ != 31:
                                k.inc(i, 't', 1)
                        grp_tw.append(k.inc(i, 't', 1))

            gsb = self.snap('g')

            @self.block.vector
            def _(vec, gsb=gsb):
                for tt in range(8):
                    for dh in range(2):
                        u = tt * 2 + dh
                        k.wait(vec, 't', grp_tw[u])
                        if dh == 0:
                            k.wait(vec, 'wx%d' % (tt % 2), mx[tt])
                            if tt >= 2:
                                k.wait(vec, 'g', gsb + tt * 16)
                        i = vec.scalar_tensor_tensor(
                            k.XF32[:, 2048 + (tt % 2) * 1024 + dh * 512:
                                   2048 + (tt % 2) * 1024 + dh * 512 + 512],
                            k.pb[u % 4][:, :], 1.0,
                            k.XF32[:, (tt % 2) * 1024 + dh * 512:
                                   (tt % 2) * 1024 + dh * 512 + 512],
                            ALU.mult, ALU.add)
                        k.inc(i, 'e', 1)
            er0 = self.S['e'] - 16

            @self.block.gpsimd
            def _(gp, h=h, er0=er0, gsb=gsb):
                for tt in range(8):
                    k.wait(gp, 'e', er0 + (tt + 1) * 2)
                    mm_ = k.inc(gp.dma_start(
                        out=k.out[(h * 8 + tt) * 128:(h * 8 + tt + 1) * 128, :],
                        in_=k.XF32[:, 2048 + (tt % 2) * 1024:2048 + (tt % 2) * 1024 + 1024]),
                        'g', 16)
                    assert mm_ == gsb + (tt + 1) * 16

    def phase_finish(self):
        k = self

        @self.block.gpsimd
        def _(gp):
            k.wait(gp, 'g')

        @self.block.sync
        def _(sync):
            k.wait(sync, 'w')


_NC_CACHE = {}


def _build(debug=False):
    import os
    key = (debug, os.environ.get("KPHASES", "99"), os.environ.get("KLN_NOSQ"), os.environ.get("KLN_NONEWT"), os.environ.get("KLN_PLAIN"))
    if key not in _NC_CACHE:
        kb = KB(debug=debug)
        nc = kb.build()
        _NC_CACHE[key] = nc
    return _NC_CACHE[key]


def _make_consts(core):
    eye = np.eye(128, dtype=np.float32)
    ones = np.ones((128, 128), dtype=np.float32)
    tri = np.triu(np.ones((128, 128), dtype=np.float32))
    cm = np.zeros((16, 128, 256), dtype=np.float32)
    for c2 in range(8):
        blk = np.zeros((128, 128), np.float32)
        if c2 < core:
            blk[:] = 1.0
        elif c2 == core:
            blk[:] = tri
        cm[c2, :, 0:128] = blk
        cm[c2, :, 128:256] = 1.0
        cm[8 + c2, :, 128:256] = blk
    cmask = np.ascontiguousarray(
        cm.transpose(1, 0, 2).reshape(128, 4096)).astype(ml_dtypes.bfloat16)
    return eye, ones, cmask


def _make_svec(w1_s, w2_s, h1_s, h2_s, w3_s):
    sv = np.zeros((128, 144), dtype=np.float32)
    sv[:, 0:32] = np.ascontiguousarray(w1_s.reshape(32, 128).T)
    sv[:, 32:64] = np.ascontiguousarray(w2_s.reshape(32, 128).T)
    sv[:, 64:96] = np.ascontiguousarray(h1_s.reshape(32, 128).T)
    sv[:, 96:128] = np.ascontiguousarray(h2_s.reshape(32, 128).T)
    sv[:, 128:136] = np.ascontiguousarray(w3_s.reshape(8, 128).T)
    return sv


def kernel(L0_attn_inputs, L0_attn_key_param_tokens, L0_attn_value_param_tokens,
           L0_ffn_w_gate, L0_ffn_w_up, L0_ffn_w_down,
           L1_attn_Wq, L1_attn_Wk, L1_attn_Wv, L1_attn_Wo,
           L1_ffn_w1_w, L1_ffn_w1_s, L1_ffn_w2_w, L1_ffn_w2_s,
           L1_ffn_h1_w, L1_ffn_h1_s, L1_ffn_h2_w, L1_ffn_h2_s,
           L1_ffn_w3_w, L1_ffn_w3_s, _debug=False):
    nc = _build(debug=_debug)
    x = np.ascontiguousarray(np.asarray(L0_attn_inputs, dtype=np.float32)
                             .reshape(128, 128, D))
    svec = _make_svec(np.asarray(L1_ffn_w1_s, np.float32),
                      np.asarray(L1_ffn_w2_s, np.float32),
                      np.asarray(L1_ffn_h1_s, np.float32),
                      np.asarray(L1_ffn_h2_s, np.float32),
                      np.asarray(L1_ffn_w3_s, np.float32))
    shared = {
        "L0_attn_key_param_tokens": np.ascontiguousarray(L0_attn_key_param_tokens),
        "L0_attn_value_param_tokens": np.ascontiguousarray(L0_attn_value_param_tokens),
        "L0_ffn_w_gate": np.ascontiguousarray(L0_ffn_w_gate),
        "L0_ffn_w_up": np.ascontiguousarray(L0_ffn_w_up),
        "L0_ffn_w_down": np.ascontiguousarray(L0_ffn_w_down),
        "L1_attn_Wq": np.ascontiguousarray(L1_attn_Wq),
        "L1_attn_Wk": np.ascontiguousarray(L1_attn_Wk),
        "L1_attn_Wv": np.ascontiguousarray(L1_attn_Wv),
        "L1_attn_Wo": np.ascontiguousarray(L1_attn_Wo),
        "L1_ffn_w1_w": np.ascontiguousarray(L1_ffn_w1_w),
        "L1_ffn_w2_w": np.ascontiguousarray(L1_ffn_w2_w),
        "L1_ffn_h1_w": np.ascontiguousarray(L1_ffn_h1_w),
        "L1_ffn_h2_w": np.ascontiguousarray(L1_ffn_h2_w),
        "L1_ffn_w3_w": np.ascontiguousarray(L1_ffn_w3_w),
        "cst_svec": svec,
    }
    in_maps = []
    for c in range(NCORE):
        eye, ones, cmask = _make_consts(c)
        m = dict(shared)
        m["L0_attn_inputs"] = np.ascontiguousarray(x[c::8].reshape(TOK, D))
        m["cst_eye"] = eye
        m["cst_ones"] = ones
        m["cst_cmask"] = cmask
        in_maps.append(m)
    res = run_bass_kernel_spmd(nc, in_maps, core_ids=list(range(NCORE)))
    full = np.empty((128, 128, D), dtype=np.float32)
    for c in range(NCORE):
        full[c::8] = res.results[c]["out"].reshape(NT, 128, D)
    out = full.reshape(1, 128 * 128, D)
    if _debug:
        dbg = []
        for name in ("dbg_x1", "dbg_x2", "dbg_x3"):
            fb = np.empty((128, 128, D), dtype=np.float32)
            for c in range(NCORE):
                fb[c::8] = res.results[c][name].reshape(NT, 128, D)
            dbg.append(fb.reshape(1, 128 * 128, D))
        raw = {n: [np.asarray(res.results[c][n]) for c in range(NCORE)]
               for n in ("act0", "act1", "gpark", "kpark", "w3s_bf")}
        return out, dbg, raw
    return out


# version 10 (final)
# speedup vs baseline: 1.0665x; 1.0065x over previous
import numpy as np
import ml_dtypes
import concourse.bass as bass
import concourse.mybir as mybir
from concourse.bass_utils import run_bass_kernel_spmd

F32 = mybir.dt.float32
BF16 = mybir.dt.bfloat16
I32 = mybir.dt.int32
AF = mybir.ActivationFunctionType
ALU = mybir.AluOpType
AX = mybir.AxisListType.X
f32 = np.float32

D = 1024
N = 4096
DK = 128
NT = 16
TOK = 2048
NCORE = 8
RSQRT_ITERS = 9
RC = 12582912.0


class KB:
    def __init__(self, debug=False):
        self.debug = debug
        nc = bass.Bass("TRN2", target_bir_lowering=False,
                       detect_race_conditions=False)
        self.nc = nc
        self.S = {}
        self._sem_h = {}
        self._waits = []

        dt = nc.dram_tensor
        self.x_in = dt("L0_attn_inputs", [TOK, D], F32, kind="ExternalInput").ap()
        self.key_w = dt("L0_attn_key_param_tokens", [N, D], F32, kind="ExternalInput").ap()
        self.val_w = dt("L0_attn_value_param_tokens", [N, D], F32, kind="ExternalInput").ap()
        self.gate_w = dt("L0_ffn_w_gate", [N, D], F32, kind="ExternalInput").ap()
        self.up_w = dt("L0_ffn_w_up", [N, D], F32, kind="ExternalInput").ap()
        self.down_w = dt("L0_ffn_w_down", [D, N], F32, kind="ExternalInput").ap()
        self.wq = dt("L1_attn_Wq", [D, DK], F32, kind="ExternalInput").ap()
        self.wk = dt("L1_attn_Wk", [D, DK], F32, kind="ExternalInput").ap()
        self.wv = dt("L1_attn_Wv", [D, DK], F32, kind="ExternalInput").ap()
        self.wo = dt("L1_attn_Wo", [DK, D], F32, kind="ExternalInput").ap()
        self.w1_w = dt("L1_ffn_w1_w", [N, D], F32, kind="ExternalInput").ap()
        self.w2_w = dt("L1_ffn_w2_w", [N, D], F32, kind="ExternalInput").ap()
        self.h1_w = dt("L1_ffn_h1_w", [N, N], F32, kind="ExternalInput").ap()
        self.h2_w = dt("L1_ffn_h2_w", [N, N], F32, kind="ExternalInput").ap()
        self.w3_w = dt("L1_ffn_w3_w", [D, N], F32, kind="ExternalInput").ap()
        self.eye_in = dt("cst_eye", [128, 128], F32, kind="ExternalInput").ap()
        self.ones_in = dt("cst_ones", [128, 128], F32, kind="ExternalInput").ap()
        self.svec_in = dt("cst_svec", [128, 144], F32, kind="ExternalInput").ap()
        self.cmask_in = dt("cst_cmask", [128, 4096], BF16, kind="ExternalInput").ap()
        self.out = dt("out", [TOK, D], F32, kind="ExternalOutput").ap()
        if debug:
            self.dbg_x1 = dt("dbg_x1", [TOK, D], F32, kind="ExternalOutput").ap()
            self.dbg_x2 = dt("dbg_x2", [TOK, D], F32, kind="ExternalOutput").ap()
            self.dbg_x3 = dt("dbg_x3", [TOK, D], F32, kind="ExternalOutput").ap()

        dkind = dict(kind="ExternalOutput") if debug else {}
        self.key_bf = dt("key_bf", [N, D], BF16).ap()
        self.w1_bf = dt("w1_bf", [N, D], BF16).ap()
        self.w2_bf = dt("w2_bf", [N, D], BF16).ap()
        self.h1_bf = dt("h1_bf", [N, N], BF16).ap()
        self.h2_bf = dt("h2_bf", [N, N], BF16).ap()
        self.gq_bf = dt("gq_bf", [N, D], BF16).ap()
        self.uq_bf = dt("uq_bf", [N, D], BF16).ap()
        self.dq_bf = dt("dq_bf", [D, N], BF16).ap()
        self.w3s_bf = dt("w3s_bf", [D, N], BF16, **dkind).ap()
        self.h1t = dt("h1t", [8, N, 512], BF16).ap()
        self.h2t = dt("h2t", [8, N, 512], BF16).ap()
        self.act0 = dt("act0", [TOK, D], BF16, **dkind).ap()
        self.act1 = dt("act1", [TOK, N], BF16, **dkind).ap()
        self.gpark = dt("gpark", [N, TOK], BF16, **dkind).ap()
        self.kpark = dt("kpark", [N, TOK], BF16, **dkind).ap()
        self.xa = dt("xa", [TOK, D], F32).ap()
        self.xb = dt("xb", [TOK, D], F32).ap()
        self.kv_in = dt("kv_in", [2, 128 * TOK], BF16).ap()
        self.kv_out = dt("kv_out", [16, 128 * TOK], BF16).ap()
        self.gqt = dt("gqt", [128, 32768], BF16).ap()
        self.uqt = dt("uqt", [128, 32768], BF16).ap()
        self.dqt = dt("dqt", [128, 32768], BF16).ap()
        self.w1tt = dt("w1tt", [128, 32768], BF16).ap()
        self.w2tt = dt("w2tt", [128, 32768], BF16).ap()
        self.w3tt = dt("w3tt", [128, 32768], BF16).ap()

    def sem(self, name):
        if name not in self._sem_h:
            self._sem_h[name] = self.ctx.enter_context(self.nc.semaphore(name))
            self.S[name] = 0
        return self._sem_h[name]

    def inc(self, instr, name, n=1):
        instr.then_inc(self.sem(name), n)
        self.S[name] += n
        return self.S[name]

    def wait(self, eng, name, val=None):
        v = self.S[name] if val is None else val
        self._waits.append((name, v))
        eng.wait_ge(self.sem(name), v)

    def snap(self, name):
        self.sem(name)
        return self.S[name]

    def _vfence(self, vec, instr=None):
        if instr is None:
            instr = vec.engine_nop()
        self.inc(instr, 'v', 1)
        self.wait(vec, 'v')

    def _newton_rsqrt(self, vec, vin_off, y_off, t_off, seed):
        ST = self.STATS
        V = ST[:, vin_off:vin_off + 64]
        Y = ST[:, y_off:y_off + 64]
        T = ST[:, t_off:t_off + 64]
        vec.tensor_scalar(Y, V, 0.0, seed, ALU.mult, ALU.add)
        for _ in range(RSQRT_ITERS):
            vec.tensor_tensor(T, Y, Y, ALU.mult)
            vec.tensor_tensor(T, T, V, ALU.mult)
            vec.tensor_scalar(T, T, -0.5, 1.5, ALU.mult, ALU.add)
            vec.tensor_tensor(Y, Y, T, ALU.mult)

    def build(self):
        nc = self.nc
        from contextlib import ExitStack
        self.ctx = ExitStack()
        e = self.ctx.enter_context
        self.BIGA = e(nc.sbuf_tensor("BIGA", [128, 32768], BF16))
        self.BIGB = e(nc.sbuf_tensor("BIGB", [128, 32768], BF16))
        self.MED0 = e(nc.sbuf_tensor("MED0", [128, 8192], BF16))
        self.MED1 = e(nc.sbuf_tensor("MED1", [128, 8192], BF16))
        self.XF32 = e(nc.sbuf_tensor("XF32", [128, 4096], F32))
        self.STATS = e(nc.sbuf_tensor("STATS", [128, 1024], F32))
        self.DMY = e(nc.sbuf_tensor("DMY", [128, 1024], F32))
        self.I32S = e(nc.sbuf_tensor("I32S", [128, 1024], I32))
        self.QF32 = e(nc.sbuf_tensor("QF32", [128, 4096], F32))
        self.EYE = e(nc.sbuf_tensor("EYEc", [128, 128], F32))
        self.ONES = e(nc.sbuf_tensor("ONESc", [128, 128], F32))
        self.SVEC = e(nc.sbuf_tensor("SVECc", [128, 144], F32))
        self.pb = [e(nc.psum_tensor(f"pb{i}", [128, 512], F32)) for i in range(5)]
        self.pP = e(nc.psum_tensor("pP", [128, 256], F32))
        self.pT = e(nc.psum_tensor("pT", [128, 512], F32))
        self.pS = e(nc.psum_tensor("pS", [128, 256], F32))
        for s in ('w', 'g', 't', 's', 'v', 'e', 'c'):
            pass
        self.block = e(nc.Block())
        for s in ('w', 'g', 't', 's', 'v', 'e', 'c'):
            self.sem(s)

        import os
        kmax = int(os.environ.get("KPHASES", "99"))
        phases = [
            self.phase_consts,
            self.phase_casts,
            lambda: self.phase_ln(src=self.x_in, dst=self.act0, kind='bf16',
                                  tag='ln0', src_ready=None),
            self.phase_l0attn_A,
            self.phase_l0attn_O,
            self.phase_qprep,
            lambda: self.dbg_copy('dbg_x1', self.xa),
            lambda: self.phase_ln(src=self.xa, dst=self.act0, kind='quant',
                                  tag='ln1', src_ready=('g', self.snap('g'))),
            self.phase_l0ffn,
            lambda: self.dbg_copy('dbg_x2', self.xb),
            lambda: self.phase_ln(src=self.xb, dst=self.act0, kind='bf16',
                                  tag='ln2', src_ready=('g', self.snap('g'))),
            self.phase_l1attn_qkv,
            self.phase_l1attn,
            lambda: self.dbg_copy('dbg_x3', self.xa),
            lambda: self.phase_ln(src=self.xa, dst=self.act0, kind='bf16',
                                  tag='ln3', src_ready=('g', self.snap('g'))),
            self.phase_l1ffn,
        ]
        for pi, ph in enumerate(phases):
            if pi < kmax:
                ph()
        self.phase_finish()
        bad = [(n, v) for (n, v) in self._waits if v > self.S[n]]
        assert not bad, f"waits beyond final counts: {bad[:10]} finals={self.S}"
        return nc

    def _biga_store(self, tdram):
        k = self
        wta_now = self.snap('wta')

        @self.block.gpsimd
        def _(gp, wta_now=wta_now, tdram=tdram):
            k.wait(gp, 'wta', wta_now)
            k.inc(gp.dma_start(out=tdram[:, :], in_=k.BIGA[:, :]), 'gt', 16)

    def _biga_reload(self, sync, tdram):
        for q4 in range(4):
            self.inc(sync.dma_start(
                out=self.BIGA[:, q4 * 8192:(q4 + 1) * 8192],
                in_=tdram[:, q4 * 8192:(q4 + 1) * 8192]), 'w', 16)

    def dbg_copy(self, name, src):
        if not self.debug:
            return
        k = self
        dst = getattr(self, name)
        gsnap = self.snap('g')

        @self.block.gpsimd
        def _(gp):
            k.wait(gp, 'g', gsnap)
            k.inc(gp.dma_start(out=dst[:, :], in_=src[:, :]), 'g', 16)

    def phase_consts(self):
        k = self

        @self.block.sync
        def _(sync):
            k.inc(sync.dma_start(out=k.EYE[:], in_=k.eye_in[:]), 'wc', 16)
            k.inc(sync.dma_start(out=k.ONES[:], in_=k.ones_in[:]), 'wc', 16)
            k.inc(sync.dma_start(out=k.SVEC[:], in_=k.svec_in[:]), 'wc', 16)
        self.const_w = self.S['wc']

        @self.block.vector
        def _(vec):
            k.inc(vec.memset(k.STATS[:, :], 1.0), 'v', 1)
        self.memset_v = self.S['v']

    def phase_casts(self):
        k = self

        @self.block.gpsimd
        def _(gp):
            k.inc(gp.dma_start(out=k.key_bf[:, :], in_=k.key_w[:, :]), 'gk', 16)
            k.cast_key = k.S['gk']
            for nch in range(8, 32):
                k.inc(gp.dma_start(out=k.BIGB[:, nch * D:(nch + 1) * D],
                                   in_=k.val_w[nch * 128:(nch + 1) * 128, :]), 'gv', 16)
            k.cast_val_hi = k.S['gv']
            k.inc(gp.dma_start(out=k.w1_bf[:, :], in_=k.w1_w[:, :]), 'gw', 16)
            k.inc(gp.dma_start(out=k.w2_bf[:, :], in_=k.w2_w[:, :]), 'gw', 16)
            k.cast_w12 = k.S['gw']
            k.inc(gp.dma_start(out=k.h1_bf[:, :], in_=k.h1_w[:, :]), 'gh1', 16)
            k.cast_h1 = k.S['gh1']
            k.inc(gp.dma_start(out=k.h2_bf[:, :], in_=k.h2_w[:, :]), 'gh2', 16)
            k.cast_h2 = k.S['gh2']

    def phase_qprep(self):
        k = self
        ST = self.STATS
        def chunk_ap(t, cc):
            R, C = t.shape
            if C == 1024:
                return t[cc * 256:(cc + 1) * 256, :].rearrange(
                    "(a p) c -> p a c", p=128)
            else:
                r, j = cc // 2, cc % 2
                return t[r * 128:(r + 1) * 128, j * 2048:(j + 1) * 2048]

        def slot_ap(buf, cc, C):
            s = buf[:, (cc % 2) * 2048:(cc % 2) * 2048 + 2048]
            if C == 1024:
                return s.rearrange("p (a c) -> p a c", c=1024)
            return s

        specs = [(k.gate_w, N * D, k.gq_bf, 592, 'gq0'),
                 (k.up_w, N * D, k.uq_bf, 593, 'gq1'),
                 (k.down_w, D * N, k.dq_bf, 594, 'gq2')]
        for wi, (src, numel, dst, mcol, gq) in enumerate(specs):
            nch = numel // (128 * 2048)
            rpc = (128 * 2048) // src.shape[1]

            v1_0 = self.snap('v')

            m1 = []

            @self.block.sync
            def _(sync, src=src, nch=nch, rpc=rpc, v1_0=v1_0):
                for cc in range(nch):
                    if cc >= 2:
                        k.wait(sync, 'v', v1_0 + (cc - 1))
                    elif cc == 0:
                        k.wait(sync, 'v', v1_0)
                    m1.append(k.inc(sync.dma_start(
                        out=slot_ap(k.QF32, cc, src.shape[1]),
                        in_=chunk_ap(src, cc)), 'wl%d' % (cc % 2), 16))

            @self.block.vector
            def _(vec, nch=nch):
                for cc in range(nch):
                    k.wait(vec, 'wl%d' % (cc % 2), m1[cc])
                    i = vec.reduce_sum(ST[:, 576 + cc:577 + cc],
                                       k.QF32[:, (cc % 2) * 2048:(cc % 2) * 2048 + 2048],
                                       AX, apply_absolute_value=True)
                    k.inc(i, 'v', 1)
                self._vfence(vec)
                i = vec.reduce_sum(ST[:, 560:561], ST[:, 576:576 + nch], AX)
                k.inc(i, 'v', 1)
            vpart = self.S['v']

            eprev = self.snap('e')

            @self.block.tensor
            def _(ten, vpart=vpart, eprev=eprev):
                k.wait(ten, 'wc', k.const_w)
                k.wait(ten, 'v', vpart)
                k.wait(ten, 'e', eprev)
                k.inc(ten.matmul(k.pT[0:1, 0:1], ST[:, 560:561], k.ONES[:, 0:1],
                                 start=True, stop=True), 't', 1)
            tsum = self.S['t']

            @self.block.vector
            def _(vec, numel=numel, tsum=tsum):
                k.wait(vec, 't', tsum)
                i = vec.tensor_scalar(ST[0:1, 561:562], k.pT[0:1, 0:1],
                                      1.0 / numel, 1e-5, ALU.mult, ALU.max)
                self._vfence(vec, i)
                i = vec.reciprocal(ST[0:1, 562:563], ST[0:1, 561:562])
                self._vfence(vec, i)
            vrecip = self.S['v']

            @self.block.tensor
            def _(ten, vrecip=vrecip):
                k.wait(ten, 'v', vrecip)
                k.inc(ten.matmul(k.pT[:, 256:257], k.ONES[0:1, :], ST[0:1, 561:562],
                                 start=True, stop=True), 't', 1)
                k.inc(ten.matmul(k.pT[:, 257:258], k.ONES[0:1, :], ST[0:1, 562:563],
                                 start=True, stop=True), 't', 1)
            tbc = self.S['t']

            @self.block.vector
            def _(vec, mcol=mcol, tbc=tbc):
                k.wait(vec, 't', tbc)
                vec.tensor_copy(ST[:, mcol:mcol + 1], k.pT[:, 256:257])
                i = vec.tensor_copy(ST[:, 595:596], k.pT[:, 257:258])
                self._vfence(vec, i)
                k.inc(vec.engine_nop(), 'e', 1)
            vsb = self.S['v']

            v2_0 = self.snap('v')

            m2 = []

            @self.block.sync
            def _(sync, src=src, nch=nch, rpc=rpc, v2_0=v2_0, vpart=vpart):
                for cc in range(nch):
                    if cc >= 2:
                        k.wait(sync, 'v', v2_0 + (cc - 1) * 2)
                    elif cc == 0:
                        k.wait(sync, 'v', vpart)
                    m2.append(k.inc(sync.dma_start(
                        out=slot_ap(k.QF32, cc, src.shape[1]),
                        in_=chunk_ap(src, cc)), 'wl%d' % (cc % 2), 16))

            @self.block.vector
            def _(vec, nch=nch, vsb=vsb, gq=gq):
                k.wait(vec, 'v', vsb)
                for cc in range(nch):
                    k.wait(vec, 'wl%d' % (cc % 2), m2[cc])
                    if cc >= 2:
                        k.wait(vec, gq, cc * 16)
                    for sub in range(2):
                        qsl = slice((cc % 2) * 2048 + sub * 1024,
                                    (cc % 2) * 2048 + sub * 1024 + 1024)
                        msl = slice((cc % 2) * 2048 + sub * 1024,
                                    (cc % 2) * 2048 + sub * 1024 + 1024)
                        vec.tensor_scalar(k.QF32[:, qsl], k.QF32[:, qsl],
                                          ST[:, 595:596], RC, ALU.mult, ALU.add)
                        vec.tensor_scalar(k.QF32[:, qsl], k.QF32[:, qsl],
                                          -RC, None, ALU.add)
                        vec.tensor_copy(k.I32S[:, :], k.QF32[:, qsl])
                        i = vec.tensor_scalar(k.MED1[:, msl], k.I32S[:, :],
                                              -1.0, 1.0, ALU.max, ALU.min)
                        k.inc(i, 'v', 1)
            vB = self.S['v'] - nch * 2

            @self.block.gpsimd
            def _(gp, dst=dst, nch=nch, rpc=rpc, vB=vB, gq=gq):
                for cc in range(nch):
                    k.wait(gp, 'v', vB + (cc + 1) * 2)
                    k.inc(gp.dma_start(
                        out=chunk_ap(dst, cc),
                        in_=slot_ap(k.MED1, cc, dst.shape[1])),
                        gq, 16)
            if wi == 0:
                k.stage_gate = self.S[gq]
            elif wi == 1:
                k.stage_up = self.S[gq]
            else:
                k.stage_down = self.S[gq]

        v3_0 = self.snap('v')

        m3 = []

        @self.block.sync
        def _(sync, v3_0=v3_0):
            for cc in range(8):
                k.wait(sync, 'v', v3_0 + cc)
                m3.append(k.inc(sync.dma_start(
                    out=k.QF32[:, :],
                    in_=k.w3_w[cc * 128:(cc + 1) * 128, :]), 'wl%d' % (cc % 2), 16))

        @self.block.vector
        def _(vec):
            k.wait(vec, 'wc', k.const_w)
            for cc in range(8):
                k.wait(vec, 'wl%d' % (cc % 2), m3[cc])
                if cc >= 1:
                    k.wait(vec, 'gq3', cc * 16)
                for sub in range(4):
                    i = vec.tensor_scalar(
                        k.MED1[:, sub * 1024:sub * 1024 + 1024],
                        k.QF32[:, sub * 1024:sub * 1024 + 1024],
                        k.SVEC[:, 128 + cc:129 + cc], None, ALU.mult)
                k.inc(i, 'v', 1)
        v3B = self.S['v'] - 8

        @self.block.gpsimd
        def _(gp, v3B=v3B):
            for cc in range(8):
                k.wait(gp, 'v', v3B + cc + 1)
                k.inc(gp.dma_start(out=k.w3s_bf[cc * 128:(cc + 1) * 128, :],
                                   in_=k.MED1[:, 0:4096]), 'gq3', 16)
        k.stage_w3 = self.S['gq3']

    def phase_ln(self, src, dst, kind, tag, src_ready):
        import os
        k = self
        ST = self.STATS
        NOSQ = os.environ.get("KLN_NOSQ") == "1"
        NONEWT = os.environ.get("KLN_NONEWT") == "1"
        PLAIN = os.environ.get("KLN_PLAIN") == "1"

        sA0 = self.snap('s')
        vA0 = self.snap('v')
        mA = []

        @self.block.sync
        def _(sync):
            for t in range(NT):
                if t >= 2:
                    k.wait(sync, 's', sA0 + (t - 1))
                    k.wait(sync, 'v', vA0 + (t - 1))
                elif t == 0 and src_ready is not None:
                    k.wait(sync, src_ready[0], src_ready[1])
                mA.append(k.inc(sync.dma_start(
                    out=k.XF32[:, (t % 2) * 1024:(t % 2) * 1024 + 1024],
                    in_=src[t * 128:(t + 1) * 128, :]), 'wl%d' % (t % 2), 16))

        @self.block.scalar
        def _(sc):
            k.wait(sc, 'v', k.memset_v)
            for t in range(NT):
                k.wait(sc, 'wl%d' % (t % 2), mA[t])
                if NOSQ:
                    i = sc.activation(out=k.DMY[:, :],
                                      in_=k.XF32[:, (t % 2) * 1024:(t % 2) * 1024 + 1024],
                                      func=AF.Square)
                else:
                    i = sc.activation(out=k.DMY[:, :],
                                      in_=k.XF32[:, (t % 2) * 1024:(t % 2) * 1024 + 1024],
                                      func=AF.Square, accum_out=ST[:, 64 + t:65 + t])
                k.inc(i, 's', 1)
        sA = self.S['s']

        @self.block.vector
        def _(vec):
            for t in range(NT):
                k.wait(vec, 'wl%d' % (t % 2), mA[t])
                i = vec.reduce_sum(ST[:, t:t + 1],
                                   k.XF32[:, (t % 2) * 1024:(t % 2) * 1024 + 1024], AX)
                k.inc(i, 'v', 1)
            k.wait(vec, 's', sA)
            self._vfence(vec)
            vec.tensor_scalar(ST[:, 0:64], ST[:, 0:64], 1.0 / D, None, ALU.mult)
            vec.tensor_tensor(ST[:, 256:320], ST[:, 0:64], ST[:, 0:64], ALU.mult)
            vec.tensor_scalar(ST[:, 128:192], ST[:, 64:128], 1.0 / D, None, ALU.mult)
            vec.tensor_tensor(ST[:, 128:192], ST[:, 128:192], ST[:, 256:320],
                              ALU.subtract)
            vec.tensor_scalar(ST[:, 128:192], ST[:, 128:192], 1e-5, None, ALU.add)
            if NONEWT:
                vec.tensor_scalar(ST[:, 320:384], ST[:, 128:192], 0.0, 1.0,
                                  ALU.mult, ALU.add)
            else:
                self._newton_rsqrt(vec, 128, 320, 256, 0.6)
            vec.tensor_tensor(ST[:, 384:448], ST[:, 0:64], ST[:, 320:384], ALU.mult)
            i = vec.tensor_scalar(ST[:, 384:448], ST[:, 384:448], -1.0, None, ALU.mult)
            k.inc(i, 'v', 1)
        vstats = self.S['v']

        sB0 = self.snap('s')
        vB0 = self.snap('v')
        mB = []
        wt_base = (self.snap('wt0'), self.snap('wt1'))
        mst = {t: wt_base[t % 2] + (t // 2 + 1) * 16 for t in range(NT)}
        ssnap = [sB0 + t + 1 for t in range(NT)]

        def emit_store(sync, t):
            if kind == 'bf16':
                sync_gate = ('s', ssnap[t])
            else:
                sync_gate = ('v', vB0 + (t + 1) * 12)
            k.wait(sync, *sync_gate)
            mm_ = k.inc(sync.dma_start(
                out=dst[t * 128:(t + 1) * 128, :],
                in_=k.MED0[:, (t % 2) * 1024:(t % 2) * 1024 + 1024]),
                'wt%d' % (t % 2), 16)
            assert mm_ == mst[t]

        @self.block.sync
        def _(sync):
            for t in range(NT):
                if t >= 2:
                    k.wait(sync, 's', sB0 + (t - 1))
                    if kind == 'quant':
                        k.wait(sync, 'v', vB0 + (t - 1) * 12)
                elif t == 0:
                    k.wait(sync, 's', sA)
                    k.wait(sync, 'v', vstats)
                mB.append(k.inc(sync.dma_start(
                    out=k.XF32[:, (t % 2) * 1024:(t % 2) * 1024 + 1024],
                    in_=src[t * 128:(t + 1) * 128, :]), 'wl%d' % (t % 2), 16))
                if t >= 2:
                    emit_store(sync, t - 2)
            emit_store(sync, NT - 2)
            emit_store(sync, NT - 1)

        @self.block.scalar
        def _(sc):
            k.wait(sc, 'v', vstats)
            for t in range(NT):
                k.wait(sc, 'wl%d' % (t % 2), mB[t])
                if kind == 'bf16':
                    if t >= 2:
                        k.wait(sc, 'wt%d' % (t % 2), mst[t - 2])
                    o = k.MED0[:, (t % 2) * 1024:(t % 2) * 1024 + 1024]
                else:
                    o = k.XF32[:, 2048 + (t % 2) * 1024:2048 + (t % 2) * 1024 + 1024]
                if PLAIN:
                    i = sc.activation(out=o,
                                      in_=k.XF32[:, (t % 2) * 1024:(t % 2) * 1024 + 1024],
                                      func=AF.Identity)
                else:
                    i = sc.activation(out=o,
                                      in_=k.XF32[:, (t % 2) * 1024:(t % 2) * 1024 + 1024],
                                      func=AF.Identity, scale=ST[:, 320 + t:321 + t],
                                      bias=ST[:, 384 + t:385 + t])
                ssnap.append(k.inc(i, 's', 1))

        if kind == 'bf16':
            pass
        else:
            @self.block.vector
            def _(vec):
                for t in range(NT):
                    vb = k.S['v']
                    k.wait(vec, 's', ssnap[t])
                    if t >= 2:
                        k.wait(vec, 'wt%d' % (t % 2), mst[t - 2])
                    lnf = k.XF32[:, 2048 + (t % 2) * 1024:2048 + (t % 2) * 1024 + 1024]
                    i = vec.reduce_max(ST[:, 528:529], lnf, AX,
                                       apply_absolute_value=True)
                    self._vfence(vec, i)
                    vec.tensor_scalar(ST[:, 528:529], ST[:, 528:529], 1e-5, None,
                                      ALU.max)
                    i = vec.tensor_scalar(ST[:, 448 + t:449 + t], ST[:, 528:529],
                                          1.0 / 127.0, None, ALU.mult)
                    self._vfence(vec, i)
                    i = vec.reciprocal(ST[:, 544:545], ST[:, 528:529])
                    self._vfence(vec, i)
                    vec.tensor_scalar(ST[:, 544:545], ST[:, 544:545], 127.0, None,
                                      ALU.mult)
                    vec.tensor_scalar(ST[:, 464 + t:465 + t], ST[:, 448 + t:449 + t],
                                      ST[:, 592:593], None, ALU.mult)
                    vec.tensor_scalar(ST[:, 480 + t:481 + t], ST[:, 448 + t:449 + t],
                                      ST[:, 593:594], None, ALU.mult)
                    self._vfence(vec)
                    vec.tensor_scalar(lnf, lnf, ST[:, 544:545], RC, ALU.mult,
                                      ALU.add)
                    vec.tensor_scalar(lnf, lnf, -RC, None, ALU.add)
                    vec.tensor_copy(k.I32S[:, :], lnf)
                    i = vec.tensor_copy(k.MED0[:, (t % 2) * 1024:(t % 2) * 1024 + 1024],
                                        k.I32S[:, :])
                    k.inc(i, 'v', 1)
                    while k.S['v'] - vb < 12:
                        k.inc(vec.engine_nop(), 'v', 1)
                    assert k.S['v'] - vb == 12

        setattr(self, tag + "_done_w", (self.S['wt0'], self.S['wt1']))

    def phase_l0attn_A(self):
        k = self

        @self.block.sync
        def _(sync):
            k.wait(sync, 'gk', k.cast_key)
            for dc in range(8):
                k.inc(sync.dma_start(out=k.BIGA[:, dc * 4096:(dc + 1) * 4096],
                                     in_=k.key_bf[0:N, dc * 128:(dc + 1) * 128],
                                     transpose=True), 'w', 16)
        wkey = self.S['w']

        for h in range(2):
            tprev = self.snap('t')

            @self.block.sync
            def _(sync, h=h, tprev=tprev):
                if h == 1:
                    k.wait(sync, 't', tprev)
                for dc in range(8):
                    k.inc(sync.dma_start(
                        out=k.MED0[:, dc * 1024:(dc + 1) * 1024],
                        in_=k.act0[h * 1024:(h + 1) * 1024, dc * 128:(dc + 1) * 128],
                        transpose=True), 'w', 16)
            wln = self.S['w']

            e0 = self.snap('e')
            grp_t = []

            @self.block.tensor
            def _(ten, wln=wln, e0=e0):
                k.wait(ten, 'w', max(wkey, wln))
                k.wait(ten, 'e', e0)
                for tt in range(8):
                    for nc4 in range(8):
                        u = tt * 8 + nc4
                        b = u % 4
                        if u >= 4:
                            k.wait(ten, 'e', e0 + u - 4 + 1)
                        for dc in range(8):
                            i = ten.matmul(k.pb[b][:, :],
                                           k.MED0[:, dc * 1024 + tt * 128:
                                                  dc * 1024 + (tt + 1) * 128],
                                           k.BIGA[:, dc * 4096 + nc4 * 512:
                                                  dc * 4096 + (nc4 + 1) * 512],
                                           start=(dc == 0), stop=(dc == 7))
                        grp_t.append(k.inc(i, 't', 1))

            v0 = self.snap('v')
            sg = []
            gst = [wln + (t + 1) * 16 for t in range(8)]

            @self.block.scalar
            def _(sc, h=h, v0=v0):
                if h == 0:
                    sc.preload_act_table(AF.Gelu)

                def evac(tt):
                    for nc4 in range(8):
                        u = tt * 8 + nc4
                        k.wait(sc, 't', grp_t[u])
                        if nc4 == 0 and tt >= 2:
                            k.wait(sc, 'w', gst[tt - 2])
                        A = k.BIGB[:, (tt % 2) * 4096:(tt % 2) * 4096 + 4096]
                        i = sc.activation(out=A[:, nc4 * 512:(nc4 + 1) * 512],
                                          in_=k.pb[u % 4][:, :], func=AF.Copy)
                        k.inc(i, 'e', 1)

                def gelu(tt):
                    k.wait(sc, 'v', v0 + (tt + 1) * 7)
                    A = k.BIGB[:, (tt % 2) * 4096:(tt % 2) * 4096 + 4096]
                    i = sc.activation(out=A[:, :], in_=A[:, :], func=AF.Gelu,
                                      scale=k.STATS[:, 560 + tt:561 + tt])
                    sg.append(k.inc(i, 's', 1))

                evac(0)
                for tt in range(1, 8):
                    evac(tt)
                    gelu(tt - 1)
                gelu(7)

            @self.block.vector
            def _(vec, v0=v0, e0=e0):
                for tt in range(8):
                    vb = k.S['v']
                    k.wait(vec, 'e', e0 + (tt + 1) * 8)
                    A = k.BIGB[:, (tt % 2) * 4096:(tt % 2) * 4096 + 4096]
                    for sub in range(4):
                        vec.scalar_tensor_tensor(
                            k.DMY[:, :], A[:, sub * 1024:(sub + 1) * 1024], 1.0,
                            A[:, sub * 1024:(sub + 1) * 1024], ALU.mult, ALU.mult,
                            accum_out=k.STATS[:, 736 + sub:737 + sub])
                    self._vfence(vec)
                    i = vec.reduce_sum(k.STATS[:, 608:609], k.STATS[:, 736:740], AX)
                    self._vfence(vec, i)
                    self._newton_rsqrt(vec, 608, 672, 256, 0.02)
                    i = vec.tensor_scalar(k.STATS[:, 560 + tt:561 + tt],
                                          k.STATS[:, 672:673], 64.0, None, ALU.mult)
                    k.inc(i, 'v', 1)
                    while k.S['v'] - vb < 7:
                        k.inc(vec.engine_nop(), 'v', 1)
                    assert k.S['v'] - vb == 7

            @self.block.sync
            def _(sync, h=h):
                for tt in range(8):
                    k.wait(sync, 's', sg[tt])
                    A = k.BIGB[:, (tt % 2) * 4096:(tt % 2) * 4096 + 4096]
                    mm_ = k.inc(sync.dma_start(
                        out=k.act1[(h * 8 + tt) * 128:(h * 8 + tt + 1) * 128, :],
                        in_=A[:, :]), 'w', 16)
                    assert mm_ == gst[tt], (mm_, gst[tt])
        k.G_parked_w = self.S['w']
        k.A_all_t = self.S['t']

    def phase_l0attn_O(self):
        k = self
        sprev = self.snap('s')

        @self.block.gpsimd
        def _(gp, sprev=sprev):
            k.wait(gp, 's', sprev)
            k.wait(gp, 'w', k.G_parked_w)
            for nch in range(8):
                k.inc(gp.dma_start(out=k.BIGB[:, nch * D:(nch + 1) * D],
                                   in_=k.val_w[nch * 128:(nch + 1) * 128, :]), 'gv', 16)
        gval = self.S['gv']

        for h in range(2):
            t_half0 = self.snap('t')

            @self.block.sync
            def _(sync, h=h, t_half0=t_half0):
                k.wait(sync, 'w', k.G_parked_w)
                k.wait(sync, 't', k.A_all_t if h == 0 else t_half0)
                for nc_ in range(32):
                    k.inc(sync.dma_start(
                        out=k.BIGA[:, nc_ * 1024:(nc_ + 1) * 1024],
                        in_=k.act1[h * 1024:(h + 1) * 1024, nc_ * 128:(nc_ + 1) * 128],
                        transpose=True), 'w', 16)
            wgt = self.S['w']

            e0x = self.snap('e')
            mx = []

            @self.block.sync
            def _(sync, h=h, e0x=e0x):
                for tt in range(8):
                    if tt >= 2:
                        k.wait(sync, 'e', e0x + (tt - 1) * 2)
                    mx.append(k.inc(sync.dma_start(
                        out=k.XF32[:, (tt % 2) * 1024:(tt % 2) * 1024 + 1024],
                        in_=k.x_in[(h * 8 + tt) * 128:(h * 8 + tt + 1) * 128, :]),
                        'wx%d' % (tt % 2), 16))

            e0 = self.snap('e')
            grp_t = []

            @self.block.tensor
            def _(ten, wgt=wgt, e0=e0):
                k.wait(ten, 'w', wgt)
                k.wait(ten, 'gv', gval)
                k.wait(ten, 'e', e0)
                for tt in range(8):
                    for dh in range(2):
                        u = tt * 2 + dh
                        b = u % 4
                        if u >= 4:
                            k.wait(ten, 'e', e0 + u - 4 + 1)
                        for nc_ in range(32):
                            i = ten.matmul(
                                k.pb[b][:, :],
                                k.BIGA[:, nc_ * 1024 + tt * 128:nc_ * 1024 + (tt + 1) * 128],
                                k.BIGB[:, nc_ * 1024 + dh * 512:nc_ * 1024 + dh * 512 + 512],
                                start=(nc_ == 0), stop=(nc_ == 31))
                            if (nc_ + 1) % 8 == 0 and nc_ != 31:
                                k.inc(i, 't', 1)
                        grp_t.append(k.inc(i, 't', 1))

            gsb = self.snap('g')

            @self.block.vector
            def _(vec, gsb=gsb):
                for tt in range(8):
                    for dh in range(2):
                        u = tt * 2 + dh
                        k.wait(vec, 't', grp_t[u])
                        if dh == 0:
                            k.wait(vec, 'wx%d' % (tt % 2), mx[tt])
                            if tt >= 2:
                                k.wait(vec, 'g', gsb + tt * 16)
                        i = vec.scalar_tensor_tensor(
                            k.XF32[:, 2048 + (tt % 2) * 1024 + dh * 512:
                                   2048 + (tt % 2) * 1024 + dh * 512 + 512],
                            k.pb[u % 4][:, :], 1.0,
                            k.XF32[:, (tt % 2) * 1024 + dh * 512:
                                   (tt % 2) * 1024 + dh * 512 + 512],
                            ALU.mult, ALU.add)
                        k.inc(i, 'e', 1)
            er0 = self.S['e'] - 16

            @self.block.gpsimd
            def _(gp, h=h, er0=er0, gsb=gsb):
                for tt in range(8):
                    k.wait(gp, 'e', er0 + (tt + 1) * 2)
                    mm_ = k.inc(gp.dma_start(
                        out=k.xa[(h * 8 + tt) * 128:(h * 8 + tt + 1) * 128, :],
                        in_=k.XF32[:, 2048 + (tt % 2) * 1024:2048 + (tt % 2) * 1024 + 1024]),
                        'g', 16)
                    assert mm_ == gsb + (tt + 1) * 16

    def phase_l0ffn(self):
        k = self
        ST = self.STATS

        for h in range(2):
            tprev = self.snap('t')

            @self.block.sync
            def _(sync, h=h, tprev=tprev):
                k.wait(sync, 'gq0', k.stage_gate)
                k.wait(sync, 't', tprev)
                k.wait(sync, 'gt', k.snap('gt'))
                if h == 0:
                    for dc in range(8):
                        k.inc(sync.dma_start(out=k.BIGA[:, dc * 4096:(dc + 1) * 4096],
                                             in_=k.gq_bf[0:N, dc * 128:(dc + 1) * 128],
                                             transpose=True), 'wta', 16)
                else:
                    k._biga_reload(sync, k.gqt)
                k.wait(sync, 'wt0', k.ln1_done_w[0])
                k.wait(sync, 'wt1', k.ln1_done_w[1])
                for dc in range(8):
                    k.inc(sync.dma_start(
                        out=k.MED0[:, dc * 1024:(dc + 1) * 1024],
                        in_=k.act0[h * 1024:(h + 1) * 1024, dc * 128:(dc + 1) * 128],
                        transpose=True), 'w', 16)
            wg = self.S['w']
            wtag = self.snap('wta')
            if h == 0:
                self._biga_store(k.gqt)

            sprev_v = self.snap('v')
            e0 = self.snap('e')
            grp_t = []

            @self.block.tensor
            def _(ten, wg=wg, e0=e0, wtag=wtag):
                k.wait(ten, 'w', wg)
                k.wait(ten, 'wta', wtag)
                k.wait(ten, 'e', e0)
                for tt in range(8):
                    for nc4 in range(8):
                        u = tt * 8 + nc4
                        b = u % 4
                        if u >= 4:
                            k.wait(ten, 'e', e0 + u - 4 + 1)
                        for dc in range(8):
                            i = ten.matmul(k.pb[b][:, :],
                                           k.MED0[:, dc * 1024 + tt * 128:
                                                  dc * 1024 + (tt + 1) * 128],
                                           k.BIGA[:, dc * 4096 + nc4 * 512:
                                                  dc * 4096 + (nc4 + 1) * 512],
                                           start=(dc == 0), stop=(dc == 7))
                        grp_t.append(k.inc(i, 't', 1))
            tgate = self.S['t']

            @self.block.scalar
            def _(sc, h=h):
                if h == 0:
                    sc.preload_act_table(AF.Silu)
                for tt in range(8):
                    for nc4 in range(8):
                        u = tt * 8 + nc4
                        k.wait(sc, 't', grp_t[u])
                        i = sc.activation(
                            out=k.BIGB[:, tt * 4096 + nc4 * 512:tt * 4096 + nc4 * 512 + 512],
                            in_=k.pb[u % 4][:, :], func=AF.Silu,
                            scale=ST[:, 464 + h * 8 + tt:465 + h * 8 + tt])
                        k.inc(i, 'e', 1)

            @self.block.sync
            def _(sync, h=h, tgate=tgate):
                k.wait(sync, 'gq1', k.stage_up)
                k.wait(sync, 't', tgate)
                k.wait(sync, 'gt', k.snap('gt'))
                if h == 0:
                    for dc in range(8):
                        k.inc(sync.dma_start(out=k.BIGA[:, dc * 4096:(dc + 1) * 4096],
                                             in_=k.uq_bf[0:N, dc * 128:(dc + 1) * 128],
                                             transpose=True), 'wta', 16)
                else:
                    k._biga_reload(sync, k.uqt)
            wu = self.S['w']
            wtau = self.snap('wta')
            if h == 0:
                self._biga_store(k.uqt)

            e0u = self.snap('e')
            grp_tu = []

            @self.block.tensor
            def _(ten, wu=wu, e0u=e0u, wtau=wtau):
                k.wait(ten, 'w', wu)
                k.wait(ten, 'wta', wtau)
                k.wait(ten, 'e', e0u)
                for tt in range(8):
                    for nc4 in range(8):
                        u = tt * 8 + nc4
                        b = u % 4
                        if u >= 4:
                            k.wait(ten, 'e', e0u + u - 4 + 1)
                        for dc in range(8):
                            i = ten.matmul(k.pb[b][:, :],
                                           k.MED0[:, dc * 1024 + tt * 128:
                                                  dc * 1024 + (tt + 1) * 128],
                                           k.BIGA[:, dc * 4096 + nc4 * 512:
                                                  dc * 4096 + (nc4 + 1) * 512],
                                           start=(dc == 0), stop=(dc == 7))
                        grp_tu.append(k.inc(i, 't', 1))
            tup = self.S['t']

            vq_marks = []

            @self.block.vector
            def _(vec, h=h, e0=e0):
                for tt in range(8):
                    for nc4 in range(8):
                        u = tt * 8 + nc4
                        k.wait(vec, 't', grp_tu[u])
                        k.wait(vec, 'e', e0 + u + 1)
                        sl = slice(tt * 4096 + nc4 * 512, tt * 4096 + nc4 * 512 + 512)
                        i = vec.scalar_tensor_tensor(
                            k.BIGB[:, sl], k.pb[u % 4][:, :],
                            ST[:, 480 + h * 8 + tt:481 + h * 8 + tt],
                            k.BIGB[:, sl], ALU.mult, ALU.mult)
                        k.inc(i, 'e', 1)
                    H = k.BIGB[:, tt * 4096:(tt + 1) * 4096]
                    i = vec.reduce_max(ST[:, 528:529], H, AX, apply_absolute_value=True)
                    self._vfence(vec, i)
                    vec.tensor_scalar(ST[:, 528:529], ST[:, 528:529], 1e-5, None,
                                      ALU.max)
                    i = vec.tensor_scalar(ST[:, 556:557], ST[:, 528:529],
                                          1.0 / 127.0, None, ALU.mult)
                    self._vfence(vec, i)
                    vec.tensor_scalar(ST[:, 496 + h * 8 + tt:497 + h * 8 + tt],
                                      ST[:, 556:557], ST[:, 594:595], None, ALU.mult)
                    i = vec.reciprocal(ST[:, 544:545], ST[:, 528:529])
                    self._vfence(vec, i)
                    i = vec.tensor_scalar(ST[:, 544:545], ST[:, 544:545], 127.0,
                                          None, ALU.mult)
                    self._vfence(vec, i)
                    for sub in range(4):
                        ssl = slice(tt * 4096 + sub * 1024, tt * 4096 + sub * 1024 + 1024)
                        vec.tensor_scalar(k.DMY[:, :], k.BIGB[:, ssl],
                                          ST[:, 544:545], RC, ALU.mult, ALU.add)
                        vec.tensor_scalar(k.DMY[:, :], k.DMY[:, :], -RC, None,
                                          ALU.add)
                        vec.tensor_copy(k.I32S[:, :], k.DMY[:, :])
                        i = vec.tensor_copy(k.BIGB[:, ssl], k.I32S[:, :])
                    vq_marks.append(k.inc(i, 'v', 1))

            @self.block.gpsimd
            def _(gp, h=h):
                for tt in range(8):
                    k.wait(gp, 'v', vq_marks[tt])
                    k.inc(gp.dma_start(
                        out=k.act1[(h * 8 + tt) * 128:(h * 8 + tt + 1) * 128, :],
                        in_=k.BIGB[:, tt * 4096:(tt + 1) * 4096]), 'g', 16)
            gc2 = self.S['g']

            @self.block.sync
            def _(sync, h=h, tup=tup, gc2=gc2):
                k.wait(sync, 'gq2', k.stage_down)
                k.wait(sync, 'g', gc2)
                k.wait(sync, 't', tup)
                k.wait(sync, 'gt', k.snap('gt'))
                if h == 0:
                    for nc_ in range(32):
                        k.inc(sync.dma_start(out=k.BIGA[:, nc_ * 1024:(nc_ + 1) * 1024],
                                             in_=k.dq_bf[0:D, nc_ * 128:(nc_ + 1) * 128],
                                             transpose=True), 'wta', 16)
                else:
                    k._biga_reload(sync, k.dqt)
                for nc_ in range(32):
                    k.inc(sync.dma_start(
                        out=k.BIGB[:, nc_ * 1024:(nc_ + 1) * 1024],
                        in_=k.act1[h * 1024:(h + 1) * 1024, nc_ * 128:(nc_ + 1) * 128],
                        transpose=True), 'w', 16)
            wd = self.S['w']
            wtad = self.snap('wta')
            if h == 0:
                self._biga_store(k.dqt)

            e0d = self.snap('e')

            mx = []

            @self.block.sync
            def _(sync, h=h, e0d=e0d, gc2=gc2):
                for tt in range(8):
                    if tt >= 2:
                        k.wait(sync, 'e', e0d + (tt - 1) * 2)
                    elif tt == 0:
                        k.wait(sync, 'g', gc2)
                    mx.append(k.inc(sync.dma_start(
                        out=k.XF32[:, (tt % 2) * 1024:(tt % 2) * 1024 + 1024],
                        in_=k.xa[(h * 8 + tt) * 128:(h * 8 + tt + 1) * 128, :]),
                        'wx%d' % (tt % 2), 16))
            grp_td = []

            @self.block.tensor
            def _(ten, wd=wd, e0d=e0d, wtad=wtad):
                k.wait(ten, 'w', wd)
                k.wait(ten, 'wta', wtad)
                k.wait(ten, 'e', e0d)
                for tt in range(8):
                    for dh in range(2):
                        u = tt * 2 + dh
                        b = u % 4
                        if u >= 4:
                            k.wait(ten, 'e', e0d + u - 4 + 1)
                        for nc_ in range(32):
                            i = ten.matmul(
                                k.pb[b][:, :],
                                k.BIGB[:, nc_ * 1024 + tt * 128:nc_ * 1024 + (tt + 1) * 128],
                                k.BIGA[:, nc_ * 1024 + dh * 512:nc_ * 1024 + dh * 512 + 512],
                                start=(nc_ == 0), stop=(nc_ == 31))
                            if (nc_ + 1) % 8 == 0 and nc_ != 31:
                                k.inc(i, 't', 1)
                        grp_td.append(k.inc(i, 't', 1))

            gsb = self.snap('g')

            @self.block.vector
            def _(vec, h=h, gsb=gsb):
                for tt in range(8):
                    for dh in range(2):
                        u = tt * 2 + dh
                        k.wait(vec, 't', grp_td[u])
                        if dh == 0:
                            k.wait(vec, 'wx%d' % (tt % 2), mx[tt])
                            if tt >= 2:
                                k.wait(vec, 'g', gsb + tt * 16)
                        i = vec.scalar_tensor_tensor(
                            k.XF32[:, 2048 + (tt % 2) * 1024 + dh * 512:
                                   2048 + (tt % 2) * 1024 + dh * 512 + 512],
                            k.pb[u % 4][:, :],
                            ST[:, 496 + h * 8 + tt:497 + h * 8 + tt],
                            k.XF32[:, (tt % 2) * 1024 + dh * 512:
                                   (tt % 2) * 1024 + dh * 512 + 512],
                            ALU.mult, ALU.add)
                        k.inc(i, 'e', 1)
            er0 = self.S['e'] - 16

            @self.block.gpsimd
            def _(gp, h=h, er0=er0, gsb=gsb):
                for tt in range(8):
                    k.wait(gp, 'e', er0 + (tt + 1) * 2)
                    mm_ = k.inc(gp.dma_start(
                        out=k.xb[(h * 8 + tt) * 128:(h * 8 + tt + 1) * 128, :],
                        in_=k.XF32[:, 2048 + (tt % 2) * 1024:2048 + (tt % 2) * 1024 + 1024]),
                        'g', 16)
                    assert mm_ == gsb + (tt + 1) * 16

    def phase_l1attn_qkv(self):
        k = self

        @self.block.gpsimd
        def _(gp):
            for dc in range(8):
                k.inc(gp.dma_start(out=k.MED1[:, dc * 128:(dc + 1) * 128],
                                   in_=k.wq[dc * 128:(dc + 1) * 128, :]), 'g', 16)
                k.inc(gp.dma_start(out=k.MED1[:, 1024 + dc * 128:1024 + (dc + 1) * 128],
                                   in_=k.wk[dc * 128:(dc + 1) * 128, :]), 'g', 16)
                k.inc(gp.dma_start(out=k.MED1[:, 2048 + dc * 128:2048 + (dc + 1) * 128],
                                   in_=k.wv[dc * 128:(dc + 1) * 128, :]), 'g', 16)
            k.inc(gp.dma_start(out=k.MED0[:, 2048:3072], in_=k.wo[:, :]), 'g', 16)
        gqkv = self.S['g']

        tprev = self.snap('t')

        @self.block.sync
        def _(sync, tprev=tprev):
            k.wait(sync, 'wt0', k.ln2_done_w[0])
            k.wait(sync, 'wt1', k.ln2_done_w[1])
            k.wait(sync, 't', tprev)
            for dc in range(8):
                k.inc(sync.dma_start(
                    out=k.BIGB[:, dc * 2048:(dc + 1) * 2048],
                    in_=k.act0[0:TOK, dc * 128:(dc + 1) * 128], transpose=True),
                    'w', 16)
        wln2 = self.S['w']

        e0 = self.snap('e')
        grp_t = []

        @self.block.tensor
        def _(ten, e0=e0):
            k.wait(ten, 'w', wln2)
            k.wait(ten, 'g', gqkv)
            k.wait(ten, 'e', e0)
            u = 0
            for woff in (0, 1024):
                for qc4 in range(4):
                    b = u % 4
                    if u >= 4:
                        k.wait(ten, 'e', e0 + u - 4 + 1)
                    for dc in range(8):
                        i = ten.matmul(k.pb[b][:, :],
                                       k.MED1[:, woff + dc * 128:woff + (dc + 1) * 128],
                                       k.BIGB[:, dc * 2048 + qc4 * 512:
                                              dc * 2048 + (qc4 + 1) * 512],
                                       start=(dc == 0), stop=(dc == 7))
                    grp_t.append(k.inc(i, 't', 1))
                    u += 1
            for tt in range(NT):
                b = u % 4
                if u >= 4:
                    k.wait(ten, 'e', e0 + u - 4 + 1)
                for dc in range(8):
                    i = ten.matmul(k.pb[b][:, 0:128],
                                   k.BIGB[:, dc * 2048 + tt * 128:dc * 2048 + (tt + 1) * 128],
                                   k.MED1[:, 2048 + dc * 128:2048 + (dc + 1) * 128],
                                   start=(dc == 0), stop=(dc == 7))
                grp_t.append(k.inc(i, 't', 1))
                u += 1

        @self.block.scalar
        def _(sc):
            u = 0
            for which in range(2):
                for qc4 in range(4):
                    k.wait(sc, 't', grp_t[u])
                    if which == 0:
                        o = k.BIGB[:, 16384 + qc4 * 512:16384 + (qc4 + 1) * 512]
                    else:
                        o = k.MED0[:, qc4 * 512:(qc4 + 1) * 512]
                    i = sc.activation(out=o, in_=k.pb[u % 4][:, :], func=AF.Copy)
                    k.inc(i, 'e', 1)
                    u += 1
            for tt in range(NT):
                k.wait(sc, 't', grp_t[u])
                i = sc.activation(out=k.BIGB[:, 18432 + tt * 128:18432 + (tt + 1) * 128],
                                  in_=k.pb[u % 4][:, 0:128], func=AF.Copy)
                k.inc(i, 'e', 1)
                u += 1
        eqkv = self.S['e']

        @self.block.gpsimd
        def _(gp, eqkv=eqkv):
            k.wait(gp, 'e', eqkv)
            k.inc(gp.dma_start(out=k.kv_in.rearrange("r (p f) -> r p f", p=128)[0, :, :],
                               in_=k.MED0[:, 0:2048]), 'g', 16)
            k.inc(gp.dma_start(out=k.kv_in.rearrange("r (p f) -> r p f", p=128)[1, :, :],
                               in_=k.BIGB[:, 18432:20480]), 'g', 16)
            k.wait(gp, 'g', k.S['g'])
            k.inc(gp.collective_compute(
                "AllGather", ALU.bypass, replica_groups=[list(range(NCORE))],
                ins=[k.kv_in.opt()], outs=[k.kv_out.opt()]), 'c', 1)
        cagg = self.S['c']

        tqkv = self.snap('t')

        @self.block.sync
        def _(sync, tqkv=tqkv, cagg=cagg):
            k.wait(sync, 'c', cagg)
            k.wait(sync, 't', tqkv)
            k.wait(sync, 'gt', k.snap('gt'))
            for cc in range(8):
                k.inc(sync.dma_start(
                    out=k.BIGA[:, cc * 2048:(cc + 1) * 2048],
                    in_=k.kv_out.rearrange("r (p f) -> r p f", p=128)[2 * cc, :, :]),
                    'w', 16)
                k.inc(sync.dma_start(
                    out=k.BIGA[:, 16384 + cc * 2048:16384 + (cc + 1) * 2048],
                    in_=k.kv_out.rearrange("r (p f) -> r p f", p=128)[2 * cc + 1, :, :]),
                    'w', 16)
            k.wait(sync, 'e', k.snap('e'))
            k.wait(sync, 'gq3', k.stage_w3)
            k.inc(sync.dma_start(out=k.MED1[:, 0:4096], in_=k.cmask_in[:, :]), 'w', 16)
        k.attn_w = self.S['w']

    def phase_l1attn(self):
        k = self
        ST = self.STATS
        SC = f32(1.0 / np.sqrt(128.0))
        BI = f32(-4.0)
        e_attn0 = self.snap('e')
        g_xb = self.snap('g')

        LOOKAHEAD = 6
        for qc in range(8):
            nk = 16 * (qc + 1)
            masked0 = 16 * qc
            t0 = self.snap('t')
            s0 = self.snap('s')
            v0 = self.snap('v')
            w0 = self.snap('w')
            t_S = [0] * nk
            t_PV = [0] * nk
            _t = t0
            for jj in range(nk + LOOKAHEAD):
                if jj >= LOOKAHEAD:
                    _t += 1
                    t_PV[jj - LOOKAHEAD] = _t
                if jj < nk:
                    _t += 1
                    t_S[jj] = _t
            s_exp = [s0 + j + 1 for j in range(nk)]
            v_after = []
            vv = v0
            for j in range(nk):
                if j >= masked0:
                    vv += 1
                vv += 1
                v_after.append(vv)
            t_lT = t_PV[-1] + 2
            t_wo = [t_lT + 1 + uu for uu in range(4)]

            mxa = []

            @self.block.sync
            def _(sync, qc=qc):
                k.wait(sync, 'w', k.attn_w)
                k.wait(sync, 'g', g_xb)
                if qc > 0:
                    k.wait(sync, 'e', e_attn0 + qc * 5)
                for u in range(2):
                    tt = 2 * qc + u
                    mxa.append(k.inc(sync.dma_start(
                        out=k.XF32[:, u * 1024:(u + 1) * 1024],
                        in_=k.xb[tt * 128:(tt + 1) * 128, :]), 'wx%d' % u, 16))

            @self.block.tensor
            def _(ten, qc=qc, nk=nk, masked0=masked0):
                k.wait(ten, 'w', k.attn_w)
                Sslots = [k.pS[:, 0:256], k.pb[4][:, 0:256],
                          k.pb[0][:, 0:256], k.pb[1][:, 0:256],
                          k.pb[2][:, 0:256], k.pb[3][:, 0:256]]
                for jj in range(nk + LOOKAHEAD):
                    if jj >= LOOKAHEAD:
                        j = jj - LOOKAHEAD
                        t2, c2 = j // 8, j % 8
                        if j >= masked0:
                            k.wait(ten, 'v', v_after[j])
                        else:
                            k.wait(ten, 's', s_exp[j])
                        if j == 0 and qc > 0:
                            k.wait(ten, 'e', e_attn0 + (qc - 1) * 5 + 1)
                        i = ten.matmul(k.pP[:, 0:256],
                                       k.BIGA[:, 16384 + (c2 * 16 + t2) * 128:
                                              16384 + (c2 * 16 + t2 + 1) * 128],
                                       k.MED0[:, 4096 + (j % 8) * 256:
                                              4096 + (j % 8) * 256 + 256],
                                       start=(j == 0), stop=(j == nk - 1))
                        assert k.inc(i, 't', 1) == t_PV[j]
                    if jj < nk:
                        t2, c2 = jj // 8, jj % 8
                        if jj >= 6:
                            k.wait(ten, 's', s_exp[jj - 6])
                        elif qc > 0 and jj in (2, 3, 4, 5):
                            k.wait(ten, 'e', e_attn0 + (qc - 1) * 5 + jj)
                        i = ten.matmul(Sslots[jj % 6],
                                       k.BIGA[:, c2 * 2048 + t2 * 128:
                                              c2 * 2048 + (t2 + 1) * 128],
                                       k.BIGB[:, 16384 + qc * 256:16384 + (qc + 1) * 256],
                                       start=True, stop=True)
                        assert k.inc(i, 't', 1) == t_S[jj]
                k.wait(ten, 'v', v_after[nk - 1])
                i = ten.transpose(k.pT[:, 0:128], ST[:, 768:896], k.EYE[:])
                k.inc(i, 't', 1)
                i = ten.transpose(k.pT[:, 128:256], ST[:, 896:1024], k.EYE[:])
                k.inc(i, 't', 1)
                k.wait(ten, 'e', e_attn0 + qc * 5 + 1)
                for u in range(2):
                    for dh in range(2):
                        if qc > 0:
                            k.wait(ten, 'e', e_attn0 + qc * 5)
                        i = ten.matmul(k.pb[u * 2 + dh][:, :],
                                       k.MED0[:, 6144 + u * 128:6144 + (u + 1) * 128],
                                       k.MED0[:, 2048 + dh * 512:2048 + dh * 512 + 512],
                                       start=True, stop=True)
                        k.inc(i, 't', 1)
                assert k.S['t'] == t_wo[-1], (k.S['t'], t_wo)

            @self.block.scalar
            def _(sc, qc=qc, nk=nk):
                if qc == 0:
                    sc.preload_act_table(AF.Exp)
                for j in range(nk):
                    k.wait(sc, 't', t_S[j])
                    if j >= 8:
                        k.wait(sc, 't', t_PV[j - 8])
                        k.wait(sc, 'v', v_after[j - 8])
                    Sslot = [k.pS[:, 0:256], k.pb[4][:, 0:256],
                             k.pb[0][:, 0:256], k.pb[1][:, 0:256],
                             k.pb[2][:, 0:256], k.pb[3][:, 0:256]][j % 6]
                    i = sc.activation(
                        out=k.MED0[:, 4096 + (j % 8) * 256:4096 + (j % 8) * 256 + 256],
                        in_=Sslot, func=AF.Exp, scale=SC, bias=BI)
                    k.inc(i, 's', 1)
                k.wait(sc, 't', t_PV[nk - 1])
                i = sc.activation(out=k.MED0[:, 6144:6400], in_=k.pP[:, 0:256],
                                  func=AF.Copy)
                k.inc(i, 'e', 1)

            @self.block.vector
            def _(vec, qc=qc, nk=nk, masked0=masked0):
                for j in range(nk):
                    k.wait(vec, 's', s_exp[j])
                    P = k.MED0[:, 4096 + (j % 8) * 256:4096 + (j % 8) * 256 + 256]
                    if j >= masked0:
                        mi = j - masked0
                        i = vec.tensor_tensor(P, P, k.MED1[:, mi * 256:(mi + 1) * 256],
                                              ALU.mult)
                        k.inc(i, 'v', 1)
                    if j == 0:
                        i = vec.tensor_scalar(ST[:, 768:1024], P, 1.0, None, ALU.mult)
                    else:
                        i = vec.tensor_tensor(ST[:, 768:1024], ST[:, 768:1024], P,
                                              ALU.add)
                    k.inc(i, 'v', 1)
                    assert k.S['v'] == v_after[j]
                k.wait(vec, 't', t_lT)
                vec.reduce_sum(ST[:, 512:513], k.pT[:, 0:128], AX)
                i = vec.reduce_sum(ST[:, 513:514], k.pT[:, 128:256], AX)
                self._vfence(vec, i)
                i = vec.reciprocal(ST[:, 514:516], ST[:, 512:514])
                self._vfence(vec, i)
                for u in range(2):
                    k.wait(vec, 'wx%d' % u, mxa[u])
                    if qc > 0:
                        k.wait(vec, 'g', g_xb + qc * 32)
                    for dh in range(2):
                        k.wait(vec, 't', t_wo[u * 2 + dh])
                        i = vec.scalar_tensor_tensor(
                            k.XF32[:, 2048 + u * 1024 + dh * 512:
                                   2048 + u * 1024 + dh * 512 + 512],
                            k.pb[u * 2 + dh][:, :],
                            ST[:, 514 + u:515 + u],
                            k.XF32[:, u * 1024 + dh * 512:u * 1024 + dh * 512 + 512],
                            ALU.mult, ALU.add)
                        k.inc(i, 'e', 1)
            ewo0 = self.S['e'] - 4

            @self.block.gpsimd
            def _(gp, qc=qc, ewo0=ewo0):
                for u in range(2):
                    k.wait(gp, 'e', ewo0 + (u + 1) * 2)
                    tt = 2 * qc + u
                    mm_ = k.inc(gp.dma_start(out=k.xa[tt * 128:(tt + 1) * 128, :],
                                             in_=k.XF32[:, 2048 + u * 1024:2048 + (u + 1) * 1024]),
                                'g', 16)
                    assert mm_ == g_xb + qc * 32 + (u + 1) * 16

    def phase_l1ffn(self):
        k = self

        for h in range(2):
            for wi, wbf in enumerate((k.w1_bf, k.w2_bf)):
                tprev = self.snap('t')
                vprev = self.snap('v')

                @self.block.sync
                def _(sync, wbf=wbf, tprev=tprev, wi=wi, h=h, vprev=vprev):
                    k.wait(sync, 'gw', k.cast_w12)
                    k.wait(sync, 't', tprev)
                    k.wait(sync, 'gt', k.snap('gt'))
                    if wi == 0:
                        k.wait(sync, 'v', vprev)
                    if h == 0:
                        for dc in range(8):
                            k.inc(sync.dma_start(
                                out=k.BIGA[:, dc * 4096:(dc + 1) * 4096],
                                in_=wbf[0:N, dc * 128:(dc + 1) * 128],
                                transpose=True), 'wta', 16)
                    else:
                        k._biga_reload(sync, k.w1tt if wi == 0 else k.w2tt)
                    if wi == 0:
                        k.wait(sync, 'wt0', k.ln3_done_w[0])
                        k.wait(sync, 'wt1', k.ln3_done_w[1])
                        for dc in range(8):
                            k.inc(sync.dma_start(
                                out=k.MED0[:, dc * 1024:(dc + 1) * 1024],
                                in_=k.act0[h * 1024:(h + 1) * 1024,
                                           dc * 128:(dc + 1) * 128],
                                transpose=True), 'w', 16)
                ww = self.S['w']
                wtaw = self.snap('wta')
                if h == 0:
                    self._biga_store(k.w1tt if wi == 0 else k.w2tt)

                e0 = self.snap('e')
                grp_t = []

                @self.block.tensor
                def _(ten, ww=ww, e0=e0, wtaw=wtaw):
                    k.wait(ten, 'w', ww)
                    k.wait(ten, 'wta', wtaw)
                    k.wait(ten, 'e', e0)
                    u = 0
                    for no in range(32):
                        for tc in range(2):
                            b = u % 4
                            if u >= 4:
                                k.wait(ten, 'e', e0 + u - 4 + 1)
                            for dc in range(8):
                                i = ten.matmul(
                                    k.pb[b][:, :],
                                    k.BIGA[:, dc * 4096 + no * 128:dc * 4096 + (no + 1) * 128],
                                    k.MED0[:, dc * 1024 + tc * 512:dc * 1024 + tc * 512 + 512],
                                    start=(dc == 0), stop=(dc == 7))
                            grp_t.append(k.inc(i, 't', 1))
                            u += 1

                if wi == 0:
                    k.w1_e0 = self.snap('e')

                    @self.block.scalar
                    def _(sc, h=h):
                        if h == 0:
                            sc.preload_act_table(AF.Silu)
                        u = 0
                        for no in range(32):
                            for tc in range(2):
                                k.wait(sc, 't', grp_t[u])
                                i = sc.activation(
                                    out=k.BIGB[:, no * 1024 + tc * 512:
                                               no * 1024 + tc * 512 + 512],
                                    in_=k.pb[u % 4][:, :], func=AF.Silu,
                                    scale=k.SVEC[:, no:no + 1])
                                k.inc(i, 'e', 1)
                                u += 1
                else:
                    @self.block.vector
                    def _(vec, e0=e0):
                        u = 0
                        for no in range(32):
                            for tc in range(2):
                                k.wait(vec, 't', grp_t[u])
                                k.wait(vec, 'e', k.w1_e0 + u + 1)
                                sl = slice(no * 1024 + tc * 512,
                                           no * 1024 + tc * 512 + 512)
                                i = vec.scalar_tensor_tensor(
                                    k.BIGB[:, sl], k.pb[u % 4][:, :],
                                    k.SVEC[:, 32 + no:33 + no],
                                    k.BIGB[:, sl], ALU.mult, ALU.mult)
                                k.inc(i, 'e', 1)
                                u += 1

            for li, (bandsrc, park, scol) in enumerate(
                    ((k.h1_bf, k.gpark, 64), (k.h2_bf, k.kpark, 96))):
                if li == 1:
                    tprev = self.snap('t')
                    gprev = self.snap('g')

                    @self.block.sync
                    def _(sync, h=h, tprev=tprev, gprev=gprev):
                        k.wait(sync, 't', tprev)
                        k.wait(sync, 'g', gprev)
                        for nc_ in range(32):
                            k.inc(sync.dma_start(
                                out=k.BIGB[:, nc_ * 1024:(nc_ + 1) * 1024],
                                in_=k.gpark[nc_ * 128:(nc_ + 1) * 128,
                                            h * 1024:(h + 1) * 1024]), 'w', 16)
                wh = self.S['w']
                t_b0 = self.snap('t')
                tprev2 = self.snap('t')

                mb = []

                @self.block.sync
                def _(sync, bandsrc=bandsrc, wh=wh, t_b0=t_b0, li=li):
                    k.wait(sync, 'gh%d' % (li + 1),
                           k.cast_h1 if li == 0 else k.cast_h2)
                    k.wait(sync, 'w', wh)
                    k.wait(sync, 't', t_b0)
                    k.wait(sync, 'gt', k.snap('gt'))
                    for bg in range(8):
                        if bg >= 2:
                            k.wait(sync, 't', t_b0 + (bg - 1) * 32)
                        for ni in range(32):
                            m = k.inc(sync.dma_start(
                                out=k.BIGA[:, (bg % 2) * 16384 + ni * 512:
                                           (bg % 2) * 16384 + (ni + 1) * 512],
                                in_=bandsrc[bg * 512:(bg + 1) * 512,
                                            ni * 128:(ni + 1) * 128],
                                transpose=True), 'wb%d' % (bg % 2), 16)
                        mb.append(m)

                e0 = self.snap('e')
                g_park0 = self.snap('g')
                grp_t = []

                @self.block.tensor
                def _(ten, e0=e0):
                    k.wait(ten, 'e', e0)
                    u = 0
                    for bg in range(8):
                        k.wait(ten, 'wb%d' % (bg % 2), mb[bg])
                        for no_l in range(4):
                            for tc in range(2):
                                b = u % 4
                                if u >= 4:
                                    k.wait(ten, 'e', e0 + u - 4 + 1)
                                for ni in range(32):
                                    i = ten.matmul(
                                        k.pb[b][:, :],
                                        k.BIGA[:, (bg % 2) * 16384 + ni * 512 + no_l * 128:
                                               (bg % 2) * 16384 + ni * 512 + (no_l + 1) * 128],
                                        k.BIGB[:, ni * 1024 + tc * 512:
                                               ni * 1024 + tc * 512 + 512],
                                        start=(ni == 0), stop=(ni == 31))
                                    if (ni + 1) % 8 == 0 and ni != 31:
                                        k.inc(i, 't', 1)
                                grp_t.append(k.inc(i, 't', 1))
                                u += 1
                assert self.S['t'] - t_b0 == 8 * 8 * 4

                @self.block.scalar
                def _(sc, h=h, li=li, scol=scol, g_park0=g_park0):
                    if li == 0 and h == 0:
                        sc.preload_act_table(AF.Gelu)
                    u = 0
                    for bg in range(8):
                        for no_l in range(4):
                            for tc in range(2):
                                k.wait(sc, 't', grp_t[u])
                                if u >= 2:
                                    k.wait(sc, 'g', g_park0 + u * 16)
                                i = sc.activation(
                                    out=k.DMY[:, (u % 2) * 512:(u % 2) * 512 + 512],
                                    in_=k.pb[u % 4][:, :], func=AF.Gelu,
                                    scale=k.SVEC[:, scol + bg * 4 + no_l:
                                                 scol + bg * 4 + no_l + 1])
                                k.inc(i, 'e', 1)
                                u += 1
                e_ev0 = self.S['e'] - 64

                @self.block.gpsimd
                def _(gp, h=h, park=park, e_ev0=e_ev0):
                    u = 0
                    for bg in range(8):
                        for no_l in range(4):
                            for tc in range(2):
                                k.wait(gp, 'e', e_ev0 + u + 1)
                                no = bg * 4 + no_l
                                k.inc(gp.dma_start(
                                    out=park[no * 128:(no + 1) * 128,
                                             h * 1024 + tc * 512:h * 1024 + tc * 512 + 512],
                                    in_=k.DMY[:, (u % 2) * 512:(u % 2) * 512 + 512]),
                                    'g', 16)
                                u += 1

            tprev = self.snap('t')
            gprev = self.snap('g')

            @self.block.sync
            def _(sync, h=h, tprev=tprev, gprev=gprev):
                k.wait(sync, 'gq3', k.stage_w3)
                k.wait(sync, 'g', gprev)
                k.wait(sync, 't', tprev)
                k.wait(sync, 'gt', k.snap('gt'))
                if h == 0:
                    for nc_ in range(32):
                        k.inc(sync.dma_start(out=k.BIGA[:, nc_ * 1024:(nc_ + 1) * 1024],
                                             in_=k.w3s_bf[0:D, nc_ * 128:(nc_ + 1) * 128],
                                             transpose=True), 'wta', 16)
                else:
                    k._biga_reload(sync, k.w3tt)
                for nc_ in range(32):
                    k.inc(sync.dma_start(
                        out=k.BIGB[:, nc_ * 1024:(nc_ + 1) * 1024],
                        in_=k.kpark[nc_ * 128:(nc_ + 1) * 128,
                                    h * 1024:(h + 1) * 1024]), 'w', 16)
            w3w = self.S['w']
            wtaw3 = self.snap('wta')
            if h == 0:
                self._biga_store(k.w3tt)

            e0w = self.snap('e')
            g_xa3 = self.snap('g')

            mx = []

            @self.block.sync
            def _(sync, h=h, e0w=e0w, g_xa3=g_xa3):
                for tt in range(8):
                    if tt >= 2:
                        k.wait(sync, 'e', e0w + (tt - 1) * 2)
                    elif tt == 0:
                        k.wait(sync, 'g', g_xa3)
                    mx.append(k.inc(sync.dma_start(
                        out=k.XF32[:, (tt % 2) * 1024:(tt % 2) * 1024 + 1024],
                        in_=k.xa[(h * 8 + tt) * 128:(h * 8 + tt + 1) * 128, :]),
                        'wx%d' % (tt % 2), 16))
            grp_tw = []

            @self.block.tensor
            def _(ten, w3w=w3w, e0w=e0w, wtaw3=wtaw3):
                k.wait(ten, 'w', w3w)
                k.wait(ten, 'wta', wtaw3)
                k.wait(ten, 'e', e0w)
                for tt in range(8):
                    for dh in range(2):
                        u = tt * 2 + dh
                        b = u % 4
                        if u >= 4:
                            k.wait(ten, 'e', e0w + u - 4 + 1)
                        for nc_ in range(32):
                            i = ten.matmul(
                                k.pb[b][:, :],
                                k.BIGB[:, nc_ * 1024 + tt * 128:nc_ * 1024 + (tt + 1) * 128],
                                k.BIGA[:, nc_ * 1024 + dh * 512:nc_ * 1024 + dh * 512 + 512],
                                start=(nc_ == 0), stop=(nc_ == 31))
                            if (nc_ + 1) % 8 == 0 and nc_ != 31:
                                k.inc(i, 't', 1)
                        grp_tw.append(k.inc(i, 't', 1))

            gsb = self.snap('g')

            @self.block.vector
            def _(vec, gsb=gsb):
                for tt in range(8):
                    for dh in range(2):
                        u = tt * 2 + dh
                        k.wait(vec, 't', grp_tw[u])
                        if dh == 0:
                            k.wait(vec, 'wx%d' % (tt % 2), mx[tt])
                            if tt >= 2:
                                k.wait(vec, 'g', gsb + tt * 16)
                        i = vec.scalar_tensor_tensor(
                            k.XF32[:, 2048 + (tt % 2) * 1024 + dh * 512:
                                   2048 + (tt % 2) * 1024 + dh * 512 + 512],
                            k.pb[u % 4][:, :], 1.0,
                            k.XF32[:, (tt % 2) * 1024 + dh * 512:
                                   (tt % 2) * 1024 + dh * 512 + 512],
                            ALU.mult, ALU.add)
                        k.inc(i, 'e', 1)
            er0 = self.S['e'] - 16

            @self.block.gpsimd
            def _(gp, h=h, er0=er0, gsb=gsb):
                for tt in range(8):
                    k.wait(gp, 'e', er0 + (tt + 1) * 2)
                    mm_ = k.inc(gp.dma_start(
                        out=k.out[(h * 8 + tt) * 128:(h * 8 + tt + 1) * 128, :],
                        in_=k.XF32[:, 2048 + (tt % 2) * 1024:2048 + (tt % 2) * 1024 + 1024]),
                        'g', 16)
                    assert mm_ == gsb + (tt + 1) * 16

    def phase_finish(self):
        k = self

        @self.block.gpsimd
        def _(gp):
            k.wait(gp, 'g')

        @self.block.sync
        def _(sync):
            k.wait(sync, 'w')


_NC_CACHE = {}


def _build(debug=False):
    import os
    key = (debug, os.environ.get("KPHASES", "99"), os.environ.get("KLN_NOSQ"), os.environ.get("KLN_NONEWT"), os.environ.get("KLN_PLAIN"))
    if key not in _NC_CACHE:
        kb = KB(debug=debug)
        nc = kb.build()
        _NC_CACHE[key] = nc
    return _NC_CACHE[key]


def _make_consts(core):
    eye = np.eye(128, dtype=np.float32)
    ones = np.ones((128, 128), dtype=np.float32)
    tri = np.triu(np.ones((128, 128), dtype=np.float32))
    cm = np.zeros((16, 128, 256), dtype=np.float32)
    for c2 in range(8):
        blk = np.zeros((128, 128), np.float32)
        if c2 < core:
            blk[:] = 1.0
        elif c2 == core:
            blk[:] = tri
        cm[c2, :, 0:128] = blk
        cm[c2, :, 128:256] = 1.0
        cm[8 + c2, :, 128:256] = blk
    cmask = np.ascontiguousarray(
        cm.transpose(1, 0, 2).reshape(128, 4096)).astype(ml_dtypes.bfloat16)
    return eye, ones, cmask


def _make_svec(w1_s, w2_s, h1_s, h2_s, w3_s):
    sv = np.zeros((128, 144), dtype=np.float32)
    sv[:, 0:32] = np.ascontiguousarray(w1_s.reshape(32, 128).T)
    sv[:, 32:64] = np.ascontiguousarray(w2_s.reshape(32, 128).T)
    sv[:, 64:96] = np.ascontiguousarray(h1_s.reshape(32, 128).T)
    sv[:, 96:128] = np.ascontiguousarray(h2_s.reshape(32, 128).T)
    sv[:, 128:136] = np.ascontiguousarray(w3_s.reshape(8, 128).T)
    return sv


def kernel(L0_attn_inputs, L0_attn_key_param_tokens, L0_attn_value_param_tokens,
           L0_ffn_w_gate, L0_ffn_w_up, L0_ffn_w_down,
           L1_attn_Wq, L1_attn_Wk, L1_attn_Wv, L1_attn_Wo,
           L1_ffn_w1_w, L1_ffn_w1_s, L1_ffn_w2_w, L1_ffn_w2_s,
           L1_ffn_h1_w, L1_ffn_h1_s, L1_ffn_h2_w, L1_ffn_h2_s,
           L1_ffn_w3_w, L1_ffn_w3_s, _debug=False):
    nc = _build(debug=_debug)
    x = np.ascontiguousarray(np.asarray(L0_attn_inputs, dtype=np.float32)
                             .reshape(128, 128, D))
    svec = _make_svec(np.asarray(L1_ffn_w1_s, np.float32),
                      np.asarray(L1_ffn_w2_s, np.float32),
                      np.asarray(L1_ffn_h1_s, np.float32),
                      np.asarray(L1_ffn_h2_s, np.float32),
                      np.asarray(L1_ffn_w3_s, np.float32))
    shared = {
        "L0_attn_key_param_tokens": np.ascontiguousarray(L0_attn_key_param_tokens),
        "L0_attn_value_param_tokens": np.ascontiguousarray(L0_attn_value_param_tokens),
        "L0_ffn_w_gate": np.ascontiguousarray(L0_ffn_w_gate),
        "L0_ffn_w_up": np.ascontiguousarray(L0_ffn_w_up),
        "L0_ffn_w_down": np.ascontiguousarray(L0_ffn_w_down),
        "L1_attn_Wq": np.ascontiguousarray(L1_attn_Wq),
        "L1_attn_Wk": np.ascontiguousarray(L1_attn_Wk),
        "L1_attn_Wv": np.ascontiguousarray(L1_attn_Wv),
        "L1_attn_Wo": np.ascontiguousarray(L1_attn_Wo),
        "L1_ffn_w1_w": np.ascontiguousarray(L1_ffn_w1_w),
        "L1_ffn_w2_w": np.ascontiguousarray(L1_ffn_w2_w),
        "L1_ffn_h1_w": np.ascontiguousarray(L1_ffn_h1_w),
        "L1_ffn_h2_w": np.ascontiguousarray(L1_ffn_h2_w),
        "L1_ffn_w3_w": np.ascontiguousarray(L1_ffn_w3_w),
        "cst_svec": svec,
    }
    in_maps = []
    for c in range(NCORE):
        eye, ones, cmask = _make_consts(c)
        m = dict(shared)
        m["L0_attn_inputs"] = np.ascontiguousarray(x[c::8].reshape(TOK, D))
        m["cst_eye"] = eye
        m["cst_ones"] = ones
        m["cst_cmask"] = cmask
        in_maps.append(m)
    res = run_bass_kernel_spmd(nc, in_maps, core_ids=list(range(NCORE)))
    full = np.empty((128, 128, D), dtype=np.float32)
    for c in range(NCORE):
        full[c::8] = res.results[c]["out"].reshape(NT, 128, D)
    out = full.reshape(1, 128 * 128, D)
    if _debug:
        dbg = []
        for name in ("dbg_x1", "dbg_x2", "dbg_x3"):
            fb = np.empty((128, 128, D), dtype=np.float32)
            for c in range(NCORE):
                fb[c::8] = res.results[c][name].reshape(NT, 128, D)
            dbg.append(fb.reshape(1, 128 * 128, D))
        raw = {n: [np.asarray(res.results[c][n]) for c in range(NCORE)]
               for n in ("act0", "act1", "gpark", "kpark", "w3s_bf")}
        return out, dbg, raw
    return out
